# version 02
# baseline (speedup 1.0000x reference)
import contextlib
import numpy as np
import ml_dtypes
import concourse.bass as bass
import concourse.mybir as mybir
from concourse.bass_utils import run_bass_kernel_spmd

F32 = mybir.dt.float32
BF16 = mybir.dt.bfloat16
U8 = mybir.dt.uint8
AF = mybir.ActivationFunctionType
ALU = mybir.AluOpType
AX = mybir.AxisListType
BF = ml_dtypes.bfloat16

D = 2048
S = 4096
NBLK = 8
BLK = 512
NSLOT = 5
QT = 256
NH, NKV, HD = 16, 4, 128
GRP = NH // NKV

SLOTS_A = [5, 6, 7, 0, 1]
SLOTS_B = [1, 2, 3, 4, 5]
PADK = [24, 28, 32, 20, 24]
NB = [p // 4 for p in PADK]
BOFF = np.cumsum([0] + NB).tolist()
MASKB = []
for _s in range(NSLOT):
    _ca = SLOTS_A[_s] + 1
    _cb = SLOTS_B[_s] + 1
    MASKB.append(sorted({_ca - 1, _cb - 1}))
SHIFT0 = 12.0
CAP1 = 20.0
SCALE0 = 1.0 / np.sqrt(128.0)

ENGS = ("sync", "tensor", "scalar", "vector", "gpsimd")


class Sem:
    __slots__ = ("h", "n", "name", "queue", "stage", "sealed_n")

    def __init__(self, h, name):
        self.h = h
        self.n = 0
        self.name = name
        self.queue = None
        self.stage = None
        self.sealed_n = 0

    def inc(self, k):
        self.n += k
        return self.n


class Prog:
    def __init__(self):
        self.nc = bass.Bass("TRN2", target_bir_lowering=False, debug=False)
        self.es = contextlib.ExitStack()
        self.streams = {k: [] for k in ENGS}
        self._semi = 0

    def sem(self, name=None, queue=None):
        name = name or f"s{self._semi}"
        self._semi += 1
        h = self.es.enter_context(self.nc.semaphore(name))
        s = Sem(h, name)
        if queue is not None:
            s.queue = queue
            s.stage = self.sem(name + "_stg")
        return s

    def sbuf(self, name, shape, dtype):
        return self.es.enter_context(self.nc.sbuf_tensor(name, list(shape), dtype))

    def psum(self, name, shape, dtype=F32):
        return self.es.enter_context(self.nc.psum_tensor(name, list(shape), dtype))

    def on(self, eng, fn):
        self.streams[eng].append(fn)

    def wait(self, eng, sem, val=None):
        if sem.stage is not None and val is None:
            if sem.n > sem.sealed_n:
                self.seal(sem.queue, sem, sem.stage)
                sem.sealed_n = sem.n
            v = sem.stage.n
            if v > 0:
                self.on(eng, lambda e, s=sem.stage.h, v=v: e.wait_ge(s, v))
            return
        v = sem.n if val is None else val
        if v > 0:
            self.on(eng, lambda e, s=sem.h, v=v: e.wait_ge(s, v))

    def dma(self, eng, out, in_, sem, **kw):
        sem.inc(16)
        self.on(eng, lambda e, o=out, i=in_, s=sem.h, kw=kw: e.dma_start(out=o, in_=i, **kw).then_inc(s, 16))

    def seal(self, eng, qsem, stage):
        n = qsem.n
        self.on(eng, lambda e, s=qsem.h, n=n: e.wait_ge(s, n))
        stage.inc(1)
        self.on(eng, lambda e, s=stage.h: e.sem_inc(s, 1))

    def dma_small(self, eng, out, in_, sem, **kw):
        self.dma(eng, out, in_, sem, **kw)

    def step(self, eng, fn, sem=None, k=1):
        if sem is not None:
            sem.inc(k)
            self.on(eng, lambda e, f=fn, s=sem.h, k=k: f(e).then_inc(s, k))
        else:
            self.on(eng, lambda e, f=fn: f(e))

    def build(self):
        with self.nc.Block() as block:
            for name in ENGS:
                fns = self.streams[name]
                if not fns:
                    continue
                getattr(block, name)(lambda e, fns=fns: [f(e) for f in fns])
        return self.nc



def rope_tables(positions):
    inv = 1.0 / 10000.0 ** (np.arange(0, HD, 2, dtype=np.float64) / HD)
    ang = np.asarray(positions, np.float64)[None, :] * inv[:, None]
    c = np.cos(ang).astype(np.float32)
    s = np.sin(ang).astype(np.float32)
    cos2 = np.concatenate([c, c], axis=0)
    sin2 = np.concatenate([-s, s], axis=0)
    return cos2, sin2


def l0_bias_for(slots):
    out = np.full((128, BOFF[-1]), -SHIFT0, np.float32)
    for s in range(NSLOT):
        c = slots[s] + 1
        for b in range(NB[s]):
            if b >= c:
                out[:, BOFF[s] + b] = -1e30
    return out


def l0_masks_for(slots):
    out = np.ones((128, NSLOT * 2 * 2 * 1024), np.float32)
    for s in range(NSLOT):
        blk = slots[s]
        cb = blk + 1
        for mi, mb in enumerate(MASKB[s]):
            for qh in range(2):
                tile = np.ones((128, 1024), np.float32)
                if mb == cb - 1:
                    for j in range(4):
                        kc = 4 * mb + j
                        keypos = 128 * kc + np.arange(128)[:, None]
                        qpos = blk * BLK + qh * QT + np.arange(QT)[None, :]
                        tile[:, j * QT:(j + 1) * QT] = (keypos <= qpos)
                col = ((s * 2 + mi) * 2 + qh) * 1024
                out[:, col:col + 1024] = tile
    return out.astype(BF)


def l1_masks_for(slots, role):
    out = np.zeros((128, 4 * 2 * 2 * 1024), np.float32)
    for qi, s in enumerate(range(1, NSLOT)):
        blk = slots[s]
        for bi in range(2):
            for qh in range(2):
                tile = np.zeros((128, 1024), np.float32)
                for j in range(4):
                    jpos = 128 * (4 * bi + j - 4) + np.arange(128)[:, None]
                    a = qh * QT + np.arange(QT)[None, :]
                    if bi == 0:
                        ok = (jpos + BLK) >= a
                        if blk == 0:
                            ok = np.zeros_like(ok)
                    else:
                        ok = jpos <= a
                    tile[:, j * QT:(j + 1) * QT] = ok
                col = ((qi * 2 + bi) * 2 + qh) * 1024
                out[:, col:col + 1024] = tile
    return out.astype(BF)


def host_prepare(inputs):
    x = np.ascontiguousarray(inputs["L0_attn_x"])
    shared = {
        "wq0": np.ascontiguousarray(inputs["L0_attn_Wq"]),
        "wkv0": np.ascontiguousarray(np.concatenate([inputs["L0_attn_Wk"], inputs["L0_attn_Wv"]], axis=1)),
        "wo0": np.ascontiguousarray(inputs["L0_attn_Wo"]),
        "ident": np.eye(128, dtype=BF),
        "ones_c_bf": np.ones((128, 1), dtype=BF),
        "ones_c_f": np.ones((128, 1), dtype=np.float32),
        "ones_r_f": np.ones((1, 128), dtype=np.float32),
        "ones_r_bf": np.ones((1, 128), dtype=BF),
        "cfix": np.array([[1e-5, 0, 0, 0, 0, 0, 0, 0]], dtype=np.float32),
        "ccol": np.tile(np.array([[-20.0, 0, 0, 0]], dtype=np.float32), (128, 1)),
    }
    cores = []
    for c in range(8):
        b, role = c // 2, c % 2
        slots = SLOTS_A if role == 0 else SLOTS_B
        xb = x[b]
        x_fm = np.ascontiguousarray(xb.T)
        x_slots = np.ascontiguousarray(
            np.concatenate([xb[s * BLK:(s + 1) * BLK] for s in slots], axis=0).T)
        d = {
            "x_fm": x_fm,
            "x_slots": x_slots,
            "l0bias": l0_bias_for(slots),
            "l0mask": l0_masks_for(slots),
        }
        cores.append(d)
    return shared, cores



def emit_load_consts(p, io, bufs):
    ld = bufs["s_ld"]
    p.dma("sync", bufs["ident"][:], io["ident"], ld)
    p.dma("sync", bufs["ones_c_bf"][:], io["ones_c_bf"], ld)
    p.dma("sync", bufs["ones_c_f"][:], io["ones_c_f"], ld)
    p.dma_small("sync", bufs["ones_r_f"][:], io["ones_r_f"], ld)
    p.dma_small("sync", bufs["ones_r_bf"][:], io["ones_r_bf"], ld)
    p.dma_small("sync", bufs["cfix"][:], io["cfix"], ld)


def emit_ln(p, bufs, xt, xst, xoff, zt, zst, zoff, nt, mur_bf, mur_off,
            s_pe, s_act, s_dve, s_gp):
    sq = bufs["sq"]
    RS = bufs.get("rows_stride", 512)
    sum_ps = bufs["stat_ps"]
    rbc_ps = bufs["rbc_ps"]
    rows = bufs["rows"]
    rbc = bufs["rbc"]
    ones_f = bufs["ones_c_f"]
    ones_r = bufs["ones_r_f"]

    p.wait("tensor", s_dve)
    p.wait("tensor", s_gp)
    p.wait("scalar", s_gp)
    for dc in range(16):
        pb = dc % 2
        if dc >= 2:
            p.wait("scalar", s_pe)
        p.step("scalar", lambda e, dc=dc, pb=pb: e.activation(
            out=sq[:, pb * 256:pb * 256 + nt], in_=xt[:, dc * xst + xoff:dc * xst + xoff + nt],
            func=AF.Square), s_act)
        p.wait("tensor", s_act)
        p.step("tensor", lambda e, dc=dc: e.matmul(
            sum_ps[0:1, 0:nt], ones_f[:], xt[:, dc * xst + xoff:dc * xst + xoff + nt],
            start=(dc == 0), stop=False))
        p.step("tensor", lambda e, dc=dc, pb=pb: e.matmul(
            sum_ps[0:1, 256:256 + nt], ones_f[:], sq[:, pb * 256:pb * 256 + nt],
            start=False, stop=(dc == 15)), s_pe)
    p.wait("vector", s_pe)
    p.step("vector", lambda e: e.tensor_scalar(
        out=rows[0:1, 0:nt], in0=sum_ps[0:1, 0:nt], scalar1=1.0 / D, scalar2=None,
        op0=ALU.mult), s_dve)
    p.wait("vector", s_dve)
    p.step("vector", lambda e: e.tensor_tensor(
        out=rows[0:1, 3 * RS:3 * RS + nt], in0=rows[0:1, 0:nt], in1=rows[0:1, 0:nt], op=ALU.mult), s_dve)
    p.wait("vector", s_dve)
    p.step("vector", lambda e: e.scalar_tensor_tensor(
        out=rows[0:1, RS:RS + nt], in0=sum_ps[0:1, 256:256 + nt], scalar=1.0 / D,
        in1=rows[0:1, 3 * RS:3 * RS + nt], op0=ALU.mult, op1=ALU.subtract), s_dve)
    p.wait("scalar", s_dve)
    p.step("scalar", lambda e: e.activation(
        out=rows[0:1, 3 * RS:3 * RS + nt], in_=rows[0:1, RS:RS + nt], func=AF.Ln, bias=bufs["cfix"][0:1, 0:1], scale=1.0), s_act)
    p.wait("scalar", s_act)
    p.step("scalar", lambda e: e.activation(
        out=rows[0:1, 2 * RS:2 * RS + nt], in_=rows[0:1, 3 * RS:3 * RS + nt], func=AF.Exp, scale=-0.5), s_act)
    p.wait("vector", s_act)
    p.step("vector", lambda e: e.tensor_tensor(
        out=mur_bf[0:1, mur_off:mur_off + nt], in0=rows[0:1, 0:nt], in1=rows[0:1, 2 * RS:2 * RS + nt],
        op=ALU.mult), s_dve)
    p.wait("tensor", s_act)
    p.step("tensor", lambda e: e.matmul(
        rbc_ps[:, 0:nt], ones_r[:], rows[0:1, 2 * RS:2 * RS + nt], start=True, stop=True), s_pe)
    p.wait("scalar", s_pe)
    p.step("scalar", lambda e: e.activation(
        out=rbc[:, 0:nt], in_=rbc_ps[:, 0:nt], func=AF.Copy, bias=0.0, scale=1.0), s_act)
    p.wait("gpsimd", s_act)
    for dc in range(16):
        p.step("gpsimd", lambda e, dc=dc: e.tensor_tensor(
            out=zt[:, dc * zst + zoff:dc * zst + zoff + nt],
            in0=xt[:, dc * xst + xoff:dc * xst + xoff + nt],
            in1=rbc[:, 0:nt], op=ALU.mult), s_gp if dc == 15 else None)


def emit_colsum(p, bufs, w_sb, ocols, out_bf, s_pe, s_act, s_dve):
    sp = bufs["stat_ps"]
    p.wait("tensor", s_dve)
    p.wait("tensor", s_act)
    ng = ocols // 256
    for g in range(ng):
        if g > 0:
            p.wait("tensor", s_act)
        for dc in range(16):
            p.step("tensor", lambda e, g=g, dc=dc: e.matmul(
                sp[0:1, 0:256], bufs["ones_c_bf"][:],
                w_sb[:, dc * ocols + g * 256: dc * ocols + g * 256 + 256],
                start=(dc == 0), stop=(dc == 15)), s_pe if dc == 15 else None)
        p.wait("scalar", s_pe)
        p.step("scalar", lambda e, g=g: e.activation(
            out=out_bf[0:1, g * 256:(g + 1) * 256], in_=sp[0:1, 0:256],
            func=AF.Copy, bias=0.0, scale=-1.0), s_act)


def emit_proj(p, w_sb, w_ocols, oc, zt, zst, zoff, nt, out_ps, s_pe,
              ncol_bf=None, mur_bf=None, mur_off=0, extra=None, stop=True):
    n_extra = 0 if extra is None else len(extra)
    last = 15 if (ncol_bf is None and n_extra == 0) else -1
    for dc in range(16):
        p.step("tensor", lambda e, dc=dc: e.matmul(
            out_ps[:, 0:nt], w_sb[:, dc * w_ocols + oc * 128: dc * w_ocols + oc * 128 + 128],
            zt[:, dc * zst + zoff:dc * zst + zoff + nt],
            start=(dc == 0), stop=(stop and dc == last)),
            s_pe if (stop and dc == last) else None)
    if extra:
        for i, fn in enumerate(extra):
            isl = (i == n_extra - 1) and ncol_bf is None
            p.step("tensor", (lambda e, fn=fn, isl=isl: fn(e, isl)),
                   s_pe if (stop and isl) else None)
    if ncol_bf is not None:
        p.step("tensor", lambda e: e.matmul(
            out_ps[:, 0:nt], ncol_bf[0:1, oc * 128:oc * 128 + 128],
            mur_bf[0:1, mur_off:mur_off + nt], start=False, stop=stop),
            s_pe if stop else None)


def build_p1a(env=None):
    if env is None:
        p = Prog()
        io = {}
    else:
        p = env["p"]
        io = env["io"]
    _pes = contextlib.ExitStack()
    def _psb(name, shape, dtype):
        return _pes.enter_context(p.nc.sbuf_tensor("p1" + name, list(shape), dtype, side="right"))
    def _pps(name, shape, dtype=F32):
        return _pes.enter_context(p.nc.psum_tensor("p1" + name, list(shape), dtype))
    def _dram(name, shape, dtype, kind=None):
        if name not in io:
            kw = {} if kind is None else {"kind": kind}
            io[name] = p.nc.dram_tensor(name, list(shape), dtype, **kw).ap()
        return io[name]
    nc = p.nc
    _dram("x_fm", [D, S], F32, "ExternalInput")
    _dram("x_slots", [D, NSLOT * BLK], F32, "ExternalInput")
    _dram("wq0", [D, D], F32, "ExternalInput")
    _dram("wkv0", [D, 1024], F32, "ExternalInput")
    _dram("wo0", [D, D], F32, "ExternalInput")
    _dram("l0bias", [128, 32], F32, "ExternalInput")
    _dram("l0mask", [128, NSLOT * 4 * 1024], BF16, "ExternalInput")
    for k, shp, dt in [("ident", [128, 128], BF16), ("ones_c_bf", [128, 1], BF16),
                       ("ones_c_f", [128, 1], F32), ("ones_r_f", [1, 128], F32),
                       ("ones_r_bf", [1, 128], BF16), ("cfix", [1, 8], F32)]:
        _dram(k, shp, dt, "ExternalInput")
    _dram("av_slots", [D, NSLOT * BLK], BF16, None if env is not None else "ExternalOutput")
    _dram("x1_slots", [D, NSLOT * BLK], F32, None if env is not None else "ExternalOutput")
    v_scr = nc.dram_tensor("v_scr", [4 * 32 * 128, 128], BF16)

    bufs = {}
    bufs["xt"] = _psb("xt", [128, 16 * 256], F32)
    bufs["zt"] = _psb("zt", [128, 16 * 512], BF16)
    bufs["probs"] = _psb("probs", [128, 3 * 1024], BF16)
    bufs["maskslot"] = _psb("maskslot", [128, 4 * 1024], BF16)
    bufs["bias"] = _psb("bias", [128, 32], F32)
    bufs["sq"] = _psb("sq", [128, 512], F32)
    bufs["rows"] = _psb("rows", [1, 4 * 512], F32)
    bufs["rbc"] = _psb("rbc", [128, 256], F32)
    bufs["rec"] = _psb("rec", [1, 256], F32)
    bufs["recbc"] = _psb("recbc", [128, 256], F32)
    bufs["mur"] = _psb("mur", [1, 512], BF16)
    bufs["ncol_q"] = _psb("ncol_q", [1, 2048], BF16)
    bufs["ncol_kv"] = _psb("ncol_kv", [1, 1024], BF16)
    bufs["vt_s"] = _psb("vt_s", [128, 256], BF16)
    for k in ["ident", "ones_c_bf", "ones_c_f", "ones_r_f", "ones_r_bf", "cfix"]:
        bufs[k] = _psb(k + "_sb", io[k].tensor.shape, io[k].dtype)
    bufs["stat_ps"] = _pps("stat_ps", [128, 512])
    bufs["rbc_ps"] = _pps("rbc_ps", [128, 512])
    l0_es = contextlib.ExitStack()
    bufs["wq"] = l0_es.enter_context(nc.sbuf_tensor("wq", [128, 16 * 2048], BF16, side="right"))
    bufs["K"] = l0_es.enter_context(nc.sbuf_tensor("K", [128, 4 * 4096], BF16, side="right"))

    if env is None:
        s_ld = p.sem("ld", queue="sync"); s_pe = p.sem("pe"); s_act = p.sem("act")
        s_dve = p.sem("dve"); s_gp = p.sem("gp"); s_st = p.sem("st", queue="sync")
        s_gld = p.sem("gld", queue="gpsimd")
    else:
        s_ld = env["s_ld"]; s_pe = env["s_pe"]; s_act = env["s_act"]
        s_dve = env["s_dve"]; s_gp = env["s_gp"]; s_st = env["s_st"]; s_gld = env["s_gld"]
    bufs["s_ld"] = s_ld

    emit_load_consts(p, io, bufs)
    p.dma("sync", bufs["bias"][:], io["l0bias"], s_ld)
    for dc in range(16):
        p.dma("gpsimd", bufs["wq"][:, dc * 2048:(dc + 1) * 2048],
              io["wq0"][dc * 128:(dc + 1) * 128, :], s_gld, max_dma_last_dim=8192)
    p.wait("tensor", s_gld)
    p.wait("tensor", s_ld)
    p.wait("scalar", s_ld)
    p.wait("vector", s_ld)
    p.wait("gpsimd", s_ld)
    emit_colsum(p, bufs, bufs["wq"], 2048, bufs["ncol_q"], s_pe, s_act, s_dve)

    with contextlib.ExitStack() as kv_es:
        wkv = kv_es.enter_context(nc.sbuf_tensor("wkv", [128, 16 * 1024], BF16, side="right"))
        kv_ps = kv_es.enter_context(nc.psum_tensor("kv_ps", [128, 512], F32))
        vt_ps = kv_es.enter_context(nc.psum_tensor("vt_ps", [128, 1024], BF16))
        for dc in range(16):
            p.dma("gpsimd", wkv[:, dc * 1024:(dc + 1) * 1024],
                  io["wkv0"][dc * 128:(dc + 1) * 128, :], s_gld, max_dma_last_dim=4096)
        p.wait("tensor", s_gld)
        emit_colsum(p, bufs, wkv, 1024, bufs["ncol_kv"], s_pe, s_act, s_dve)
        for st in range(16):
            t0 = st * 256
            p.wait("sync", s_gp)
            p.wait("sync", s_pe)
            for dc in range(16):
                p.dma("sync", bufs["xt"][:, dc * 256:(dc + 1) * 256],
                      io["x_fm"][dc * 128:(dc + 1) * 128, t0:t0 + 256], s_ld)
            p.wait("scalar", s_ld)
            p.wait("tensor", s_ld)
            p.wait("gpsimd", s_ld)
            emit_ln(p, bufs, bufs["xt"], 256, 0, bufs["zt"], 512, 0, 256,
                    bufs["mur"], 0, s_pe, s_act, s_dve, s_gp)
            p.wait("tensor", s_gp)
            for oc in range(8):
                p.wait("tensor", s_act)
                p.wait("tensor", s_st)
                emit_proj(p, wkv, 1024, oc, bufs["zt"], 512, 0, 256, kv_ps,
                          s_pe, ncol_bf=bufs["ncol_kv"], mur_bf=bufs["mur"], mur_off=0)
                g = oc % 4
                if oc < 4:
                    p.wait("scalar", s_pe)
                    p.step("scalar", lambda e, g=g, t0=t0: e.activation(
                        out=bufs["K"][:, g * 4096 + t0: g * 4096 + t0 + 256],
                        in_=kv_ps[:, 0:256], func=AF.Copy, bias=0.0, scale=1.0), s_act)
                else:
                    p.wait("scalar", s_pe)
                    p.step("scalar", lambda e: e.activation(
                        out=bufs["vt_s"][:, 0:256], in_=kv_ps[:, 0:256],
                        func=AF.Copy, bias=0.0, scale=1.0), s_act)
                    p.wait("tensor", s_act)
                    for half in range(2):
                        p.step("tensor", lambda e, half=half: e.transpose(
                            vt_ps[:, half * 128:(half + 1) * 128],
                            bufs["vt_s"][:, half * 128:(half + 1) * 128],
                            bufs["ident"][:]), s_pe)
                    p.wait("vector", s_pe)
                    p.step("vector", lambda e: e.tensor_copy(
                        out=bufs["vt_s"][:, 0:256], in_=vt_ps[:, 0:256]), s_dve)
                    base = (g * 32 + t0 // 128) * 128
                    p.wait("sync", s_dve)
                    p.dma("sync",
                          v_scr[base:base + 256, :].rearrange("(h t) j -> t h j", h=2),
                          bufs["vt_s"][:].rearrange("t (h j) -> t h j", h=2), s_st)
                    p.wait("tensor", s_dve)

    with contextlib.ExitStack() as at_es:
        av_ps = [at_es.enter_context(nc.psum_tensor(f"av_ps{h}", [128, 512], F32))
                 for h in range(4)]
        sc_ps = at_es.enter_context(nc.psum_tensor("sc_ps", [128, 1024], F32))
        q_sb = at_es.enter_context(nc.sbuf_tensor("q_sb", [128, 16 * 512], BF16, side="right"))
        av_sb = at_es.enter_context(nc.sbuf_tensor("av_sb", [128, 16 * 512], BF16, side="right"))
        vbuf = at_es.enter_context(nc.sbuf_tensor("vbuf", [128, 32 * 128], BF16, side="right"))
        for sl in range(NSLOT):
            p.wait("sync", s_dve)
            p.wait("sync", s_gp)
            p.dma("sync", bufs["maskslot"][:], io["l0mask"][:, sl * 4096:(sl + 1) * 4096], s_ld)
            p.wait("sync", s_pe)
            for hh in range(2):
                for dc in range(16):
                    p.dma("sync", bufs["xt"][:, dc * 256:(dc + 1) * 256],
                          io["x_slots"][dc * 128:(dc + 1) * 128,
                                        sl * 512 + hh * 256:sl * 512 + hh * 256 + 256], s_ld)
                p.wait("scalar", s_ld)
                p.wait("tensor", s_ld)
                p.wait("gpsimd", s_ld)
                emit_ln(p, bufs, bufs["xt"], 256, 0, bufs["zt"], 512, hh * 256, 256,
                        bufs["mur"], hh * 256, s_pe, s_act, s_dve, s_gp)
                if hh == 0:
                    p.wait("sync", s_gp)
                    p.wait("sync", s_pe)
            p.wait("tensor", s_gp)
            for oc in range(16):
                p.wait("tensor", s_act)
                emit_proj(p, bufs["wq"], 2048, oc, bufs["zt"], 512, 0, 512, sc_ps,
                          s_pe, ncol_bf=bufs["ncol_q"], mur_bf=bufs["mur"], mur_off=0)
                p.wait("scalar", s_pe)
                p.step("scalar", lambda e, oc=oc: e.activation(
                    out=q_sb[:, oc * 512:(oc + 1) * 512], in_=sc_ps[:, 0:512],
                    func=AF.Copy, bias=0.0, scale=1.0), s_act)
            for g in range(4):
                p.wait("sync", s_pe)
                nk = PADK[sl]
                for part in range(0, nk, 8):
                    pk = min(8, nk - part)
                    base = g * 4096 + part * 128
                    p.dma("sync",
                          vbuf[:, part * 128:(part + pk) * 128].rearrange("t (kc j) -> t kc j", j=128),
                          v_scr[base:base + pk * 128, :].rearrange("(kc t) j -> t kc j", t=128),
                          s_ld)
                p.wait("tensor", s_ld)
                for qh in range(2):
                    for h4 in range(4):
                        h = g * 4 + h4
                        for b in range(NB[sl]):
                            pb = b % 2
                            gb = BOFF[sl] + b
                            p.wait("tensor", s_act)
                            p.wait("tensor", s_dve)
                            for j in range(4):
                                kc = 4 * b + j
                                p.step("tensor", lambda e, g=g, kc=kc, h=h, qh=qh, j=j: e.matmul(
                                    sc_ps[:, j * 256:(j + 1) * 256],
                                    bufs["K"][:, g * 4096 + kc * 128: g * 4096 + (kc + 1) * 128],
                                    q_sb[:, h * 512 + qh * 256: h * 512 + qh * 256 + 256],
                                    start=(j % 2 == 0), stop=(j % 2 == 1)), s_pe if j == 3 else None)
                            p.wait("scalar", s_pe)
                            p.wait("scalar", s_dve)
                            p.wait("scalar", s_gp)
                            mi = MASKB[sl].index(b) if b in MASKB[sl] else None
                            p.step("scalar", lambda e, pb=pb, gb=gb: e.activation(
                                out=bufs["probs"][:, pb * 1024:(pb + 1) * 1024],
                                in_=sc_ps[:, 0:1024], func=AF.Exp,
                                bias=bufs["bias"][:, gb:gb + 1], scale=SCALE0), s_act)
                            use = pb
                            if mi is not None:
                                eng = "vector" if (b % 2 == 0) else "gpsimd"
                                sem = s_dve if eng == "vector" else s_gp
                                p.wait(eng, s_act)
                                p.wait(eng, s_pe)
                                p.step(eng, lambda e, pb=pb, mi=mi, qh=qh: e.tensor_tensor(
                                    out=bufs["probs"][:, 2048:3072],
                                    in0=bufs["probs"][:, pb * 1024:(pb + 1) * 1024],
                                    in1=bufs["maskslot"][:, (mi * 2 + qh) * 1024:(mi * 2 + qh + 1) * 1024],
                                    op=ALU.mult), sem)
                                p.wait("tensor", sem)
                                use = 2
                            else:
                                p.wait("tensor", s_act)
                            first = (b == 0)
                            lastb = (b == NB[sl] - 1)
                            def em_av(stop_on_j3, b=b, h4=h4, use=use, first=first):
                                for j in range(4):
                                    kc = 4 * b + j
                                    stp = stop_on_j3 and (j == 3)
                                    p.step("tensor", lambda e, kc=kc, h4=h4, use=use, j=j, first=first, stp=stp: e.matmul(
                                        av_ps[h4][:, 0:256],
                                        vbuf[:, kc * 128:(kc + 1) * 128],
                                        bufs["probs"][:, use * 1024 + j * 256: use * 1024 + (j + 1) * 256],
                                        start=(first and j == 0), stop=stp),
                                        s_pe if stp else None)
                            def em_sums(h4=h4, use=use):
                                for j in range(4):
                                    p.step("tensor", lambda e, h4=h4, use=use, j=j: e.matmul(
                                        av_ps[h4][0:1, 256:512],
                                        bufs["ones_c_bf"][:],
                                        bufs["probs"][:, use * 1024 + j * 256: use * 1024 + (j + 1) * 256],
                                        start=False, stop=False))
                            if lastb:
                                em_sums()
                                em_av(True)
                            else:
                                em_av(False)
                                em_sums()
                    for h4 in range(4):
                        h = g * 4 + h4
                        p.wait("vector", s_pe)
                        p.wait("vector", s_act)
                        p.step("vector", lambda e, h4=h4: e.reciprocal(
                            out=bufs["rec"][0:1, 0:256], in_=av_ps[h4][0:1, 256:512]), s_dve)
                        p.wait("tensor", s_dve)
                        p.wait("tensor", s_act)
                        p.step("tensor", lambda e: e.matmul(
                            bufs["rbc_ps"][:, 0:256], bufs["ones_r_f"][:],
                            bufs["rec"][0:1, 0:256], start=True, stop=True), s_pe)
                        p.wait("scalar", s_pe)
                        p.step("scalar", lambda e: e.activation(
                            out=bufs["recbc"][:, 0:256], in_=bufs["rbc_ps"][:, 0:256],
                            func=AF.Copy, bias=0.0, scale=1.0), s_act)
                        p.wait("vector", s_act)
                        p.step("vector", lambda e, h=h, h4=h4, qh=qh: e.tensor_tensor(
                            out=av_sb[:, h * 512 + qh * 256: h * 512 + qh * 256 + 256],
                            in0=av_ps[h4][:, 0:256], in1=bufs["recbc"][:, 0:256],
                            op=ALU.mult), s_dve)
            p.wait("sync", s_dve)
            for oc in range(16):
                p.dma("sync", io["av_slots"][oc * 128:(oc + 1) * 128, sl * 512:(sl + 1) * 512],
                      av_sb[:, oc * 512:(oc + 1) * 512], s_st)
    l0_es.close()
    with contextlib.ExitStack() as wo_es:
        wo = wo_es.enter_context(nc.sbuf_tensor("wo_sb", [128, 16 * 2048], BF16, side="right"))
        avs = wo_es.enter_context(nc.sbuf_tensor("avs", [128, 16 * 512], BF16, side="right"))
        xr = wo_es.enter_context(nc.sbuf_tensor("xr", [128, 2 * 512], F32, side="right"))
        x1o = wo_es.enter_context(nc.sbuf_tensor("x1o", [128, 2 * 512], F32, side="right"))
        wo_ps = wo_es.enter_context(nc.psum_tensor("wo_ps", [128, 512], F32))
        p.wait("gpsimd", s_st)
        for dc in range(16):
            p.dma("gpsimd", wo[:, dc * 2048:(dc + 1) * 2048],
                  io["wo0"][dc * 128:(dc + 1) * 128, :], s_gld, max_dma_last_dim=8192)
        p.wait("tensor", s_gld)
        for sl in range(NSLOT):
            p.wait("sync", s_pe)
            for oc in range(16):
                p.dma("sync", avs[:, oc * 512:(oc + 1) * 512],
                      io["av_slots"][oc * 128:(oc + 1) * 128, sl * 512:(sl + 1) * 512], s_ld)
            p.wait("tensor", s_ld)
            for oc in range(16):
                pb = oc % 2
                p.wait("sync", s_dve)
                p.dma("sync", xr[:, pb * 512:(pb + 1) * 512],
                      io["x_slots"][oc * 128:(oc + 1) * 128, sl * 512:(sl + 1) * 512], s_ld)
                p.wait("tensor", s_dve)
                emit_proj(p, wo, 2048, oc, avs, 512, 0, 512, wo_ps, s_pe)
                p.wait("vector", s_pe)
                p.wait("vector", s_ld)
                p.wait("vector", s_st)
                p.step("vector", lambda e, pb=pb: e.tensor_tensor(
                    out=x1o[:, pb * 512:(pb + 1) * 512], in0=wo_ps[:, 0:512],
                    in1=xr[:, pb * 512:(pb + 1) * 512], op=ALU.add), s_dve)
                p.wait("sync", s_dve)
                p.dma("sync", io["x1_slots"][oc * 128:(oc + 1) * 128, sl * 512:(sl + 1) * 512],
                      x1o[:, pb * 512:(pb + 1) * 512], s_st)
    p.wait("gpsimd", s_st)
    _pes.close()
    if env is None:
        p.build()
    return p, io


def build_p2(env=None, debug=False, up_only=False):
    if env is None:
        p = Prog()
        io = {}
    else:
        p = env["p"]
        io = env["io"]
    _pes = contextlib.ExitStack()
    def _psb(name, shape, dtype):
        return _pes.enter_context(p.nc.sbuf_tensor("p2" + name, list(shape), dtype, side="right"))
    def _pps(name, shape, dtype=F32):
        return _pes.enter_context(p.nc.psum_tensor("p2" + name, list(shape), dtype))
    def _dram(name, shape, dtype, kind=None):
        if name not in io:
            kw = {} if kind is None else {"kind": kind}
            io[name] = p.nc.dram_tensor(name, list(shape), dtype, **kw).ap()
        return io[name]
    nc = p.nc
    _dram("x1_slots", [D, NSLOT * BLK], F32, "ExternalInput")
    _dram("wup_b", [64 * 128, 16 * 128], U8, "ExternalInput")
    _dram("wdn_b", [16 * 128, 64 * 128], U8, "ExternalInput")
    _dram("sclT_u", [32, 8192], F32, "ExternalInput")
    _dram("sclT_d", [128, 2048], F32, "ExternalInput")
    _dram("la_up", [D, 16], F32, "ExternalInput")
    _dram("lb_up", [16, 8192], F32, "ExternalInput")
    _dram("la_dn", [8192, 16], F32, "ExternalInput")
    _dram("lb_dn", [16, D], F32, "ExternalInput")
    _dram("bup_r", [128, 64], F32, "ExternalInput")
    _dram("bdn_r", [128, 16], F32, "ExternalInput")
    _dram("sel2", [2, 128], F32, "ExternalInput")
    for k, shp, dt in [("ident", [128, 128], BF16), ("ones_c_bf", [128, 1], BF16),
                       ("ones_c_f", [128, 1], F32), ("ones_r_f", [1, 128], F32),
                       ("ones_r_bf", [1, 128], BF16), ("cfix", [1, 8], F32)]:
        _dram(k, shp, dt, "ExternalInput")
    _dram("x2_slots", [D, NSLOT * BLK], F32, None if env is not None else "ExternalOutput")
    if debug:
        io["dbg_z"] = nc.dram_tensor("dbg_z", [128, 16 * 512], BF16, kind="ExternalOutput").ap()
        io["dbg_x2"] = nc.dram_tensor("dbg_x2", [128, 64 * 512], BF16, kind="ExternalOutput").ap()
        io["dbg_tlo"] = nc.dram_tensor("dbg_tlo", [16, 512], BF16, kind="ExternalOutput").ap()
        io["dbg_wdq"] = nc.dram_tensor("dbg_wdq", [128, 2048], BF16, kind="ExternalOutput").ap()
        io["dbg_mur"] = nc.dram_tensor("dbg_mur", [1, 512], BF16, kind="ExternalOutput").ap()
        io["dbg_ncb"] = nc.dram_tensor("dbg_ncb", [1, 144], BF16, kind="ExternalOutput").ap()
        io["dbg_la"] = nc.dram_tensor("dbg_la", [128, 256], BF16, kind="ExternalOutput").ap()
        io["dbg_x2b"] = nc.dram_tensor("dbg_x2b", [128, 64 * 512], BF16, kind="ExternalOutput").ap()
        io["dbg_x2c"] = nc.dram_tensor("dbg_x2c", [128, 512], BF16, kind="ExternalOutput").ap()
        io["dbg_w63"] = nc.dram_tensor("dbg_w63", [128, 2048], BF16, kind="ExternalOutput").ap()
        io["dbg_y63"] = nc.dram_tensor("dbg_y63", [128, 512], F32, kind="ExternalOutput").ap()
        io["dbg_z63"] = nc.dram_tensor("dbg_z63", [128, 8192], BF16, kind="ExternalOutput").ap()
        io["dbg_ddq"] = nc.dram_tensor("dbg_ddq", [128, 8192], BF16, kind="ExternalOutput").ap()

    sclu_v = io["sclT_u"].rearrange("(dc two) (hc h) -> two dc hc h", two=2, h=128)
    scld_v = io["sclT_d"].rearrange("(hc two) (dc dd) -> two hc dc dd", two=2, dd=128)

    bufs = {}
    bufs["xt"] = _psb("xt", [128, 16 * 256], F32)
    bufs["zt"] = _psb("zt", [128, 16 * 512], BF16)
    bufs["x2"] = _psb("x2", [128, 64 * 512], BF16)
    bufs["sq"] = _psb("sq", [128, 512], F32)
    bufs["rows"] = _psb("rows", [1, 4 * 512], F32)
    bufs["rbc"] = _psb("rbc", [128, 256], F32)
    bufs["mur"] = _psb("mur", [1, 512], BF16)
    bufs["la_u"] = _psb("la_u", [128, 16 * 16], BF16)
    bufs["lb_u"] = _psb("lb_u", [16, 8192], BF16)
    bufs["la_d"] = _psb("la_d", [128, 64 * 16], BF16)
    bufs["lb_d"] = _psb("lb_d", [16, 2048], BF16)
    bufs["bup"] = _psb("bup", [128, 64], F32)
    bufs["bdn"] = _psb("bdn", [128, 16], F32)
    bufs["sel2"] = _psb("sel2_sb", [2, 128], F32)
    bufs["tlo"] = _psb("tlo", [16, 512], BF16)
    bufs["ncb"] = _psb("ncb", [1, 16 + 128], BF16)
    bufs["xr"] = _psb("xr", [128, 512], F32)
    bufs["x2o"] = _psb("x2o", [128, 512], F32)
    for k in ["ident", "ones_c_bf", "ones_c_f", "ones_r_f", "ones_r_bf", "cfix"]:
        bufs[k] = _psb(k + "_sb", io[k].tensor.shape, io[k].dtype)
    bufs["stat_ps"] = _pps("stat_ps", [128, 512])
    bufs["rbc_ps"] = _pps("rbc_ps", [128, 512])
    y_ps = _pps("y_ps", [128, 512])
    sc_ps = _pps("sc_ps", [128, 512])
    tlo_ps = _pps("tlo_ps", [128, 512])

    if env is None:
        s_ld = p.sem("ld", queue="sync"); s_pe = p.sem("pe"); s_act = p.sem("act")
        s_dve = p.sem("dve"); s_gp = p.sem("gp"); s_st = p.sem("st", queue="sync")
        s_gld = p.sem("gld", queue="gpsimd")
    else:
        s_ld = env["s_ld"]; s_pe = env["s_pe"]; s_act = env["s_act"]
        s_dve = env["s_dve"]; s_gp = env["s_gp"]; s_st = env["s_st"]; s_gld = env["s_gld"]
    bufs["s_ld"] = s_ld

    emit_load_consts(p, io, bufs)
    p.dma("sync", bufs["bup"][:], io["bup_r"], s_ld)
    p.dma("sync", bufs["bdn"][:], io["bdn_r"], s_ld)
    p.dma_small("sync", bufs["sel2"][:], io["sel2"], s_ld)
    for dc in range(16):
        p.dma("gpsimd", bufs["la_u"][:, dc * 16:(dc + 1) * 16],
              io["la_up"][dc * 128:(dc + 1) * 128, :], s_gld)
    p.dma("gpsimd", bufs["lb_u"][:], io["lb_up"], s_gld, max_dma_last_dim=8192)
    for hc in range(64):
        p.dma("gpsimd", bufs["la_d"][:, hc * 16:(hc + 1) * 16],
              io["la_dn"][hc * 128:(hc + 1) * 128, :], s_gld)
    p.dma("gpsimd", bufs["lb_d"][:], io["lb_dn"], s_gld, max_dma_last_dim=8192)
    p.wait("tensor", s_ld)
    p.wait("tensor", s_gld)
    p.wait("scalar", s_ld)
    p.wait("vector", s_ld)
    p.wait("gpsimd", s_ld)

    for dc in range(16):
        p.step("tensor", lambda e, dc=dc: e.matmul(
            bufs["stat_ps"][0:1, 0:16], bufs["ones_c_bf"][:],
            bufs["la_u"][:, dc * 16:(dc + 1) * 16],
            start=(dc == 0), stop=(dc == 15)), s_pe if dc == 15 else None)
    p.wait("scalar", s_pe)
    p.step("scalar", lambda e: e.activation(
        out=bufs["ncb"][0:1, 0:16], in_=bufs["stat_ps"][0:1, 0:16],
        func=AF.Copy, bias=0.0, scale=-1.0), s_act)

    for sl in range(NSLOT):
        p.wait("sync", s_gp)
        p.wait("sync", s_pe)
        for hh in range(2):
            for dc in range(16):
                p.dma("sync", bufs["xt"][:, dc * 256:(dc + 1) * 256],
                      io["x1_slots"][dc * 128:(dc + 1) * 128,
                                     sl * 512 + hh * 256:sl * 512 + hh * 256 + 256], s_ld)
            p.wait("scalar", s_ld)
            p.wait("tensor", s_ld)
            p.wait("gpsimd", s_ld)
            emit_ln(p, bufs, bufs["xt"], 256, 0, bufs["zt"], 512, hh * 256, 256,
                    bufs["mur"], hh * 256, s_pe, s_act, s_dve, s_gp)
            if hh == 0:
                p.wait("sync", s_gp)
                p.wait("sync", s_pe)
        p.wait("tensor", s_gp)
        p.wait("tensor", s_act)
        p.wait("tensor", s_dve)
        for dc in range(16):
            p.step("tensor", lambda e, dc=dc: e.matmul(
                tlo_ps[0:16, 0:512], bufs["la_u"][:, dc * 16:(dc + 1) * 16],
                bufs["zt"][:, dc * 512:(dc + 1) * 512],
                start=(dc == 0), stop=False))
        p.step("tensor", lambda e: e.matmul(
            tlo_ps[0:16, 0:512], bufs["ncb"][0:1, 0:16], bufs["mur"][0:1, 0:512],
            start=False, stop=True), s_pe)
        p.wait("scalar", s_pe)
        p.step("scalar", lambda e: e.activation(
            out=bufs["tlo"][:], in_=tlo_ps[0:16, 0:512], func=AF.Copy,
            bias=0.0, scale=1.0), s_act)
        if debug and sl == 0:
            p.wait("sync", s_act)
            p.dma("sync", io["dbg_tlo"], bufs["tlo"][:], s_st)
        with contextlib.ExitStack() as up_es:
            ubnd = up_es.enter_context(nc.sbuf_tensor(f"ubnd{sl}", [128, 2048], BF16, side="right"))
            wdq = up_es.enter_context(nc.sbuf_tensor(f"wdq{sl}", [128, 2048], BF16, side="right"))
            urows = up_es.enter_context(nc.sbuf_tensor(f"urows{sl}", [2, 2048], F32, side="right"))
            for hc in range(64):
                p.wait("gpsimd", s_dve)
                p.dma("gpsimd", ubnd[:], io["wup_b"][hc * 128:(hc + 1) * 128, :], s_gld,
                      max_dma_last_dim=2048)
                p.wait("sync", s_pe)
                p.dma_small("sync", urows[:].rearrange("two (dc h) -> two dc h", h=128),
                      sclu_v[:, :, hc, :], s_ld)
                p.wait("tensor", s_ld)
                p.wait("tensor", s_gld)
                for ss in range(4):
                    p.wait("tensor", s_dve)
                    p.step("tensor", lambda e, ss=ss: e.matmul(
                        sc_ps[:, 0:512], bufs["sel2"][:], urows[:, ss * 512:(ss + 1) * 512],
                        start=True, stop=True), s_pe)
                    p.wait("vector", s_pe)
                    p.wait("vector", s_gld)
                    p.step("vector", lambda e, ss=ss: e.scalar_tensor_tensor(
                        out=wdq[:, ss * 512:(ss + 1) * 512], in0=ubnd[:, ss * 512:(ss + 1) * 512],
                        scalar=7.5, in1=sc_ps[:, 0:512], op0=ALU.subtract, op1=ALU.mult), s_dve)
                if debug and sl == 0 and hc == 0:
                    p.wait("sync", s_dve)
                    p.dma("sync", io["dbg_wdq"], wdq[:], s_st)
                if debug and sl == 0 and hc == 63:
                    p.wait("sync", s_dve)
                    p.dma("sync", io["dbg_w63"], wdq[:], s_st)
                    p.wait("tensor", s_st)
                p.wait("tensor", s_dve)
                p.wait("tensor", s_act)
                for dc in range(16):
                    p.step("tensor", lambda e, dc=dc: e.matmul(
                        bufs["stat_ps"][0:1, 0:128], bufs["ones_c_bf"][:],
                        wdq[:, dc * 128:(dc + 1) * 128],
                        start=(dc == 0), stop=(dc == 15)), s_pe if dc == 15 else None)
                p.wait("scalar", s_pe)
                p.step("scalar", lambda e: e.activation(
                    out=bufs["ncb"][0:1, 16:144], in_=bufs["stat_ps"][0:1, 0:128],
                    func=AF.Copy, bias=0.0, scale=-1.0), s_act)
                p.wait("tensor", s_act)
                for dc in range(16):
                    p.step("tensor", lambda e, dc=dc: e.matmul(
                        y_ps[:, 0:512], wdq[:, dc * 128:(dc + 1) * 128],
                        bufs["zt"][:, dc * 512:(dc + 1) * 512],
                        start=(dc == 0), stop=False))
                p.step("tensor", lambda e, hc=hc: e.matmul(
                    y_ps[:, 0:512], bufs["lb_u"][:, hc * 128:(hc + 1) * 128],
                    bufs["tlo"][:], start=False, stop=False))
                p.step("tensor", lambda e: e.matmul(
                    y_ps[:, 0:512], bufs["ncb"][0:1, 16:144], bufs["mur"][0:1, 0:512],
                    start=False, stop=True), s_pe)
                p.wait("scalar", s_pe)
                if debug and sl == 0 and hc == 63:
                    p.step("scalar", lambda e: e.activation(
                        out=bufs["xr"][:], in_=y_ps[:, 0:512],
                        func=AF.Copy, bias=0.0, scale=1.0), s_act)
                    p.wait("sync", s_act)
                    p.dma("sync", io["dbg_y63"], bufs["xr"][:], s_st)
                    p.dma("sync", io["dbg_z63"], bufs["zt"][:], s_st)
                    p.wait("scalar", s_st)
                p.step("scalar", lambda e, hc=hc: e.activation(
                    out=bufs["x2"][:, hc * 512:(hc + 1) * 512], in_=y_ps[:, 0:512],
                    func=AF.Relu, bias=bufs["bup"][:, hc:hc + 1], scale=1.0), s_act)
        if debug and sl == 0:
            p.wait("sync", s_act)
            p.dma("sync", io["dbg_mur"], bufs["mur"][:], s_st)
            p.dma("sync", io["dbg_ncb"], bufs["ncb"][:], s_st)
            p.dma("sync", io["dbg_la"], bufs["la_u"][:], s_st)
            p.dma("sync", io["dbg_z"], bufs["zt"][:], s_st)
            p.dma("sync", io["dbg_x2"], bufs["x2"][:], s_st)
        if up_only:
            continue
        p.wait("tensor", s_act)
        p.wait("tensor", s_dve)
        for hc in range(64):
            p.step("tensor", lambda e, hc=hc: e.matmul(
                tlo_ps[0:16, 0:512], bufs["la_d"][:, hc * 16:(hc + 1) * 16],
                bufs["x2"][:, hc * 512:(hc + 1) * 512],
                start=(hc == 0), stop=(hc == 63)), s_pe if hc == 63 else None)
        p.wait("scalar", s_pe)
        p.step("scalar", lambda e: e.activation(
            out=bufs["tlo"][:], in_=tlo_ps[0:16, 0:512], func=AF.Copy,
            bias=0.0, scale=1.0), s_act)
        with contextlib.ExitStack() as dn_es:
            dbnd = dn_es.enter_context(nc.sbuf_tensor(f"dbnd{sl}", [128, 8192], BF16, side="right"))
            ddq = dn_es.enter_context(nc.sbuf_tensor(f"ddq{sl}", [128, 8192], BF16, side="right"))
            drows = dn_es.enter_context(nc.sbuf_tensor(f"drows{sl}", [2, 8192], F32, side="right"))
            for dc in range(16):
                p.wait("gpsimd", s_dve)
                p.wait("gpsimd", s_pe)
                p.dma("gpsimd", dbnd[:], io["wdn_b"][dc * 128:(dc + 1) * 128, :], s_gld,
                      max_dma_last_dim=2048)
                p.wait("sync", s_pe)
                p.dma_small("sync", drows[:].rearrange("two (hc h) -> two hc h", h=128),
                      scld_v[:, :, dc, :], s_ld)
                p.wait("tensor", s_ld)
                p.wait("tensor", s_gld)
                for ss in range(16):
                    p.wait("tensor", s_dve)
                    p.step("tensor", lambda e, ss=ss: e.matmul(
                        sc_ps[:, 0:512], bufs["sel2"][:], drows[:, ss * 512:(ss + 1) * 512],
                        start=True, stop=True), s_pe)
                    p.wait("vector", s_pe)
                    p.wait("vector", s_gld)
                    p.step("vector", lambda e, ss=ss: e.scalar_tensor_tensor(
                        out=ddq[:, ss * 512:(ss + 1) * 512], in0=dbnd[:, ss * 512:(ss + 1) * 512],
                        scalar=7.5, in1=sc_ps[:, 0:512], op0=ALU.subtract, op1=ALU.mult), s_dve)
                if debug and sl == 0 and dc == 0:
                    p.wait("sync", s_dve)
                    p.dma("sync", io["dbg_ddq"], ddq[:], s_st)
                p.wait("tensor", s_dve)
                for j in range(64):
                    p.step("tensor", lambda e, j=j: e.matmul(
                        y_ps[:, 0:512], ddq[:, j * 128:(j + 1) * 128],
                        bufs["x2"][:, j * 512:(j + 1) * 512],
                        start=(j == 0), stop=False))
                p.step("tensor", lambda e, dc=dc: e.matmul(
                    y_ps[:, 0:512], bufs["lb_d"][:, dc * 128:(dc + 1) * 128],
                    bufs["tlo"][:], start=False, stop=True), s_pe)
                p.wait("sync", s_dve)
                p.dma("sync", bufs["xr"][:],
                      io["x1_slots"][dc * 128:(dc + 1) * 128, sl * 512:(sl + 1) * 512], s_ld)
                p.wait("vector", s_pe)
                p.wait("vector", s_ld)
                p.wait("vector", s_st)
                p.step("vector", lambda e, dc=dc: e.scalar_tensor_tensor(
                    out=bufs["x2o"][:], in0=y_ps[:, 0:512], scalar=bufs["bdn"][:, dc:dc + 1],
                    in1=bufs["xr"][:], op0=ALU.add, op1=ALU.add), s_dve)
                p.wait("sync", s_dve)
                p.dma("sync", io["x2_slots"][dc * 128:(dc + 1) * 128, sl * 512:(sl + 1) * 512],
                      bufs["x2o"][:], s_st)
                if debug and sl == 0 and dc == 0:
                    p.dma("sync", io["dbg_x2c"], bufs["x2"][:, 63 * 512:64 * 512], s_st)
        if debug and sl == 0:
            p.wait("sync", s_dve)
            p.dma("sync", io["dbg_x2b"], bufs["x2"][:], s_st)
    p.wait("gpsimd", s_st)
    _pes.close()
    if env is None:
        p.build()
    return p, io


def emit_proj_cols(p, bufs, w_dram, w_ncols, oc, zt, zst, zoff, nt, out_ps,
                   s_ld2, s_pe, s_act, wcol, do_rank1, mur_bf=None, mur_off=0):
    p.wait("gpsimd", s_pe)
    for dc in range(16):
        p.dma("gpsimd", wcol[:, dc * 128:(dc + 1) * 128],
              w_dram[dc * 128:(dc + 1) * 128, oc * 128:(oc + 1) * 128], s_ld2)
    p.wait("tensor", s_ld2)
    if do_rank1:
        p.wait("tensor", bufs["s_dve_ln"])
        p.wait("tensor", s_act)
        for dc in range(16):
            p.step("tensor", lambda e, dc=dc: e.matmul(
                bufs["stat_ps"][0:1, 0:128], bufs["ones_c_bf"][:],
                wcol[:, dc * 128:(dc + 1) * 128],
                start=(dc == 0), stop=(dc == 15)), s_pe if dc == 15 else None)
        p.wait("scalar", s_pe)
        p.step("scalar", lambda e: e.activation(
            out=bufs["ncb1"][0:1, 0:128], in_=bufs["stat_ps"][0:1, 0:128],
            func=AF.Copy, bias=0.0, scale=-1.0), s_act)
        p.wait("tensor", s_act)
    for dc in range(16):
        last = (dc == 15) and not do_rank1
        p.step("tensor", lambda e, dc=dc, last=last: e.matmul(
            out_ps[:, 0:nt], wcol[:, dc * 128:(dc + 1) * 128],
            zt[:, dc * zst + zoff:dc * zst + zoff + nt],
            start=(dc == 0), stop=last), s_pe if last else None)
    if do_rank1:
        p.step("tensor", lambda e: e.matmul(
            out_ps[:, 0:nt], bufs["ncb1"][0:1, 0:128],
            mur_bf[0:1, mur_off:mur_off + nt], start=False, stop=True), s_pe)


def build_p3(env=None):
    if env is None:
        p = Prog()
        io = {}
    else:
        p = env["p"]
        io = env["io"]
    _pes = contextlib.ExitStack()
    def _psb(name, shape, dtype):
        return _pes.enter_context(p.nc.sbuf_tensor("p3" + name, list(shape), dtype, side="right"))
    def _pps(name, shape, dtype=F32):
        return _pes.enter_context(p.nc.psum_tensor("p3" + name, list(shape), dtype))
    def _dram(name, shape, dtype, kind=None):
        if name not in io:
            kw = {} if kind is None else {"kind": kind}
            io[name] = p.nc.dram_tensor(name, list(shape), dtype, **kw).ap()
        return io[name]
    nc = p.nc
    _dram("x2_slots", [D, NSLOT * BLK], F32, "ExternalInput")
    _dram("wqT1", [D, D], F32, "ExternalInput")
    _dram("wkvT1", [D, 1024], F32, "ExternalInput")
    _dram("woT1", [D, D], F32, "ExternalInput")
    _dram("rope_cos", [128, NSLOT * BLK], F32, "ExternalInput")
    _dram("rope_sin", [128, NSLOT * BLK], F32, "ExternalInput")
    _dram("l1mask", [128, 4 * 4 * 1024], BF16, "ExternalInput")
    _dram("swapm", [128, 128], BF16, "ExternalInput")
    for k, shp, dt in [("ident", [128, 128], BF16), ("ones_c_bf", [128, 1], BF16),
                       ("ones_c_f", [128, 1], F32), ("ones_r_f", [1, 128], F32),
                       ("ones_r_bf", [1, 128], BF16), ("cfix", [1, 8], F32),
                       ("ccol", [128, 4], F32)]:
        _dram(k, shp, dt, "ExternalInput")
    if env is None:
        io["av3_slots"] = nc.dram_tensor("av_slots", [D, 4 * BLK], BF16, kind="ExternalOutput").ap()
    else:
        io["av3_slots"] = nc.dram_tensor("av3_slots", [D, 4 * BLK], BF16).ap()
    _dram("x3_slots", [D, 4 * BLK], F32, None if env is not None else "ExternalOutput")

    bufs = {}
    bufs["xt"] = _psb("xt", [128, 16 * 256], F32)
    bufs["zt"] = _psb("zt", [128, 16 * 512], BF16)
    bufs["kbuf"] = _psb("kbuf", [128, 4 * NSLOT * 512], BF16)
    bufs["vbuf"] = _psb("vbuf", [128, 4 * NSLOT * 512], BF16)
    bufs["probs"] = _psb("probs", [128, 3 * 1024], BF16)
    bufs["tnh"] = _psb("tnh", [128, 1024], F32)
    bufs["maskslot"] = _psb("maskslot", [128, 4 * 1024], BF16)
    bufs["sq"] = _psb("sq", [128, 512], F32)
    bufs["rows"] = _psb("rows", [1, 4 * 512], F32)
    bufs["rbc"] = _psb("rbc", [128, 256], F32)
    bufs["rec"] = _psb("rec", [1, 256], F32)
    bufs["recbc"] = _psb("recbc", [128, 256], F32)
    bufs["mur"] = _psb("mur", [1, 512], BF16)
    bufs["ncol_q"] = _psb("ncol_q", [1, 2048], BF16)
    bufs["ncol_kv"] = _psb("ncol_kv", [1, 1024], BF16)
    bufs["rc"] = _psb("rc", [128, 512], F32)
    bufs["rs"] = _psb("rs", [128, 512], F32)
    bufs["psc"] = _psb("psc", [128, 512], BF16)
    bufs["rt1"] = _psb("rt1", [128, 512], F32)
    bufs["rt2"] = _psb("rt2", [128, 512], F32)
    bufs["swap"] = _psb("swap_sb", [128, 128], BF16)
    bufs["ncb1"] = _psb("ncb1", [1, 128], BF16)
    bufs["wcol"] = _psb("wcol", [128, 16 * 128], BF16)
    for k in ["ident", "ones_c_bf", "ones_c_f", "ones_r_f", "ones_r_bf", "cfix", "ccol"]:
        bufs[k] = _psb(k + "_sb", io[k].tensor.shape, io[k].dtype)
    bufs["stat_ps"] = _pps("stat_ps", [128, 512])
    bufs["rbc_ps"] = _pps("rbc_ps", [128, 512])
    sc_ps = _pps("sc_ps", [128, 1024])

    if env is None:
        s_ld = p.sem("ld", queue="sync"); s_pe = p.sem("pe"); s_act = p.sem("act")
        s_dve = p.sem("dve"); s_gp = p.sem("gp"); s_st = p.sem("st", queue="sync")
        s_gld = p.sem("gld", queue="gpsimd")
    else:
        s_ld = env["s_ld"]; s_pe = env["s_pe"]; s_act = env["s_act"]
        s_dve = env["s_dve"]; s_gp = env["s_gp"]; s_st = env["s_st"]; s_gld = env["s_gld"]
    bufs["s_ld"] = s_ld

    emit_load_consts(p, io, bufs)
    p.dma("sync", bufs["ccol"][:], io["ccol"], s_ld)
    p.dma("sync", bufs["swap"][:], io["swapm"], s_ld)

    def rope_emit(src_bf, dst_bf, qsw_ps):
        p.wait("vector", s_pe)
        p.step("vector", lambda e: e.tensor_tensor(
            out=bufs["rt1"][:], in0=qsw_ps, in1=bufs["rs"][:], op=ALU.mult), s_dve)
        p.wait("vector", s_dve)
        p.wait("gpsimd", s_act)
        p.wait("gpsimd", s_dve)
        p.step("gpsimd", lambda e: e.tensor_tensor(
            out=bufs["rt2"][:], in0=src_bf, in1=bufs["rc"][:], op=ALU.mult), s_gp)
        p.wait("vector", s_gp)
        p.step("vector", lambda e: e.tensor_tensor(
            out=dst_bf, in0=bufs["rt1"][:], in1=bufs["rt2"][:], op=ALU.add), s_dve)

    bufs["s_dve_ln"] = s_dve
    with contextlib.ExitStack() as l1_es:
        q_sb = l1_es.enter_context(nc.sbuf_tensor("q1", [128, 16 * 512], BF16, side="right"))
        av_sb = l1_es.enter_context(nc.sbuf_tensor("av1", [128, 16 * 512], BF16, side="right"))
        p.wait("tensor", s_ld)
        p.wait("scalar", s_ld)
        p.wait("vector", s_ld)
        p.wait("gpsimd", s_ld)

        for sl in range(NSLOT):
            p.wait("sync", s_gp)
            p.wait("sync", s_dve)
            p.dma("sync", bufs["rc"][:], io["rope_cos"][:, sl * 512:(sl + 1) * 512], s_ld)
            p.dma("sync", bufs["rs"][:], io["rope_sin"][:, sl * 512:(sl + 1) * 512], s_ld)
            p.wait("sync", s_pe)
            for hh in range(2):
                for dc in range(16):
                    p.dma("sync", bufs["xt"][:, dc * 256:(dc + 1) * 256],
                          io["x2_slots"][dc * 128:(dc + 1) * 128,
                                         sl * 512 + hh * 256:sl * 512 + hh * 256 + 256], s_ld)
                p.wait("scalar", s_ld)
                p.wait("tensor", s_ld)
                p.wait("gpsimd", s_ld)
                emit_ln(p, bufs, bufs["xt"], 256, 0, bufs["zt"], 512, hh * 256, 256,
                        bufs["mur"], hh * 256, s_pe, s_act, s_dve, s_gp)
                if hh == 0:
                    p.wait("sync", s_gp)
                    p.wait("sync", s_pe)
            p.wait("tensor", s_gp)
            with contextlib.ExitStack() as kv_es:
                vt_ps = kv_es.enter_context(nc.psum_tensor(f"vt_ps{sl}", [128, 1024], BF16))
                for oc in range(8):
                    g = oc % 4
                    p.wait("tensor", s_act)
                    p.wait("tensor", s_dve)
                    emit_proj_cols(p, bufs, io["wkvT1"], 1024, oc, bufs["zt"], 512, 0, 512,
                                   sc_ps[:, 0:512], s_gld, s_pe, s_act, bufs["wcol"], True,
                                   mur_bf=bufs["mur"], mur_off=0)
                    p.wait("scalar", s_pe)
                    p.wait("scalar", s_dve)
                    p.wait("scalar", s_gp)
                    p.step("scalar", lambda e: e.activation(
                        out=bufs["psc"][:], in_=sc_ps[:, 0:512], func=AF.Copy,
                        bias=0.0, scale=1.0), s_act)
                    if oc < 4:
                        p.wait("tensor", s_act)
                        p.step("tensor", lambda e: e.matmul(
                            sc_ps[:, 512:1024], bufs["swap"][:], bufs["psc"][:],
                            start=True, stop=True), s_pe)
                        rope_emit(bufs["psc"][:],
                                  bufs["kbuf"][:, (g * NSLOT + sl) * 512:(g * NSLOT + sl) * 512 + 512],
                                  sc_ps[:, 512:1024])
                    else:
                        p.wait("tensor", s_act)
                        for half in range(4):
                            p.step("tensor", lambda e, half=half: e.transpose(
                                vt_ps[:, half * 128:(half + 1) * 128],
                                bufs["psc"][:, half * 128:(half + 1) * 128],
                                bufs["ident"][:]), s_pe if half == 3 else None)
                        p.wait("vector", s_pe)
                        p.step("vector", lambda e, g=g, sl=sl: e.tensor_copy(
                            out=bufs["vbuf"][:, (g * NSLOT + sl) * 512:(g * NSLOT + sl) * 512 + 512],
                            in_=vt_ps[:, 0:512]), s_dve)
                        p.wait("tensor", s_dve)
            if sl == 0:
                continue
            for oc in range(16):
                p.wait("tensor", s_act)
                p.wait("tensor", s_dve)
                emit_proj_cols(p, bufs, io["wqT1"], 2048, oc, bufs["zt"], 512, 0, 512,
                               sc_ps[:, 0:512], s_gld, s_pe, s_act, bufs["wcol"], True,
                               mur_bf=bufs["mur"], mur_off=0)
                p.wait("scalar", s_pe)
                p.wait("scalar", s_dve)
                p.wait("scalar", s_gp)
                p.step("scalar", lambda e: e.activation(
                    out=bufs["psc"][:], in_=sc_ps[:, 0:512], func=AF.Copy,
                    bias=0.0, scale=1.0), s_act)
                p.wait("tensor", s_act)
                p.step("tensor", lambda e: e.matmul(
                    sc_ps[:, 512:1024], bufs["swap"][:], bufs["psc"][:],
                    start=True, stop=True), s_pe)
                rope_emit(bufs["psc"][:], q_sb[:, oc * 512:(oc + 1) * 512], sc_ps[:, 512:1024])
            for _eng in ("tensor", "scalar", "vector", "gpsimd", "sync"):
                for _sem in (s_ld, s_gld, s_pe, s_act, s_dve, s_gp, s_st):
                    p.wait(_eng, _sem)
            qi = sl - 1
            p.wait("sync", s_dve)
            p.wait("sync", s_gp)
            p.dma("sync", bufs["maskslot"][:], io["l1mask"][:, qi * 4096:(qi + 1) * 4096], s_ld)
            with contextlib.ExitStack() as at_es:
                av_ps = [at_es.enter_context(nc.psum_tensor(f"av3_{sl}_{h}", [128, 512], F32))
                         for h in range(4)]
                p.wait("tensor", s_ld)
                for g in range(4):
                    for qh in range(2):
                        for _eng in ("tensor", "scalar", "vector", "gpsimd"):
                            for _sem in (s_ld, s_pe, s_act, s_dve, s_gp):
                                p.wait(_eng, _sem)
                        for h4 in range(4):
                            h = g * 4 + h4
                            for b in range(2):
                                for _eng in ("tensor", "scalar", "vector", "gpsimd"):
                                    for _sem in (s_pe, s_act, s_dve, s_gp):
                                        p.wait(_eng, _sem)
                                kvs = sl - 1 + b
                                pb = b % 2
                                p.wait("tensor", s_act)
                                p.wait("tensor", s_dve)
                                p.wait("tensor", s_gp)
                                for j in range(4):
                                    p.step("tensor", lambda e, g=g, kvs=kvs, j=j, h=h, qh=qh: e.matmul(
                                        sc_ps[:, j * 256:(j + 1) * 256],
                                        bufs["kbuf"][:, (g * NSLOT + kvs) * 512 + j * 128:
                                                     (g * NSLOT + kvs) * 512 + (j + 1) * 128],
                                        q_sb[:, h * 512 + qh * 256: h * 512 + qh * 256 + 256],
                                        start=(j % 2 == 0), stop=(j % 2 == 1)),
                                        s_pe if j == 3 else None)
                                p.wait("scalar", s_pe)
                                p.wait("scalar", s_dve)
                                p.wait("scalar", s_gp)
                                p.step("scalar", lambda e: e.activation(
                                    out=bufs["tnh"][:], in_=sc_ps[:, 0:1024], func=AF.Tanh,
                                    scale=SCALE0 / CAP1), s_act)
                                p.wait("scalar", s_act)
                                p.step("scalar", lambda e, pb=pb: e.activation(
                                    out=bufs["probs"][:, pb * 1024:(pb + 1) * 1024],
                                    in_=bufs["tnh"][:], func=AF.Exp,
                                    bias=bufs["ccol"][:, 0:1], scale=CAP1), s_act)
                                eng = "vector" if b == 0 else "gpsimd"
                                sem = s_dve if b == 0 else s_gp
                                p.wait(eng, s_act)
                                p.wait(eng, s_pe)
                                p.step(eng, lambda e, pb=pb, b=b, qh=qh: e.tensor_tensor(
                                    out=bufs["probs"][:, 2048:3072],
                                    in0=bufs["probs"][:, pb * 1024:(pb + 1) * 1024],
                                    in1=bufs["maskslot"][:, (b * 2 + qh) * 1024:(b * 2 + qh + 1) * 1024],
                                    op=ALU.mult), sem)
                                p.wait("tensor", sem)
                                first = (b == 0)
                                lastb = (b == 1)
                                def em_av(stop_on_j3, first=first, g=g, kvs=kvs, h4=h4):
                                    for j in range(4):
                                        stp = stop_on_j3 and (j == 3)
                                        p.step("tensor", lambda e, j=j, g=g, kvs=kvs, h4=h4, first=first, stp=stp: e.matmul(
                                            av_ps[h4][:, 0:256],
                                            bufs["vbuf"][:, (g * NSLOT + kvs) * 512 + j * 128:
                                                         (g * NSLOT + kvs) * 512 + (j + 1) * 128],
                                            bufs["probs"][:, 2048 + j * 256: 2048 + (j + 1) * 256],
                                            start=(first and j == 0), stop=stp),
                                            s_pe if stp else None)
                                def em_sums(h4=h4):
                                    for j in range(4):
                                        p.step("tensor", lambda e, h4=h4, j=j: e.matmul(
                                            av_ps[h4][0:1, 256:512],
                                            bufs["ones_c_bf"][:],
                                            bufs["probs"][:, 2048 + j * 256: 2048 + (j + 1) * 256],
                                            start=False, stop=False))
                                if lastb:
                                    em_sums()
                                    em_av(True)
                                else:
                                    em_av(False)
                                    em_sums()
                        for _eng in ("tensor", "scalar", "vector", "gpsimd"):
                            for _sem in (s_ld, s_pe, s_act, s_dve, s_gp):
                                p.wait(_eng, _sem)
                        for h4 in range(4):
                            h = g * 4 + h4
                            p.wait("vector", s_pe)
                            p.wait("vector", s_act)
                            p.step("vector", lambda e, h4=h4: e.reciprocal(
                                out=bufs["rec"][0:1, 0:256], in_=av_ps[h4][0:1, 256:512]), s_dve)
                            p.wait("tensor", s_dve)
                            p.wait("tensor", s_act)
                            p.step("tensor", lambda e: e.matmul(
                                bufs["rbc_ps"][:, 0:256], bufs["ones_r_f"][:],
                                bufs["rec"][0:1, 0:256], start=True, stop=True), s_pe)
                            p.wait("scalar", s_pe)
                            p.step("scalar", lambda e: e.activation(
                                out=bufs["recbc"][:, 0:256], in_=bufs["rbc_ps"][:, 0:256],
                                func=AF.Copy, bias=0.0, scale=1.0), s_act)
                            p.wait("vector", s_act)
                            p.step("vector", lambda e, h=h, h4=h4, qh=qh: e.tensor_tensor(
                                out=av_sb[:, h * 512 + qh * 256: h * 512 + qh * 256 + 256],
                                in0=av_ps[h4][:, 0:256], in1=bufs["recbc"][:, 0:256],
                                op=ALU.mult), s_dve)
            p.wait("sync", s_dve)
            for oc in range(16):
                p.dma("sync", io["av3_slots"][oc * 128:(oc + 1) * 128, qi * 512:(qi + 1) * 512],
                      av_sb[:, oc * 512:(oc + 1) * 512], s_st)
    with contextlib.ExitStack() as wo_es:
        avs = wo_es.enter_context(nc.sbuf_tensor("avs3", [128, 16 * 512], BF16, side="right"))
        xr = wo_es.enter_context(nc.sbuf_tensor("xr3", [128, 2 * 512], F32, side="right"))
        x3o = wo_es.enter_context(nc.sbuf_tensor("x3o", [128, 2 * 512], F32, side="right"))
        wo_ps = wo_es.enter_context(nc.psum_tensor("wo3_ps", [128, 512], F32))
        p.wait("gpsimd", s_st)
        p.wait("gpsimd", s_pe)
        for qi in range(4):
            p.wait("sync", s_pe)
            for oc in range(16):
                p.dma("sync", avs[:, oc * 512:(oc + 1) * 512],
                      io["av3_slots"][oc * 128:(oc + 1) * 128, qi * 512:(qi + 1) * 512], s_ld)
            p.wait("tensor", s_ld)
            for oc in range(16):
                pb = oc % 2
                p.wait("sync", s_dve)
                p.dma("sync", xr[:, pb * 512:(pb + 1) * 512],
                      io["x2_slots"][oc * 128:(oc + 1) * 128, (qi + 1) * 512:(qi + 2) * 512], s_ld)
                p.wait("tensor", s_dve)
                emit_proj_cols(p, bufs, io["woT1"], 2048, oc, avs, 512, 0, 512,
                               wo_ps, s_gld, s_pe, s_act, bufs["wcol"], False)
                p.wait("vector", s_pe)
                p.wait("vector", s_ld)
                p.wait("vector", s_st)
                p.step("vector", lambda e, pb=pb: e.tensor_tensor(
                    out=x3o[:, pb * 512:(pb + 1) * 512], in0=wo_ps[:, 0:512],
                    in1=xr[:, pb * 512:(pb + 1) * 512], op=ALU.add), s_dve)
                p.wait("sync", s_dve)
                p.dma("sync", io["x3_slots"][oc * 128:(oc + 1) * 128, qi * 512:(qi + 1) * 512],
                      x3o[:, pb * 512:(pb + 1) * 512], s_st)
    p.wait("gpsimd", s_st)
    _pes.close()
    if env is None:
        p.build()
    return p, io


def build_p4(env=None):
    if env is None:
        p = Prog()
        io = {}
    else:
        p = env["p"]
        io = env["io"]
    _pes = contextlib.ExitStack()
    def _psb(name, shape, dtype):
        return _pes.enter_context(p.nc.sbuf_tensor("p4" + name, list(shape), dtype, side="right"))
    def _pps(name, shape, dtype=F32):
        return _pes.enter_context(p.nc.psum_tensor("p4" + name, list(shape), dtype))
    def _dram(name, shape, dtype, kind=None):
        if name not in io:
            kw = {} if kind is None else {"kind": kind}
            io[name] = p.nc.dram_tensor(name, list(shape), dtype, **kw).ap()
        return io[name]
    nc = p.nc
    _dram("x3_slots", [D, 4 * BLK], F32, "ExternalInput")
    _dram("w1_b", [64 * 128, 16 * 128], F32, "ExternalInput")
    _dram("w2_b", [16 * 128, 64 * 128], F32, "ExternalInput")
    _dram("wr", [D, 8], F32, "ExternalInput")
    _dram("br_r", [8, 1], F32, "ExternalInput")
    _dram("b1_r", [128, 64], F32, "ExternalInput")
    _dram("b2_r", [1, D], F32, "ExternalInput")
    _dram("a1t_c", [8 * 128, 256], F32, "ExternalInput")
    _dram("a2t_c", [8 * 128, 1024], F32, "ExternalInput")
    _dram("b1t", [128, 8192], F32, "ExternalInput")
    _dram("b2t", [128, 2048], F32, "ExternalInput")
    _dram("sel8", [8, 8 * 128], BF16, "ExternalInput")
    for k, shp, dt in [("ident", [128, 128], BF16), ("ones_c_bf", [128, 1], BF16),
                       ("ones_c_f", [128, 1], F32), ("ones_r_f", [1, 128], F32),
                       ("ones_r_bf", [1, 128], BF16), ("cfix", [1, 8], F32)]:
        _dram(k, shp, dt, "ExternalInput")
    _dram("out_slots", [D, 4 * BLK], F32, "ExternalOutput")
    t2scr = nc.dram_tensor("t2scr", [16, 8 * 512], BF16)

    bufs = {}
    bufs["xt"] = _psb("xt", [128, 16 * 128], F32)
    bufs["base"] = _psb("base", [128, 64 * 512], BF16)
    bufs["mix"] = _psb("mix", [128, 64 * 512], BF16)
    bufs["sq"] = _psb("sq", [128, 512], F32)
    bufs["rows"] = _psb("rows", [1, 4 * 256], F32)
    bufs["rows_stride"] = 256
    bufs["rbc"] = _psb("rbc", [128, 256], F32)
    bufs["mur"] = _psb("mur", [1, 512], BF16)
    bufs["a1t"] = _psb("a1t", [128, 256], BF16)
    bufs["a2t"] = _psb("a2t", [128, 1024], BF16)
    bufs["b1t"] = _psb("b1t_sb", [16, 2048], BF16)
    bufs["b2t"] = _psb("b2t_sb", [128, 2048], BF16)
    bufs["wr"] = _psb("wr_sb", [128, 16 * 8], BF16)
    bufs["br"] = _psb("br_sb", [8, 1], F32)
    bufs["b1"] = _psb("b1_sb", [128, 64], F32)
    bufs["b2"] = _psb("b2_sb", [1, 2048], BF16)
    bufs["sel8"] = _psb("sel8_sb", [8, 8 * 128], BF16)
    bufs["t1"] = _psb("t1_sb", [16, 8 * 512], BF16)
    bufs["t2st"] = _psb("t2st", [16, 512], BF16)
    bufs["t2a"] = _psb("t2a", [128, 512], BF16)
    bufs["cbc"] = _psb("cbc_sb", [128, 512], BF16)
    bufs["ssb"] = _psb("ssb", [128, 512], BF16)
    bufs["asb"] = _psb("asb", [128, 512], BF16)
    bufs["ncb"] = _psb("ncb", [1, 512], BF16)
    bufs["lg"] = _psb("lg", [8, 512], BF16)
    bufs["lt"] = _psb("lt", [128, 32], F32)
    bufs["rtw"] = _psb("rtw", [128, 40], F32)
    bufs["cmb"] = _psb("cmb", [128, 8], BF16)
    bufs["comb8"] = _psb("comb8", [8, 512], BF16)
    bufs["csumr"] = _psb("csumr", [1, 512], BF16)
    bufs["xr"] = _psb("xr", [128, 512], F32)
    bufs["oout"] = _psb("oout", [128, 512], F32)
    for k in ["ident", "ones_c_bf", "ones_c_f", "ones_r_f", "ones_r_bf", "cfix"]:
        bufs[k] = _psb(k + "_sb", io[k].tensor.shape, io[k].dtype)
    bufs["stat_ps"] = _pps("stat_ps", [128, 512])
    bufs["rbc_ps"] = _pps("rbc_ps", [128, 512])
    y_ps = _pps("y_ps", [128, 512])
    t2_ps = _pps("t2_ps", [128, 512])
    tr_ps = _pps("tr_ps", [128, 512], BF16)
    cb_ps = _pps("cb_ps", [128, 512])

    if env is None:
        s_ld = p.sem("ld", queue="sync"); s_pe = p.sem("pe"); s_act = p.sem("act")
        s_dve = p.sem("dve"); s_gp = p.sem("gp"); s_st = p.sem("st", queue="sync")
        s_gld = p.sem("gld", queue="gpsimd")
    else:
        s_ld = env["s_ld"]; s_pe = env["s_pe"]; s_act = env["s_act"]
        s_dve = env["s_dve"]; s_gp = env["s_gp"]; s_st = env["s_st"]; s_gld = env["s_gld"]
    bufs["s_ld"] = s_ld

    emit_load_consts(p, io, bufs)
    p.dma("sync", bufs["br"][:], io["br_r"], s_ld)
    p.dma("sync", bufs["b1"][:], io["b1_r"], s_ld)
    p.dma("sync", bufs["sel8"][:], io["sel8"], s_ld)
    p.dma("gpsimd", bufs["b2"][:], io["b2_r"], s_gld)
    for dc in range(16):
        p.dma("gpsimd", bufs["wr"][:, dc * 8:(dc + 1) * 8],
              io["wr"][dc * 128:(dc + 1) * 128, :], s_gld)
    p.dma("gpsimd", bufs["b2t"][:], io["b2t"], s_gld, max_dma_last_dim=8192)
    p.wait("tensor", s_ld)
    p.wait("tensor", s_gld)
    p.wait("scalar", s_ld)
    p.wait("scalar", s_gld)
    p.wait("vector", s_ld)
    p.wait("gpsimd", s_ld)

    p.wait("tensor", s_dve)
    for dc in range(16):
        p.step("tensor", lambda e, dc=dc: e.matmul(
            bufs["stat_ps"][0:1, 0:8], bufs["ones_c_bf"][:],
            bufs["wr"][:, dc * 8:(dc + 1) * 8],
            start=(dc == 0), stop=(dc == 15)), s_pe if dc == 15 else None)
    p.wait("scalar", s_pe)
    p.step("scalar", lambda e: e.activation(
        out=bufs["ncb"][0:1, 0:8], in_=bufs["stat_ps"][0:1, 0:8],
        func=AF.Copy, bias=0.0, scale=-1.0), s_act)
    for ex in range(8):
        p.wait("gpsimd", s_pe)
        p.dma("gpsimd", bufs["a1t"][:], io["a1t_c"][ex * 128:(ex + 1) * 128, :], s_gld)
        p.wait("tensor", s_gld)
        p.wait("tensor", s_act)
        for dc in range(16):
            p.step("tensor", lambda e, ex=ex, dc=dc: e.matmul(
                bufs["stat_ps"][0:1, 0:16], bufs["ones_c_bf"][:],
                bufs["a1t"][:, dc * 16:(dc + 1) * 16],
                start=(dc == 0), stop=(dc == 15)), s_pe if dc == 15 else None)
        p.wait("scalar", s_pe)
        p.step("scalar", lambda e, ex=ex: e.activation(
            out=bufs["ncb"][0:1, 16 + ex * 16:16 + (ex + 1) * 16],
            in_=bufs["stat_ps"][0:1, 0:16],
            func=AF.Copy, bias=0.0, scale=-1.0), s_act)

    for sl in range(4):
        zsc = contextlib.ExitStack()
        ztl = zsc.enter_context(nc.sbuf_tensor(f"zt{sl}", [128, 16 * 512], BF16, side="right"))
        w1b = zsc.enter_context(nc.sbuf_tensor(f"w1b{sl}", [128, 2048], BF16, side="right"))
        bufs["zt"] = ztl
        p.wait("gpsimd", s_pe)
        for hh in range(4):
            p.wait("sync", s_gp)
            p.wait("sync", s_pe)
            for dc in range(16):
                p.dma("sync", bufs["xt"][:, dc * 128:(dc + 1) * 128],
                      io["x3_slots"][dc * 128:(dc + 1) * 128,
                                     sl * 512 + hh * 128:sl * 512 + hh * 128 + 128], s_ld)
            p.wait("scalar", s_ld)
            p.wait("tensor", s_ld)
            p.wait("gpsimd", s_ld)
            emit_ln(p, bufs, bufs["xt"], 128, 0, ztl, 512, hh * 128, 128,
                    bufs["mur"], hh * 128, s_pe, s_act, s_dve, s_gp)
        p.wait("tensor", s_gp)
        p.wait("tensor", s_act)
        p.wait("tensor", s_dve)
        for dc in range(16):
            p.step("tensor", lambda e, dc=dc, ztl=ztl: e.matmul(
                y_ps[0:8, 0:512], bufs["wr"][:, dc * 8:(dc + 1) * 8],
                ztl[:, dc * 512:(dc + 1) * 512],
                start=(dc == 0), stop=False))
        p.step("tensor", lambda e: e.matmul(
            y_ps[0:8, 0:512], bufs["ncb"][0:1, 0:8], bufs["mur"][0:1, 0:512],
            start=False, stop=True), s_pe)
        p.wait("scalar", s_pe)
        p.step("scalar", lambda e: e.activation(
            out=bufs["lg"][:], in_=y_ps[0:8, 0:512], func=AF.Identity,
            bias=bufs["br"][:], scale=1.0), s_act)
        p.wait("tensor", s_act)
        p.wait("tensor", s_dve)
        for ts in range(4):
            p.step("tensor", lambda e, ts=ts: e.transpose(
                tr_ps[:, ts * 128:ts * 128 + 8], bufs["lg"][:, ts * 128:(ts + 1) * 128],
                bufs["ident"][0:8, 0:8]), s_pe if ts == 3 else None)
        p.wait("vector", s_pe)
        for ts in range(4):
            p.step("vector", lambda e, ts=ts: e.tensor_copy(
                out=bufs["lt"][:, ts * 8:(ts + 1) * 8],
                in_=tr_ps[:, ts * 128:ts * 128 + 8]), s_dve)
        p.wait("tensor", s_dve)
        for ts in range(4):
            sl8 = bufs["lt"][:, ts * 8:(ts + 1) * 8]
            W = bufs["rtw"]
            def dv(fn):
                p.wait("vector", s_dve)
                p.step("vector", fn, s_dve)
            dv(lambda e, sl8=sl8: e.reduce_max(out=W[:, 0:1], in_=sl8, axis=AX.X))
            dv(lambda e, sl8=sl8: e.tensor_scalar(
                out=W[:, 8:16], in0=sl8, scalar1=W[:, 0:1], scalar2=None, op0=ALU.is_equal))
            dv(lambda e, sl8=sl8: e.scalar_tensor_tensor(
                out=W[:, 16:24], in0=W[:, 8:16], scalar=-1e30, in1=sl8,
                op0=ALU.mult, op1=ALU.add))
            dv(lambda e: e.reduce_max(out=W[:, 1:2], in_=W[:, 16:24], axis=AX.X))
            dv(lambda e: e.tensor_scalar(
                out=W[:, 2:3], in0=W[:, 0:1], scalar1=-1.0, scalar2=None, op0=ALU.mult))
            p.wait("scalar", s_dve)
            p.step("scalar", lambda e, sl8=sl8: e.activation(
                out=W[:, 24:32], in_=sl8, func=AF.Exp, bias=W[:, 2:3], scale=1.0), s_act)
            p.wait("vector", s_act)
            dv(lambda e: e.reduce_sum(out=W[:, 3:4], in_=W[:, 24:32], axis=AX.X))
            dv(lambda e: e.reciprocal(out=W[:, 4:5], in_=W[:, 3:4]))
            dv(lambda e, sl8=sl8: e.tensor_scalar(
                out=W[:, 16:24], in0=sl8, scalar1=W[:, 1:2], scalar2=None, op0=ALU.is_equal))
            dv(lambda e: e.tensor_tensor(
                out=W[:, 32:40], in0=W[:, 8:16], in1=W[:, 16:24], op=ALU.add))
            dv(lambda e: e.tensor_tensor(
                out=W[:, 8:16], in0=W[:, 24:32], in1=W[:, 32:40], op=ALU.mult))
            p.wait("vector", s_pe)
            dv(lambda e: e.tensor_scalar(
                out=bufs["cmb"][:], in0=W[:, 8:16], scalar1=W[:, 4:5], scalar2=None,
                op0=ALU.mult))
            p.wait("tensor", s_dve)
            p.step("tensor", lambda e, ts=ts: e.transpose(
                tr_ps[0:8, ts * 128:(ts + 1) * 128], bufs["cmb"][:], bufs["ident"][:]),
                s_pe)
        p.wait("scalar", s_pe)
        p.step("scalar", lambda e: e.activation(
            out=bufs["comb8"][:], in_=tr_ps[0:8, 0:512], func=AF.Copy,
            bias=0.0, scale=1.0), s_act)
        p.wait("tensor", s_act)
        p.wait("tensor", s_dve)
        p.step("tensor", lambda e: e.matmul(
            cb_ps[0:1, 0:512], bufs["ones_c_bf"][0:8, 0:1], bufs["comb8"][:],
            start=True, stop=True), s_pe)
        p.wait("scalar", s_pe)
        p.step("scalar", lambda e: e.activation(
            out=bufs["csumr"][:], in_=cb_ps[0:1, 0:512], func=AF.Copy,
            bias=0.0, scale=1.0), s_act)
        for ex in range(8):
            p.wait("gpsimd", s_pe)
            p.dma("gpsimd", bufs["a1t"][:], io["a1t_c"][ex * 128:(ex + 1) * 128, :], s_gld)
            p.wait("tensor", s_gld)
            p.wait("tensor", s_act)
            p.wait("tensor", s_dve)
            for dc in range(16):
                p.step("tensor", lambda e, ex=ex, dc=dc, ztl=ztl: e.matmul(
                    t2_ps[0:16, 0:512], bufs["a1t"][:, dc * 16:(dc + 1) * 16],
                    ztl[:, dc * 512:(dc + 1) * 512],
                    start=(dc == 0), stop=False))
            p.step("tensor", lambda e, ex=ex: e.matmul(
                t2_ps[0:16, 0:512], bufs["ncb"][0:1, 16 + ex * 16:16 + (ex + 1) * 16],
                bufs["mur"][0:1, 0:512], start=False, stop=True), s_pe)
            p.wait("scalar", s_pe)
            p.step("scalar", lambda e, ex=ex: e.activation(
                out=bufs["t1"][:, ex * 512:(ex + 1) * 512], in_=t2_ps[0:16, 0:512],
                func=AF.Copy, bias=0.0, scale=2.0), s_act)
        for hc in range(64):
            p.wait("gpsimd", s_pe)
            p.dma("gpsimd", w1b[:],
                  io["w1_b"][hc * 128:(hc + 1) * 128, :], s_gld, max_dma_last_dim=8192)
            p.wait("tensor", s_gld)
            p.wait("tensor", s_act)
            p.wait("tensor", s_dve)
            for dc in range(16):
                p.step("tensor", lambda e, dc=dc, w1b=w1b: e.matmul(
                    bufs["stat_ps"][0:1, 0:128], bufs["ones_c_bf"][:],
                    w1b[:, dc * 128:(dc + 1) * 128],
                    start=(dc == 0), stop=(dc == 15)), s_pe if dc == 15 else None)
            p.wait("scalar", s_pe)
            p.step("scalar", lambda e: e.activation(
                out=bufs["ncb"][0:1, 256:384], in_=bufs["stat_ps"][0:1, 0:128],
                func=AF.Copy, bias=0.0, scale=-1.0), s_act)
            p.wait("tensor", s_act)
            for dc in range(16):
                p.step("tensor", lambda e, dc=dc, w1b=w1b, ztl=ztl: e.matmul(
                    y_ps[:, 0:512], w1b[:, dc * 128:(dc + 1) * 128],
                    ztl[:, dc * 512:(dc + 1) * 512],
                    start=(dc == 0), stop=False))
            p.step("tensor", lambda e: e.matmul(
                y_ps[:, 0:512], bufs["ncb"][0:1, 256:384], bufs["mur"][0:1, 0:512],
                start=False, stop=True), s_pe)
            p.wait("scalar", s_pe)
            p.step("scalar", lambda e, hc=hc: e.activation(
                out=bufs["base"][:, hc * 512:(hc + 1) * 512], in_=y_ps[:, 0:512],
                func=AF.Identity, bias=bufs["b1"][:, hc:hc + 1], scale=1.0), s_act)
        for ex in range(8):
            p.wait("tensor", s_act)
            p.wait("tensor", s_dve)
            p.step("tensor", lambda e, ex=ex: e.matmul(
                cb_ps[:, 0:512], bufs["sel8"][:, ex * 128:(ex + 1) * 128],
                bufs["comb8"][:], start=True, stop=True), s_pe)
            p.wait("scalar", s_pe)
            p.step("scalar", lambda e: e.activation(
                out=bufs["cbc"][:], in_=cb_ps[:, 0:512], func=AF.Copy,
                bias=0.0, scale=1.0), s_act)
            p.wait("gpsimd", s_pe)
            p.dma("gpsimd", bufs["a2t"][:],
                  io["a2t_c"][ex * 128:(ex + 1) * 128, :], s_gld, max_dma_last_dim=4096)
            p.wait("tensor", s_gld)
            for hc in range(64):
                if hc % 16 == 0:
                    p.wait("gpsimd", s_pe)
                    p.dma("gpsimd", bufs["b1t"][:],
                          io["b1t"][ex * 16:(ex + 1) * 16,
                                    hc * 128:(hc + 16) * 128], s_gld, max_dma_last_dim=4096)
                    p.wait("tensor", s_gld)
                p.wait("tensor", s_act)
                p.wait("tensor", s_dve)
                p.step("tensor", lambda e, ex=ex, hc=hc: e.matmul(
                    y_ps[:, 0:512], bufs["b1t"][:, (hc % 16) * 128:(hc % 16 + 1) * 128],
                    bufs["t1"][:, ex * 512:(ex + 1) * 512],
                    start=True, stop=True), s_pe)
                p.wait("vector", s_pe)
                p.wait("vector", s_gp)
                p.step("vector", lambda e, hc=hc: e.tensor_tensor(
                    out=bufs["ssb"][:], in0=y_ps[:, 0:512],
                    in1=bufs["base"][:, hc * 512:(hc + 1) * 512], op=ALU.add), s_dve)
                p.wait("scalar", s_dve)
                p.step("scalar", lambda e: e.activation(
                    out=bufs["asb"][:], in_=bufs["ssb"][:], func=AF.Gelu_apprx_tanh), s_act)
                p.wait("vector", s_act)
                p.wait("vector", s_pe)
                p.step("vector", lambda e: e.tensor_tensor(
                    out=bufs["ssb"][:], in0=bufs["asb"][:], in1=bufs["cbc"][:],
                    op=ALU.mult), s_dve)
                p.wait("gpsimd", s_dve)
                p.step("gpsimd", lambda e, hc=hc, ex=ex: e.tensor_tensor(
                    out=bufs["mix"][:, hc * 512:(hc + 1) * 512],
                    in0=bufs["ssb"][:],
                    in1=bufs["mix"][:, hc * 512:(hc + 1) * 512], op=ALU.add)
                    if ex > 0 else e.tensor_copy(
                    out=bufs["mix"][:, hc * 512:(hc + 1) * 512], in_=bufs["ssb"][:]), s_gp)
                p.wait("tensor", s_dve)
                p.step("tensor", lambda e, hc=hc: e.matmul(
                    t2_ps[0:16, 0:512], bufs["a2t"][:, hc * 16:(hc + 1) * 16],
                    bufs["ssb"][:], start=(hc == 0), stop=(hc == 63)),
                    s_pe if hc == 63 else None)
            p.wait("scalar", s_pe)
            p.wait("scalar", s_st)
            p.step("scalar", lambda e, ex=ex: e.activation(
                out=bufs["t2st"][:], in_=t2_ps[0:16, 0:512],
                func=AF.Copy, bias=0.0, scale=2.0), s_act)
            p.wait("sync", s_act)
            p.dma("sync", t2scr[:, ex * 512:(ex + 1) * 512], bufs["t2st"][:], s_st)
        zsc.close()
        wsc = contextlib.ExitStack()
        w2b = wsc.enter_context(nc.sbuf_tensor(f"w2b{sl}", [128, 8192], BF16, side="right"))
        p.wait("sync", s_st)
        p.dma("sync", bufs["t2a"][:],
              t2scr.ap().rearrange("r (e t) -> e r t", t=512), s_ld)
        for dc in range(16):
            p.wait("gpsimd", s_pe)
            if dc == 0:
                p.wait("gpsimd", s_gp)
            p.dma("gpsimd", w2b[:],
                  io["w2_b"][dc * 128:(dc + 1) * 128, :], s_gld, max_dma_last_dim=8192)
            p.wait("tensor", s_gld)
            p.wait("tensor", s_act)
            p.wait("tensor", s_gp)
            p.wait("tensor", s_ld)
            p.wait("tensor", s_dve)
            for hc in range(64):
                p.step("tensor", lambda e, hc=hc, w2b=w2b: e.matmul(
                    y_ps[:, 0:512], w2b[:, hc * 128:(hc + 1) * 128],
                    bufs["mix"][:, hc * 512:(hc + 1) * 512],
                    start=(hc == 0), stop=False))
            p.step("tensor", lambda e, dc=dc: e.matmul(
                y_ps[:, 0:512], bufs["b2t"][:, dc * 128:(dc + 1) * 128],
                bufs["t2a"][:], start=False, stop=False))
            p.step("tensor", lambda e, dc=dc: e.matmul(
                y_ps[:, 0:512], bufs["b2"][0:1, dc * 128:(dc + 1) * 128],
                bufs["csumr"][0:1, 0:512], start=False, stop=True), s_pe)
            p.wait("sync", s_dve)
            p.dma("sync", bufs["xr"][:],
                  io["x3_slots"][dc * 128:(dc + 1) * 128, sl * 512:(sl + 1) * 512], s_ld)
            p.wait("vector", s_pe)
            p.wait("vector", s_ld)
            p.wait("vector", s_st)
            p.step("vector", lambda e: e.tensor_tensor(
                out=bufs["oout"][:], in0=y_ps[:, 0:512], in1=bufs["xr"][:],
                op=ALU.add), s_dve)
            p.wait("sync", s_dve)
            p.dma("sync", io["out_slots"][dc * 128:(dc + 1) * 128, sl * 512:(sl + 1) * 512],
                  bufs["oout"][:], s_st)
        wsc.close()
    p.wait("gpsimd", s_st)
    _pes.close()
    if env is None:
        p.build()
    return p, io


def emit_phase_barrier(p, sems):
    for eng in ENGS:
        for s in sems:
            p.wait(eng, s)


def build_fused():
    p = Prog()
    io = {}
    env = {"p": p, "io": io,
           "s_ld": p.sem("ld", queue="sync"), "s_pe": p.sem("pe"),
           "s_act": p.sem("act"), "s_dve": p.sem("dve"), "s_gp": p.sem("gp"),
           "s_st": p.sem("st", queue="sync"), "s_gld": p.sem("gld", queue="gpsimd")}
    sems = [env[k] for k in ("s_ld", "s_pe", "s_act", "s_dve", "s_gp", "s_st", "s_gld")]
    build_p1a(env)
    emit_phase_barrier(p, sems)
    build_p2(env)
    emit_phase_barrier(p, sems)
    build_p3(env)
    emit_phase_barrier(p, sems)
    build_p4(env)
    p.build()
    return p, io



def _l1_attn_inputs(inputs):
    perm = np.zeros(128, np.int64)
    perm[:64] = np.arange(0, 128, 2)
    perm[64:] = np.arange(1, 128, 2)
    wq1, wk1, wv1, wo1 = (inputs["L1_attn_wq"], inputs["L1_attn_wk"],
                          inputs["L1_attn_wv"], inputs["L1_attn_wo"])
    wqp = wq1.reshape(16, 128, 2048)[:, perm, :].reshape(2048, 2048)
    wkp = wk1.reshape(4, 128, 2048)[:, perm, :].reshape(512, 2048)
    swapm = np.zeros((128, 128), np.float32)
    for m in range(128):
        swapm[(m + 64) % 128, m] = 1.0
    return {
        "wqT1": np.ascontiguousarray(wqp.T),
        "wkvT1": np.ascontiguousarray(np.concatenate([wkp, wv1], axis=0).T),
        "woT1": np.ascontiguousarray(wo1.T),
        "swapm": swapm.astype(BF),
    }


def _l0_ffn_inputs(inputs):
    sel2 = np.zeros((2, 128), np.float32)
    sel2[0, :64] = 1
    sel2[1, 64:] = 1
    return {
        "wup_b": np.ascontiguousarray(
            inputs["L0_ffn_w_up_q"].T.reshape(16, 128, 64, 128).transpose(2, 1, 0, 3)
        ).reshape(64 * 128, 16 * 128).astype(np.uint8),
        "wdn_b": np.ascontiguousarray(
            inputs["L0_ffn_w_down_q"].T.reshape(64, 128, 16, 128).transpose(2, 1, 0, 3)
        ).reshape(16 * 128, 64 * 128).astype(np.uint8),
        "sclT_u": np.ascontiguousarray(inputs["L0_ffn_w_up_scale"].T),
        "sclT_d": np.ascontiguousarray(inputs["L0_ffn_w_down_scale"].T),
        "sel2": sel2,
        "la_up": inputs["L0_ffn_w_up_lora_a"],
        "lb_up": inputs["L0_ffn_w_up_lora_b"],
        "la_dn": inputs["L0_ffn_w_down_lora_a"],
        "lb_dn": inputs["L0_ffn_w_down_lora_b"],
        "bup_r": np.ascontiguousarray(inputs["L0_ffn_b_up"].reshape(64, 128).T),
        "bdn_r": np.ascontiguousarray(inputs["L0_ffn_b_down"].reshape(16, 128).T),
    }


def _l1_ffn_inputs(inputs):
    W1, W2 = inputs["L1_ffn_W1"], inputs["L1_ffn_W2"]
    A1T = inputs["L1_ffn_A1"].transpose(0, 2, 1)
    A2T = inputs["L1_ffn_A2"].transpose(0, 2, 1)
    sel8 = np.zeros((8, 8 * 128), np.float32)
    for e in range(8):
        sel8[e, e * 128:(e + 1) * 128] = 1.0
    return {
        "w1_b": np.ascontiguousarray(
            W1.reshape(16, 128, 64, 128).transpose(2, 1, 0, 3)).reshape(8192, 2048),
        "w2_b": np.ascontiguousarray(
            W2.reshape(64, 128, 16, 128).transpose(2, 1, 0, 3)).reshape(2048, 8192),
        "wr": inputs["L1_ffn_Wr"],
        "br_r": inputs["L1_ffn_br"].reshape(8, 1),
        "b1_r": np.ascontiguousarray(inputs["L1_ffn_b1"].reshape(64, 128).T),
        "b2_r": inputs["L1_ffn_b2"].reshape(1, 2048),
        "a1t_c": np.ascontiguousarray(
            A1T.reshape(8, 16, 128, 16).transpose(0, 2, 1, 3)).reshape(8 * 128, 256),
        "a2t_c": np.ascontiguousarray(
            A2T.reshape(8, 64, 128, 16).transpose(0, 2, 1, 3)).reshape(8 * 128, 1024),
        "b1t": np.ascontiguousarray(inputs["L1_ffn_B1"].transpose(0, 2, 1)).reshape(128, 8192),
        "b2t": np.ascontiguousarray(inputs["L1_ffn_B2"].transpose(0, 2, 1)).reshape(128, 2048),
        "sel8": sel8.astype(BF),
    }


def _rope_inputs(slots):
    posl = np.concatenate([np.arange(s * BLK, (s + 1) * BLK) for s in slots])
    cos2, sin2 = rope_tables(posl)
    return {"rope_cos": cos2, "rope_sin": sin2}


def kernel(**inputs):
    inputs = {k: np.asarray(v) for k, v in inputs.items()}
    shared, _ = host_prepare(inputs)
    x = np.ascontiguousarray(inputs["L0_attn_x"].astype(np.float32, copy=False))
    l0w = {"wq0": inputs["L0_attn_Wq"],
           "wkv0": np.ascontiguousarray(
               np.concatenate([inputs["L0_attn_Wk"], inputs["L0_attn_Wv"]], axis=1)),
           "wo0": inputs["L0_attn_Wo"]}
    ffw = _l0_ffn_inputs(inputs)
    a1w = _l1_attn_inputs(inputs)
    f1w = _l1_ffn_inputs(inputs)

    p, io = build_fused()
    names = {t.tensor.name for t in io.values()}
    maps = []
    for c in range(8):
        b, role = c // 2, c % 2
        slots = SLOTS_A if role == 0 else SLOTS_B
        xb = x[b]
        m = dict(shared); m.update(l0w); m.update(ffw); m.update(a1w); m.update(f1w)
        m.update(_rope_inputs(slots))
        m["x_fm"] = np.ascontiguousarray(xb.T)
        m["x_slots"] = np.ascontiguousarray(
            np.concatenate([xb[s * BLK:(s + 1) * BLK] for s in slots], axis=0).T)
        m["l0bias"] = l0_bias_for(slots)
        m["l0mask"] = l0_masks_for(slots)
        m["l1mask"] = l1_masks_for(slots, role)
        maps.append({k: v for k, v in m.items() if k in names})

    res = run_bass_kernel_spmd(p.nc, maps, core_ids=list(range(8)))

    out = np.empty((4, S, D), np.float32)
    for c in range(8):
        b, role = c // 2, c % 2
        slots = SLOTS_A if role == 0 else SLOTS_B
        o = np.asarray(res.results[c]["out_slots"])
        for qi in range(4):
            blk = slots[qi + 1]
            out[b, blk * BLK:(blk + 1) * BLK, :] = o[:, qi * BLK:(qi + 1) * BLK].T
    return out


# version 03 (final)
# speedup vs baseline: 1.2009x; 1.2009x over previous
import contextlib
import numpy as np
import ml_dtypes
import concourse.bass as bass
import concourse.mybir as mybir
from concourse.bass_utils import run_bass_kernel_spmd

F32 = mybir.dt.float32
BF16 = mybir.dt.bfloat16
U8 = mybir.dt.uint8
AF = mybir.ActivationFunctionType
ALU = mybir.AluOpType
AX = mybir.AxisListType
BF = ml_dtypes.bfloat16

D = 2048
S = 4096
NBLK = 8
BLK = 512
NSLOT = 5
QT = 256
NH, NKV, HD = 16, 4, 128
GRP = NH // NKV

SLOTS_A = [5, 6, 7, 0, 1]
SLOTS_B = [1, 2, 3, 4, 5]
PADK = [24, 28, 32, 20, 24]
NB = [p // 4 for p in PADK]
BOFF = np.cumsum([0] + NB).tolist()
MASKB = []
for _s in range(NSLOT):
    _ca = SLOTS_A[_s] + 1
    _cb = SLOTS_B[_s] + 1
    MASKB.append(sorted({_ca - 1, _cb - 1}))
SHIFT0 = 12.0
CAP1 = 20.0
SCALE0 = 1.0 / np.sqrt(128.0)

ENGS = ("sync", "tensor", "scalar", "vector", "gpsimd")


class Sem:
    __slots__ = ("h", "n", "name", "queue", "stage", "sealed_n")

    def __init__(self, h, name):
        self.h = h
        self.n = 0
        self.name = name
        self.queue = None
        self.stage = None
        self.sealed_n = 0

    def inc(self, k):
        self.n += k
        return self.n


class Prog:
    def __init__(self):
        self.nc = bass.Bass("TRN2", target_bir_lowering=False, debug=False)
        self.es = contextlib.ExitStack()
        self.streams = {k: [] for k in ENGS}
        self._semi = 0

    def sem(self, name=None, queue=None):
        name = name or f"s{self._semi}"
        self._semi += 1
        h = self.es.enter_context(self.nc.semaphore(name))
        s = Sem(h, name)
        if queue is not None:
            s.queue = queue
            s.stage = self.sem(name + "_stg")
        return s

    def sbuf(self, name, shape, dtype):
        return self.es.enter_context(self.nc.sbuf_tensor(name, list(shape), dtype))

    def psum(self, name, shape, dtype=F32):
        return self.es.enter_context(self.nc.psum_tensor(name, list(shape), dtype))

    def on(self, eng, fn):
        self.streams[eng].append(fn)

    def wait(self, eng, sem, val=None):
        if sem.stage is not None and val is None:
            if sem.n > sem.sealed_n:
                self.seal(sem.queue, sem, sem.stage)
                sem.sealed_n = sem.n
            v = sem.stage.n
            if v > 0:
                self.on(eng, lambda e, s=sem.stage.h, v=v: e.wait_ge(s, v))
            return
        v = sem.n if val is None else val
        if v > 0:
            self.on(eng, lambda e, s=sem.h, v=v: e.wait_ge(s, v))

    def dma(self, eng, out, in_, sem, **kw):
        sem.inc(16)
        self.on(eng, lambda e, o=out, i=in_, s=sem.h, kw=kw: e.dma_start(out=o, in_=i, **kw).then_inc(s, 16))

    def seal(self, eng, qsem, stage):
        n = qsem.n
        self.on(eng, lambda e, s=qsem.h, n=n: e.wait_ge(s, n))
        stage.inc(1)
        self.on(eng, lambda e, s=stage.h: e.sem_inc(s, 1))

    def dma_small(self, eng, out, in_, sem, **kw):
        self.dma(eng, out, in_, sem, **kw)

    def step(self, eng, fn, sem=None, k=1):
        if sem is not None:
            sem.inc(k)
            self.on(eng, lambda e, f=fn, s=sem.h, k=k: f(e).then_inc(s, k))
        else:
            self.on(eng, lambda e, f=fn: f(e))

    def build(self):
        with self.nc.Block() as block:
            for name in ENGS:
                fns = self.streams[name]
                if not fns:
                    continue
                getattr(block, name)(lambda e, fns=fns: [f(e) for f in fns])
        return self.nc



def rope_tables(positions):
    inv = 1.0 / 10000.0 ** (np.arange(0, HD, 2, dtype=np.float64) / HD)
    ang = np.asarray(positions, np.float64)[None, :] * inv[:, None]
    c = np.cos(ang).astype(np.float32)
    s = np.sin(ang).astype(np.float32)
    cos2 = np.concatenate([c, c], axis=0)
    sin2 = np.concatenate([-s, s], axis=0)
    return cos2, sin2


def l0_bias_for(slots):
    out = np.full((128, BOFF[-1]), -SHIFT0, np.float32)
    for s in range(NSLOT):
        c = slots[s] + 1
        for b in range(NB[s]):
            if b >= c:
                out[:, BOFF[s] + b] = -1e30
    return out


def l0_masks_for(slots):
    out = np.ones((128, NSLOT * 2 * 2 * 1024), np.float32)
    for s in range(NSLOT):
        blk = slots[s]
        cb = blk + 1
        for mi, mb in enumerate(MASKB[s]):
            for qh in range(2):
                tile = np.ones((128, 1024), np.float32)
                if mb == cb - 1:
                    for j in range(4):
                        kc = 4 * mb + j
                        keypos = 128 * kc + np.arange(128)[:, None]
                        qpos = blk * BLK + qh * QT + np.arange(QT)[None, :]
                        tile[:, j * QT:(j + 1) * QT] = (keypos <= qpos)
                col = ((s * 2 + mi) * 2 + qh) * 1024
                out[:, col:col + 1024] = tile
    return out.astype(BF)


def l1_masks_for(slots, role):
    out = np.zeros((128, 4 * 2 * 2 * 1024), np.float32)
    for qi, s in enumerate(range(1, NSLOT)):
        blk = slots[s]
        for bi in range(2):
            for qh in range(2):
                tile = np.zeros((128, 1024), np.float32)
                for j in range(4):
                    jpos = 128 * (4 * bi + j - 4) + np.arange(128)[:, None]
                    a = qh * QT + np.arange(QT)[None, :]
                    if bi == 0:
                        ok = (jpos + BLK) >= a
                        if blk == 0:
                            ok = np.zeros_like(ok)
                    else:
                        ok = jpos <= a
                    tile[:, j * QT:(j + 1) * QT] = ok
                col = ((qi * 2 + bi) * 2 + qh) * 1024
                out[:, col:col + 1024] = tile
    return out.astype(BF)


def host_prepare(inputs):
    x = np.ascontiguousarray(inputs["L0_attn_x"])
    shared = {
        "wq0": np.ascontiguousarray(inputs["L0_attn_Wq"]),
        "wkv0": np.ascontiguousarray(np.concatenate([inputs["L0_attn_Wk"], inputs["L0_attn_Wv"]], axis=1)),
        "wo0": np.ascontiguousarray(inputs["L0_attn_Wo"]),
        "ident": np.eye(128, dtype=BF),
        "ones_c_bf": np.ones((128, 1), dtype=BF),
        "ones_c_f": np.ones((128, 1), dtype=np.float32),
        "ones_r_f": np.ones((1, 128), dtype=np.float32),
        "ones_r_bf": np.ones((1, 128), dtype=BF),
        "cfix": np.array([[1e-5, 0, 0, 0, 0, 0, 0, 0]], dtype=np.float32),
        "ccol": np.tile(np.array([[-20.0, 0, 0, 0]], dtype=np.float32), (128, 1)),
    }
    cores = []
    for c in range(8):
        b, role = c // 2, c % 2
        slots = SLOTS_A if role == 0 else SLOTS_B
        xb = x[b]
        x_fm = np.ascontiguousarray(xb.T)
        x_slots = np.ascontiguousarray(
            np.concatenate([xb[s * BLK:(s + 1) * BLK] for s in slots], axis=0).T)
        d = {
            "x_fm": x_fm,
            "x_slots": x_slots,
            "l0bias": l0_bias_for(slots),
            "l0mask": l0_masks_for(slots),
        }
        cores.append(d)
    return shared, cores



def emit_load_consts(p, io, bufs):
    ld = bufs["s_ld"]
    p.dma("sync", bufs["ident"][:], io["ident"], ld)
    p.dma("sync", bufs["ones_c_bf"][:], io["ones_c_bf"], ld)
    p.dma("sync", bufs["ones_c_f"][:], io["ones_c_f"], ld)
    p.dma_small("sync", bufs["ones_r_f"][:], io["ones_r_f"], ld)
    p.dma_small("sync", bufs["ones_r_bf"][:], io["ones_r_bf"], ld)
    p.dma_small("sync", bufs["cfix"][:], io["cfix"], ld)


def emit_ln(p, bufs, xt, xst, xoff, zt, zst, zoff, nt, mur_bf, mur_off,
            s_pe, s_act, s_dve, s_gp):
    sq = bufs["sq"]
    RS = bufs.get("rows_stride", 512)
    sum_ps = bufs["stat_ps"]
    rbc_ps = bufs["rbc_ps"]
    rows = bufs["rows"]
    rbc = bufs["rbc"]
    ones_f = bufs["ones_c_f"]
    ones_r = bufs["ones_r_f"]

    p.wait("tensor", s_dve)
    p.wait("tensor", s_gp)
    p.wait("scalar", s_gp)
    for dc in range(16):
        pb = dc % 2
        if dc >= 2:
            p.wait("scalar", s_pe)
        p.step("scalar", lambda e, dc=dc, pb=pb: e.activation(
            out=sq[:, pb * 256:pb * 256 + nt], in_=xt[:, dc * xst + xoff:dc * xst + xoff + nt],
            func=AF.Square), s_act)
        p.wait("tensor", s_act)
        p.step("tensor", lambda e, dc=dc: e.matmul(
            sum_ps[0:1, 0:nt], ones_f[:], xt[:, dc * xst + xoff:dc * xst + xoff + nt],
            start=(dc == 0), stop=False))
        p.step("tensor", lambda e, dc=dc, pb=pb: e.matmul(
            sum_ps[0:1, 256:256 + nt], ones_f[:], sq[:, pb * 256:pb * 256 + nt],
            start=False, stop=(dc == 15)), s_pe)
    p.wait("vector", s_pe)
    p.step("vector", lambda e: e.tensor_scalar(
        out=rows[0:1, 0:nt], in0=sum_ps[0:1, 0:nt], scalar1=1.0 / D, scalar2=None,
        op0=ALU.mult), s_dve)
    p.wait("vector", s_dve)
    p.step("vector", lambda e: e.tensor_tensor(
        out=rows[0:1, 3 * RS:3 * RS + nt], in0=rows[0:1, 0:nt], in1=rows[0:1, 0:nt], op=ALU.mult), s_dve)
    p.wait("vector", s_dve)
    p.step("vector", lambda e: e.scalar_tensor_tensor(
        out=rows[0:1, RS:RS + nt], in0=sum_ps[0:1, 256:256 + nt], scalar=1.0 / D,
        in1=rows[0:1, 3 * RS:3 * RS + nt], op0=ALU.mult, op1=ALU.subtract), s_dve)
    p.wait("scalar", s_dve)
    p.step("scalar", lambda e: e.activation(
        out=rows[0:1, 3 * RS:3 * RS + nt], in_=rows[0:1, RS:RS + nt], func=AF.Ln, bias=bufs["cfix"][0:1, 0:1], scale=1.0), s_act)
    p.wait("scalar", s_act)
    p.step("scalar", lambda e: e.activation(
        out=rows[0:1, 2 * RS:2 * RS + nt], in_=rows[0:1, 3 * RS:3 * RS + nt], func=AF.Exp, scale=-0.5), s_act)
    p.wait("vector", s_act)
    p.step("vector", lambda e: e.tensor_tensor(
        out=mur_bf[0:1, mur_off:mur_off + nt], in0=rows[0:1, 0:nt], in1=rows[0:1, 2 * RS:2 * RS + nt],
        op=ALU.mult), s_dve)
    p.wait("tensor", s_act)
    p.step("tensor", lambda e: e.matmul(
        rbc_ps[:, 0:nt], ones_r[:], rows[0:1, 2 * RS:2 * RS + nt], start=True, stop=True), s_pe)
    p.wait("scalar", s_pe)
    p.step("scalar", lambda e: e.activation(
        out=rbc[:, 0:nt], in_=rbc_ps[:, 0:nt], func=AF.Copy, bias=0.0, scale=1.0), s_act)
    p.wait("gpsimd", s_act)
    for dc in range(16):
        p.step("gpsimd", lambda e, dc=dc: e.tensor_tensor(
            out=zt[:, dc * zst + zoff:dc * zst + zoff + nt],
            in0=xt[:, dc * xst + xoff:dc * xst + xoff + nt],
            in1=rbc[:, 0:nt], op=ALU.mult), s_gp if dc == 15 else None)


def emit_colsum(p, bufs, w_sb, ocols, out_bf, s_pe, s_act, s_dve):
    sp = bufs["stat_ps"]
    p.wait("tensor", s_dve)
    p.wait("tensor", s_act)
    ng = ocols // 256
    for g in range(ng):
        if g > 0:
            p.wait("tensor", s_act)
        for dc in range(16):
            p.step("tensor", lambda e, g=g, dc=dc: e.matmul(
                sp[0:1, 0:256], bufs["ones_c_bf"][:],
                w_sb[:, dc * ocols + g * 256: dc * ocols + g * 256 + 256],
                start=(dc == 0), stop=(dc == 15)), s_pe if dc == 15 else None)
        p.wait("scalar", s_pe)
        p.step("scalar", lambda e, g=g: e.activation(
            out=out_bf[0:1, g * 256:(g + 1) * 256], in_=sp[0:1, 0:256],
            func=AF.Copy, bias=0.0, scale=-1.0), s_act)


def emit_proj(p, w_sb, w_ocols, oc, zt, zst, zoff, nt, out_ps, s_pe,
              ncol_bf=None, mur_bf=None, mur_off=0, extra=None, stop=True):
    n_extra = 0 if extra is None else len(extra)
    last = 15 if (ncol_bf is None and n_extra == 0) else -1
    for dc in range(16):
        p.step("tensor", lambda e, dc=dc: e.matmul(
            out_ps[:, 0:nt], w_sb[:, dc * w_ocols + oc * 128: dc * w_ocols + oc * 128 + 128],
            zt[:, dc * zst + zoff:dc * zst + zoff + nt],
            start=(dc == 0), stop=(stop and dc == last)),
            s_pe if (stop and dc == last) else None)
    if extra:
        for i, fn in enumerate(extra):
            isl = (i == n_extra - 1) and ncol_bf is None
            p.step("tensor", (lambda e, fn=fn, isl=isl: fn(e, isl)),
                   s_pe if (stop and isl) else None)
    if ncol_bf is not None:
        p.step("tensor", lambda e: e.matmul(
            out_ps[:, 0:nt], ncol_bf[0:1, oc * 128:oc * 128 + 128],
            mur_bf[0:1, mur_off:mur_off + nt], start=False, stop=stop),
            s_pe if stop else None)


def build_p1a(env=None):
    if env is None:
        p = Prog()
        io = {}
    else:
        p = env["p"]
        io = env["io"]
    _pes = contextlib.ExitStack()
    def _psb(name, shape, dtype):
        return _pes.enter_context(p.nc.sbuf_tensor("p1" + name, list(shape), dtype, side="right"))
    def _pps(name, shape, dtype=F32):
        return _pes.enter_context(p.nc.psum_tensor("p1" + name, list(shape), dtype))
    def _dram(name, shape, dtype, kind=None):
        if name not in io:
            kw = {} if kind is None else {"kind": kind}
            io[name] = p.nc.dram_tensor(name, list(shape), dtype, **kw).ap()
        return io[name]
    nc = p.nc
    _dram("x_fm", [D, S], F32, "ExternalInput")
    _dram("x_slots", [D, NSLOT * BLK], F32, "ExternalInput")
    _dram("wq0", [D, D], F32, "ExternalInput")
    _dram("wkv0", [D, 1024], F32, "ExternalInput")
    _dram("wo0", [D, D], F32, "ExternalInput")
    _dram("l0bias", [128, 32], F32, "ExternalInput")
    _dram("l0mask", [128, NSLOT * 4 * 1024], BF16, "ExternalInput")
    for k, shp, dt in [("ident", [128, 128], BF16), ("ones_c_bf", [128, 1], BF16),
                       ("ones_c_f", [128, 1], F32), ("ones_r_f", [1, 128], F32),
                       ("ones_r_bf", [1, 128], BF16), ("cfix", [1, 8], F32)]:
        _dram(k, shp, dt, "ExternalInput")
    _dram("av_slots", [D, NSLOT * BLK], BF16, None if env is not None else "ExternalOutput")
    _dram("x1_slots", [D, NSLOT * BLK], F32, None if env is not None else "ExternalOutput")
    v_scr = nc.dram_tensor("v_scr", [4 * 32 * 128, 128], BF16)

    bufs = {}
    bufs["xt"] = _psb("xt", [128, 16 * 256], F32)
    bufs["zt"] = _psb("zt", [128, 16 * 512], BF16)
    bufs["probs"] = _psb("probs", [128, 3 * 1024], BF16)
    bufs["maskslot"] = _psb("maskslot", [128, 4 * 1024], BF16)
    bufs["bias"] = _psb("bias", [128, 32], F32)
    bufs["sq"] = _psb("sq", [128, 512], F32)
    bufs["rows"] = _psb("rows", [1, 4 * 512], F32)
    bufs["rbc"] = _psb("rbc", [128, 256], F32)
    bufs["rec"] = _psb("rec", [1, 256], F32)
    bufs["recbc"] = _psb("recbc", [128, 256], F32)
    bufs["mur"] = _psb("mur", [1, 512], BF16)
    bufs["ncol_q"] = _psb("ncol_q", [1, 2048], BF16)
    bufs["ncol_kv"] = _psb("ncol_kv", [1, 1024], BF16)
    bufs["vt_s"] = _psb("vt_s", [128, 256], BF16)
    for k in ["ident", "ones_c_bf", "ones_c_f", "ones_r_f", "ones_r_bf", "cfix"]:
        bufs[k] = _psb(k + "_sb", io[k].tensor.shape, io[k].dtype)
    bufs["stat_ps"] = _pps("stat_ps", [128, 512])
    bufs["rbc_ps"] = _pps("rbc_ps", [128, 512])
    l0_es = contextlib.ExitStack()
    bufs["wq"] = l0_es.enter_context(nc.sbuf_tensor("wq", [128, 16 * 2048], BF16, side="right"))
    bufs["K"] = l0_es.enter_context(nc.sbuf_tensor("K", [128, 4 * 4096], BF16, side="right"))

    if env is None:
        s_ld = p.sem("ld", queue="sync"); s_pe = p.sem("pe"); s_act = p.sem("act")
        s_dve = p.sem("dve"); s_gp = p.sem("gp"); s_st = p.sem("st", queue="sync")
        s_gld = p.sem("gld", queue="gpsimd")
    else:
        s_ld = env["s_ld"]; s_pe = env["s_pe"]; s_act = env["s_act"]
        s_dve = env["s_dve"]; s_gp = env["s_gp"]; s_st = env["s_st"]; s_gld = env["s_gld"]
    bufs["s_ld"] = s_ld

    emit_load_consts(p, io, bufs)
    p.dma("sync", bufs["bias"][:], io["l0bias"], s_ld)
    for dc in range(16):
        p.dma("gpsimd", bufs["wq"][:, dc * 2048:(dc + 1) * 2048],
              io["wq0"][dc * 128:(dc + 1) * 128, :], s_gld, max_dma_last_dim=8192)
    p.wait("tensor", s_gld)
    p.wait("tensor", s_ld)
    p.wait("scalar", s_ld)
    p.wait("vector", s_ld)
    p.wait("gpsimd", s_ld)
    emit_colsum(p, bufs, bufs["wq"], 2048, bufs["ncol_q"], s_pe, s_act, s_dve)

    with contextlib.ExitStack() as kv_es:
        wkv = kv_es.enter_context(nc.sbuf_tensor("wkv", [128, 16 * 1024], BF16, side="right"))
        kv_ps = kv_es.enter_context(nc.psum_tensor("kv_ps", [128, 512], F32))
        vt_ps = kv_es.enter_context(nc.psum_tensor("vt_ps", [128, 1024], BF16))
        for dc in range(16):
            p.dma("gpsimd", wkv[:, dc * 1024:(dc + 1) * 1024],
                  io["wkv0"][dc * 128:(dc + 1) * 128, :], s_gld, max_dma_last_dim=4096)
        p.wait("tensor", s_gld)
        emit_colsum(p, bufs, wkv, 1024, bufs["ncol_kv"], s_pe, s_act, s_dve)
        for st in range(16):
            t0 = st * 256
            p.wait("sync", s_gp)
            p.wait("sync", s_pe)
            for dc in range(16):
                p.dma("sync", bufs["xt"][:, dc * 256:(dc + 1) * 256],
                      io["x_fm"][dc * 128:(dc + 1) * 128, t0:t0 + 256], s_ld)
            p.wait("scalar", s_ld)
            p.wait("tensor", s_ld)
            p.wait("gpsimd", s_ld)
            emit_ln(p, bufs, bufs["xt"], 256, 0, bufs["zt"], 512, 0, 256,
                    bufs["mur"], 0, s_pe, s_act, s_dve, s_gp)
            p.wait("tensor", s_gp)
            for oc in range(8):
                p.wait("tensor", s_act)
                p.wait("tensor", s_st)
                emit_proj(p, wkv, 1024, oc, bufs["zt"], 512, 0, 256, kv_ps,
                          s_pe, ncol_bf=bufs["ncol_kv"], mur_bf=bufs["mur"], mur_off=0)
                g = oc % 4
                if oc < 4:
                    p.wait("scalar", s_pe)
                    p.step("scalar", lambda e, g=g, t0=t0: e.activation(
                        out=bufs["K"][:, g * 4096 + t0: g * 4096 + t0 + 256],
                        in_=kv_ps[:, 0:256], func=AF.Copy, bias=0.0, scale=1.0), s_act)
                else:
                    p.wait("scalar", s_pe)
                    p.step("scalar", lambda e: e.activation(
                        out=bufs["vt_s"][:, 0:256], in_=kv_ps[:, 0:256],
                        func=AF.Copy, bias=0.0, scale=1.0), s_act)
                    p.wait("tensor", s_act)
                    for half in range(2):
                        p.step("tensor", lambda e, half=half: e.transpose(
                            vt_ps[:, half * 128:(half + 1) * 128],
                            bufs["vt_s"][:, half * 128:(half + 1) * 128],
                            bufs["ident"][:]), s_pe)
                    p.wait("vector", s_pe)
                    p.step("vector", lambda e: e.tensor_copy(
                        out=bufs["vt_s"][:, 0:256], in_=vt_ps[:, 0:256]), s_dve)
                    base = (g * 32 + t0 // 128) * 128
                    p.wait("sync", s_dve)
                    p.dma("sync",
                          v_scr[base:base + 256, :].rearrange("(h t) j -> t h j", h=2),
                          bufs["vt_s"][:].rearrange("t (h j) -> t h j", h=2), s_st)
                    p.wait("tensor", s_dve)

    with contextlib.ExitStack() as at_es:
        av_ps = [at_es.enter_context(nc.psum_tensor(f"av_ps{h}", [128, 512], F32))
                 for h in range(4)]
        sc_ps = at_es.enter_context(nc.psum_tensor("sc_ps", [128, 1024], F32))
        q_sb = at_es.enter_context(nc.sbuf_tensor("q_sb", [128, 16 * 512], BF16, side="right"))
        av_sb = at_es.enter_context(nc.sbuf_tensor("av_sb", [128, 16 * 512], BF16, side="right"))
        vbuf = at_es.enter_context(nc.sbuf_tensor("vbuf", [128, 32 * 128], BF16, side="right"))
        for sl in range(NSLOT):
            p.wait("sync", s_dve)
            p.wait("sync", s_gp)
            p.dma("sync", bufs["maskslot"][:], io["l0mask"][:, sl * 4096:(sl + 1) * 4096], s_ld)
            p.wait("sync", s_pe)
            for hh in range(2):
                for dc in range(16):
                    p.dma("sync", bufs["xt"][:, dc * 256:(dc + 1) * 256],
                          io["x_slots"][dc * 128:(dc + 1) * 128,
                                        sl * 512 + hh * 256:sl * 512 + hh * 256 + 256], s_ld)
                p.wait("scalar", s_ld)
                p.wait("tensor", s_ld)
                p.wait("gpsimd", s_ld)
                emit_ln(p, bufs, bufs["xt"], 256, 0, bufs["zt"], 512, hh * 256, 256,
                        bufs["mur"], hh * 256, s_pe, s_act, s_dve, s_gp)
                if hh == 0:
                    p.wait("sync", s_gp)
                    p.wait("sync", s_pe)
            p.wait("tensor", s_gp)
            for oc in range(16):
                p.wait("tensor", s_act)
                emit_proj(p, bufs["wq"], 2048, oc, bufs["zt"], 512, 0, 512, sc_ps,
                          s_pe, ncol_bf=bufs["ncol_q"], mur_bf=bufs["mur"], mur_off=0)
                p.wait("scalar", s_pe)
                p.step("scalar", lambda e, oc=oc: e.activation(
                    out=q_sb[:, oc * 512:(oc + 1) * 512], in_=sc_ps[:, 0:512],
                    func=AF.Copy, bias=0.0, scale=1.0), s_act)
            for g in range(4):
                p.wait("sync", s_pe)
                nk = PADK[sl]
                for part in range(0, nk, 8):
                    pk = min(8, nk - part)
                    base = g * 4096 + part * 128
                    p.dma("sync",
                          vbuf[:, part * 128:(part + pk) * 128].rearrange("t (kc j) -> t kc j", j=128),
                          v_scr[base:base + pk * 128, :].rearrange("(kc t) j -> t kc j", t=128),
                          s_ld)
                p.wait("tensor", s_ld)
                for qh in range(2):
                    for h4 in range(4):
                        h = g * 4 + h4
                        for b in range(NB[sl]):
                            pb = b % 2
                            gb = BOFF[sl] + b
                            p.wait("tensor", s_act)
                            p.wait("tensor", s_dve)
                            for j in range(4):
                                kc = 4 * b + j
                                p.step("tensor", lambda e, g=g, kc=kc, h=h, qh=qh, j=j: e.matmul(
                                    sc_ps[:, j * 256:(j + 1) * 256],
                                    bufs["K"][:, g * 4096 + kc * 128: g * 4096 + (kc + 1) * 128],
                                    q_sb[:, h * 512 + qh * 256: h * 512 + qh * 256 + 256],
                                    start=(j % 2 == 0), stop=(j % 2 == 1)), s_pe if j == 3 else None)
                            p.wait("scalar", s_pe)
                            p.wait("scalar", s_dve)
                            p.wait("scalar", s_gp)
                            mi = MASKB[sl].index(b) if b in MASKB[sl] else None
                            p.step("scalar", lambda e, pb=pb, gb=gb: e.activation(
                                out=bufs["probs"][:, pb * 1024:(pb + 1) * 1024],
                                in_=sc_ps[:, 0:1024], func=AF.Exp,
                                bias=bufs["bias"][:, gb:gb + 1], scale=SCALE0), s_act)
                            use = pb
                            if mi is not None:
                                eng = "vector" if (b % 2 == 0) else "gpsimd"
                                sem = s_dve if eng == "vector" else s_gp
                                p.wait(eng, s_act)
                                p.wait(eng, s_pe)
                                p.step(eng, lambda e, pb=pb, mi=mi, qh=qh: e.tensor_tensor(
                                    out=bufs["probs"][:, 2048:3072],
                                    in0=bufs["probs"][:, pb * 1024:(pb + 1) * 1024],
                                    in1=bufs["maskslot"][:, (mi * 2 + qh) * 1024:(mi * 2 + qh + 1) * 1024],
                                    op=ALU.mult), sem)
                                p.wait("tensor", sem)
                                use = 2
                            else:
                                p.wait("tensor", s_act)
                            first = (b == 0)
                            lastb = (b == NB[sl] - 1)
                            def em_av(stop_on_j3, b=b, h4=h4, use=use, first=first):
                                for j in range(4):
                                    kc = 4 * b + j
                                    stp = stop_on_j3 and (j == 3)
                                    p.step("tensor", lambda e, kc=kc, h4=h4, use=use, j=j, first=first, stp=stp: e.matmul(
                                        av_ps[h4][:, 0:256],
                                        vbuf[:, kc * 128:(kc + 1) * 128],
                                        bufs["probs"][:, use * 1024 + j * 256: use * 1024 + (j + 1) * 256],
                                        start=(first and j == 0), stop=stp),
                                        s_pe if stp else None)
                            def em_sums(h4=h4, use=use):
                                for j in range(4):
                                    p.step("tensor", lambda e, h4=h4, use=use, j=j: e.matmul(
                                        av_ps[h4][0:1, 256:512],
                                        bufs["ones_c_bf"][:],
                                        bufs["probs"][:, use * 1024 + j * 256: use * 1024 + (j + 1) * 256],
                                        start=False, stop=False))
                            if lastb:
                                em_sums()
                                em_av(True)
                            else:
                                em_av(False)
                                em_sums()
                    for h4 in range(4):
                        h = g * 4 + h4
                        p.wait("vector", s_pe)
                        p.wait("vector", s_act)
                        p.step("vector", lambda e, h4=h4: e.reciprocal(
                            out=bufs["rec"][0:1, 0:256], in_=av_ps[h4][0:1, 256:512]), s_dve)
                        p.wait("tensor", s_dve)
                        p.wait("tensor", s_act)
                        p.step("tensor", lambda e: e.matmul(
                            bufs["rbc_ps"][:, 0:256], bufs["ones_r_f"][:],
                            bufs["rec"][0:1, 0:256], start=True, stop=True), s_pe)
                        p.wait("scalar", s_pe)
                        p.step("scalar", lambda e: e.activation(
                            out=bufs["recbc"][:, 0:256], in_=bufs["rbc_ps"][:, 0:256],
                            func=AF.Copy, bias=0.0, scale=1.0), s_act)
                        p.wait("vector", s_act)
                        p.step("vector", lambda e, h=h, h4=h4, qh=qh: e.tensor_tensor(
                            out=av_sb[:, h * 512 + qh * 256: h * 512 + qh * 256 + 256],
                            in0=av_ps[h4][:, 0:256], in1=bufs["recbc"][:, 0:256],
                            op=ALU.mult), s_dve)
            p.wait("sync", s_dve)
            for oc in range(16):
                p.dma("sync", io["av_slots"][oc * 128:(oc + 1) * 128, sl * 512:(sl + 1) * 512],
                      av_sb[:, oc * 512:(oc + 1) * 512], s_st)
    l0_es.close()
    with contextlib.ExitStack() as wo_es:
        wo = wo_es.enter_context(nc.sbuf_tensor("wo_sb", [128, 16 * 2048], BF16, side="right"))
        avs = wo_es.enter_context(nc.sbuf_tensor("avs", [128, 16 * 512], BF16, side="right"))
        xr = wo_es.enter_context(nc.sbuf_tensor("xr", [128, 2 * 512], F32, side="right"))
        x1o = wo_es.enter_context(nc.sbuf_tensor("x1o", [128, 2 * 512], F32, side="right"))
        wo_ps = wo_es.enter_context(nc.psum_tensor("wo_ps", [128, 512], F32))
        p.wait("gpsimd", s_st)
        for dc in range(16):
            p.dma("gpsimd", wo[:, dc * 2048:(dc + 1) * 2048],
                  io["wo0"][dc * 128:(dc + 1) * 128, :], s_gld, max_dma_last_dim=8192)
        p.wait("tensor", s_gld)
        for sl in range(NSLOT):
            p.wait("sync", s_pe)
            for oc in range(16):
                p.dma("sync", avs[:, oc * 512:(oc + 1) * 512],
                      io["av_slots"][oc * 128:(oc + 1) * 128, sl * 512:(sl + 1) * 512], s_ld)
            p.wait("tensor", s_ld)
            for oc in range(16):
                pb = oc % 2
                p.wait("sync", s_dve)
                p.dma("sync", xr[:, pb * 512:(pb + 1) * 512],
                      io["x_slots"][oc * 128:(oc + 1) * 128, sl * 512:(sl + 1) * 512], s_ld)
                p.wait("tensor", s_dve)
                emit_proj(p, wo, 2048, oc, avs, 512, 0, 512, wo_ps, s_pe)
                p.wait("vector", s_pe)
                p.wait("vector", s_ld)
                p.wait("vector", s_st)
                p.step("vector", lambda e, pb=pb: e.tensor_tensor(
                    out=x1o[:, pb * 512:(pb + 1) * 512], in0=wo_ps[:, 0:512],
                    in1=xr[:, pb * 512:(pb + 1) * 512], op=ALU.add), s_dve)
                p.wait("sync", s_dve)
                p.dma("sync", io["x1_slots"][oc * 128:(oc + 1) * 128, sl * 512:(sl + 1) * 512],
                      x1o[:, pb * 512:(pb + 1) * 512], s_st)
    p.wait("gpsimd", s_st)
    _pes.close()
    if env is None:
        p.build()
    return p, io


def build_p2(env=None, debug=False, up_only=False, nslots=NSLOT):
    if env is None:
        p = Prog()
        io = {}
    else:
        p = env["p"]
        io = env["io"]
    _pes = contextlib.ExitStack()
    def _psb(name, shape, dtype):
        return _pes.enter_context(p.nc.sbuf_tensor("p2" + name, list(shape), dtype, side="right"))
    def _pps(name, shape, dtype=F32):
        return _pes.enter_context(p.nc.psum_tensor("p2" + name, list(shape), dtype))
    def _dram(name, shape, dtype, kind=None):
        if name not in io:
            kw = {} if kind is None else {"kind": kind}
            io[name] = p.nc.dram_tensor(name, list(shape), dtype, **kw).ap()
        return io[name]
    nc = p.nc
    _dram("x1_slots", [D, NSLOT * BLK], F32, "ExternalInput")
    _dram("wup_b", [64 * 128, 16 * 128], U8, "ExternalInput")
    _dram("wdn_b", [16 * 128, 64 * 128], U8, "ExternalInput")
    _dram("slu", [64, 4096], F32, "ExternalInput")
    _dram("scld", [64, 4096], F32, "ExternalInput")
    _dram("selm", [64, 2048], BF16, "ExternalInput")
    _dram("la_up", [D, 16], F32, "ExternalInput")
    _dram("lb_up", [16, 8192], F32, "ExternalInput")
    _dram("la_dn", [8192, 16], F32, "ExternalInput")
    _dram("lb_dn", [16, D], F32, "ExternalInput")
    _dram("bup_r", [128, 64], F32, "ExternalInput")
    _dram("bdn_r", [128, 16], F32, "ExternalInput")
    for k, shp, dt in [("ident", [128, 128], BF16), ("ones_c_bf", [128, 1], BF16),
                       ("ones_c_f", [128, 1], F32), ("ones_r_f", [1, 128], F32),
                       ("ones_r_bf", [1, 128], BF16), ("cfix", [1, 8], F32)]:
        _dram(k, shp, dt, "ExternalInput")
    _dram("x2_slots", [D, NSLOT * BLK], F32, None if env is not None else "ExternalOutput")
    if debug:
        io["dbg_z"] = nc.dram_tensor("dbg_z", [128, 16 * 512], BF16, kind="ExternalOutput").ap()
        io["dbg_x2"] = nc.dram_tensor("dbg_x2", [128, 64 * 512], BF16, kind="ExternalOutput").ap()
        io["dbg_tlo"] = nc.dram_tensor("dbg_tlo", [16, 512], BF16, kind="ExternalOutput").ap()
        io["dbg_wdq"] = nc.dram_tensor("dbg_wdq", [128, 2048], BF16, kind="ExternalOutput").ap()
        io["dbg_mur"] = nc.dram_tensor("dbg_mur", [1, 512], BF16, kind="ExternalOutput").ap()
        io["dbg_ncb"] = nc.dram_tensor("dbg_ncb", [1, 144], BF16, kind="ExternalOutput").ap()
        io["dbg_la"] = nc.dram_tensor("dbg_la", [128, 256], BF16, kind="ExternalOutput").ap()
        io["dbg_x2b"] = nc.dram_tensor("dbg_x2b", [128, 64 * 512], BF16, kind="ExternalOutput").ap()
        io["dbg_x2c"] = nc.dram_tensor("dbg_x2c", [128, 512], BF16, kind="ExternalOutput").ap()
        io["dbg_w63"] = nc.dram_tensor("dbg_w63", [128, 2048], BF16, kind="ExternalOutput").ap()
        io["dbg_y63"] = nc.dram_tensor("dbg_y63", [128, 512], F32, kind="ExternalOutput").ap()
        io["dbg_z63"] = nc.dram_tensor("dbg_z63", [128, 8192], BF16, kind="ExternalOutput").ap()
        io["dbg_ddq"] = nc.dram_tensor("dbg_ddq", [128, 8192], BF16, kind="ExternalOutput").ap()

    bufs = {}
    bufs["xt"] = _psb("xt", [128, 16 * 256], F32)
    bufs["zt"] = _psb("zt", [128, 16 * 512], BF16)
    bufs["x2"] = _psb("x2", [128, 64 * 512], BF16)
    bufs["sq"] = _psb("sq", [128, 512], F32)
    bufs["rows"] = _psb("rows", [1, 4 * 512], F32)
    bufs["rbc"] = _psb("rbc", [128, 256], F32)
    bufs["mur"] = _psb("mur", [1, 512], BF16)
    bufs["la_u"] = _psb("la_u", [128, 16 * 16], BF16)
    bufs["lb_u"] = _psb("lb_u", [16, 8192], BF16)
    bufs["la_d"] = _psb("la_d", [128, 64 * 16], BF16)
    bufs["lb_d"] = _psb("lb_d", [16, 2048], BF16)
    bufs["bup"] = _psb("bup", [128, 64], F32)
    bufs["bdn"] = _psb("bdn", [128, 16], F32)
    bufs["selm"] = _psb("selm_sb", [64, 2048], BF16)
    bufs["sluh"] = _psb("sluh", [64, 4096], BF16)
    bufs["slul"] = _psb("slul", [64, 4096], BF16)
    bufs["scdh"] = _psb("scdh", [64, 4096], BF16)
    bufs["scdl"] = _psb("scdl", [64, 4096], BF16)
    bufs["tlo"] = _psb("tlo", [16, 512], BF16)
    bufs["ncb"] = _psb("ncb", [1, 16 + 128], BF16)
    bufs["xr"] = _psb("xr", [128, 512], F32)
    bufs["x2o"] = _psb("x2o", [128, 512], F32)
    for k in ["ident", "ones_c_bf", "ones_c_f", "ones_r_f", "ones_r_bf", "cfix"]:
        bufs[k] = _psb(k + "_sb", io[k].tensor.shape, io[k].dtype)
    bufs["stat_ps"] = _pps("stat_ps", [128, 512])
    bufs["rbc_ps"] = _pps("rbc_ps", [128, 512])
    y_ps = _pps("y_ps", [128, 512])
    scps = [_pps("sc_ps0", [128, 512]), _pps("sc_ps1", [128, 512])]
    tlo_ps = _pps("tlo_ps", [128, 512])

    if env is None:
        s_ld = p.sem("ld", queue="sync"); s_pe = p.sem("pe"); s_act = p.sem("act")
        s_dve = p.sem("dve"); s_gp = p.sem("gp"); s_st = p.sem("st", queue="sync")
        s_gld = p.sem("gld", queue="gpsimd")
    else:
        s_ld = env["s_ld"]; s_pe = env["s_pe"]; s_act = env["s_act"]
        s_dve = env["s_dve"]; s_gp = env["s_gp"]; s_st = env["s_st"]; s_gld = env["s_gld"]
    bufs["s_ld"] = s_ld

    s_bA = p.sem(); s_bB = p.sem()
    sbp = [s_bA, s_bB]

    emit_load_consts(p, io, bufs)
    p.dma("sync", bufs["bup"][:], io["bup_r"], s_ld)
    p.dma("sync", bufs["bdn"][:], io["bdn_r"], s_ld)
    p.dma("sync", bufs["selm"][:], io["selm"], s_ld)
    for dc in range(16):
        p.dma("gpsimd", bufs["la_u"][:, dc * 16:(dc + 1) * 16],
              io["la_up"][dc * 128:(dc + 1) * 128, :], s_gld)
    p.dma("gpsimd", bufs["lb_u"][:], io["lb_up"], s_gld, max_dma_last_dim=8192)
    for hc in range(64):
        p.dma("gpsimd", bufs["la_d"][:, hc * 16:(hc + 1) * 16],
              io["la_dn"][hc * 128:(hc + 1) * 128, :], s_gld)
    p.dma("gpsimd", bufs["lb_d"][:], io["lb_dn"], s_gld, max_dma_last_dim=8192)
    p.wait("tensor", s_ld)
    p.wait("tensor", s_gld)
    p.wait("scalar", s_ld)
    p.wait("vector", s_ld)
    p.wait("gpsimd", s_ld)

    with contextlib.ExitStack() as tb_es:
        scf = tb_es.enter_context(nc.sbuf_tensor("p2scf", [64, 4096], F32, side="right"))
        for src_io, h, l in (("slu", "sluh", "slul"), ("scld", "scdh", "scdl")):
            p.wait("sync", s_dve)
            p.dma("sync", scf[:], io[src_io], s_ld, max_dma_last_dim=8192)
            p.wait("scalar", s_ld)
            p.step("scalar", lambda e, h=h: e.activation(
                out=bufs[h][:], in_=scf[:], func=AF.Copy, bias=0.0, scale=1.0), s_act)
            p.wait("vector", s_act)
            p.step("vector", lambda e, h=h, l=l: e.tensor_tensor(
                out=bufs[l][:], in0=scf[:], in1=bufs[h][:], op=ALU.subtract), s_dve)
        p.wait("sync", s_dve)

    for dc in range(16):
        p.step("tensor", lambda e, dc=dc: e.matmul(
            bufs["stat_ps"][0:1, 0:16], bufs["ones_c_bf"][:],
            bufs["la_u"][:, dc * 16:(dc + 1) * 16],
            start=(dc == 0), stop=(dc == 15)), s_pe if dc == 15 else None)
    p.wait("scalar", s_pe)
    p.step("scalar", lambda e: e.activation(
        out=bufs["ncb"][0:1, 0:16], in_=bufs["stat_ps"][0:1, 0:16],
        func=AF.Copy, bias=0.0, scale=-1.0), s_act)

    for sl in range(nslots):
        p.wait("sync", s_gp)
        p.wait("sync", s_pe)
        for hh in range(2):
            for dc in range(16):
                p.dma("sync", bufs["xt"][:, dc * 256:(dc + 1) * 256],
                      io["x1_slots"][dc * 128:(dc + 1) * 128,
                                     sl * 512 + hh * 256:sl * 512 + hh * 256 + 256], s_ld)
            p.wait("scalar", s_ld)
            p.wait("tensor", s_ld)
            p.wait("gpsimd", s_ld)
            emit_ln(p, bufs, bufs["xt"], 256, 0, bufs["zt"], 512, hh * 256, 256,
                    bufs["mur"], hh * 256, s_pe, s_act, s_dve, s_gp)
            if hh == 0:
                p.wait("sync", s_gp)
                p.wait("sync", s_pe)
        p.wait("tensor", s_gp)
        p.wait("tensor", s_act)
        p.wait("tensor", s_dve)
        for dc in range(16):
            p.step("tensor", lambda e, dc=dc: e.matmul(
                tlo_ps[0:16, 0:512], bufs["la_u"][:, dc * 16:(dc + 1) * 16],
                bufs["zt"][:, dc * 512:(dc + 1) * 512],
                start=(dc == 0), stop=False))
        p.step("tensor", lambda e: e.matmul(
            tlo_ps[0:16, 0:512], bufs["ncb"][0:1, 0:16], bufs["mur"][0:1, 0:512],
            start=False, stop=True), s_pe)
        p.wait("scalar", s_pe)
        p.step("scalar", lambda e: e.activation(
            out=bufs["tlo"][:], in_=tlo_ps[0:16, 0:512], func=AF.Copy,
            bias=0.0, scale=1.0), s_act)
        if debug and sl == 0:
            p.wait("sync", s_act)
            p.dma("sync", io["dbg_tlo"], bufs["tlo"][:], s_st)
        with contextlib.ExitStack() as up_es:
            ub8 = [up_es.enter_context(nc.sbuf_tensor(f"ub8_{sl}{i}", [128, 2048], U8, side="right"))
                   for i in range(2)]
            wdqb = [up_es.enter_context(nc.sbuf_tensor(f"wdq{sl}{i}", [128, 2048], BF16, side="right"))
                    for i in range(2)]
            dq_mark = {}

            def em_load_u(hc):
                p.wait("gpsimd", s_dve)
                if hc <= 1:
                    p.wait("gpsimd", s_pe)
                p.dma("gpsimd", ub8[hc % 2][:], io["wup_b"][hc * 128:(hc + 1) * 128, :],
                      sbp[hc % 2], max_dma_last_dim=8192)

            em_load_u(0)
            for hc in range(64):
                if hc + 1 < 64:
                    em_load_u(hc + 1)
                wdq = wdqb[hc % 2]
                for ss in range(4):
                    scp = scps[ss % 2]
                    prev = ("u", hc, ss - 2) if ss >= 2 else ("u", hc - 1, ss + 2)
                    if prev in dq_mark:
                        p.wait("tensor", s_dve, val=dq_mark[prev])
                    elif hc == 0:
                        p.wait("tensor", s_dve)
                    for d4 in range(4):
                        dc = ss * 4 + d4
                        g, cc = hc % 2, hc // 2
                        p.step("tensor", lambda e, dc=dc, d4=d4, g=g, cc=cc, scp=scp: e.matmul(
                            scp[:, d4 * 128:(d4 + 1) * 128],
                            bufs["selm"][g * 32:g * 32 + 32, dc * 128:(dc + 1) * 128],
                            bufs["sluh"][g * 32:g * 32 + 32, cc * 128:(cc + 1) * 128],
                            start=(d4 == 0), stop=False))
                        p.step("tensor", lambda e, dc=dc, d4=d4, g=g, cc=cc, scp=scp: e.matmul(
                            scp[:, d4 * 128:(d4 + 1) * 128],
                            bufs["selm"][g * 32:g * 32 + 32, dc * 128:(dc + 1) * 128],
                            bufs["slul"][g * 32:g * 32 + 32, cc * 128:(cc + 1) * 128],
                            start=False, stop=(d4 == 3)), s_pe if d4 == 3 else None)
                    p.wait("vector", s_pe)
                    p.wait("vector", sbp[hc % 2])
                    p.step("vector", lambda e, ss=ss, scp=scp, hc=hc, wdq=wdq: e.scalar_tensor_tensor(
                        out=wdq[:, ss * 512:(ss + 1) * 512],
                        in0=ub8[hc % 2][:, ss * 512:(ss + 1) * 512],
                        scalar=7.5, in1=scp[:, 0:512], op0=ALU.subtract, op1=ALU.mult), s_dve)
                    dq_mark[("u", hc, ss)] = s_dve.n
                p.wait("tensor", s_dve)
                p.wait("tensor", s_act)
                for dc in range(16):
                    p.step("tensor", lambda e, dc=dc, wdq=wdq: e.matmul(
                        bufs["stat_ps"][0:1, 0:128], bufs["ones_c_bf"][:],
                        wdq[:, dc * 128:(dc + 1) * 128],
                        start=(dc == 0), stop=(dc == 15)), s_pe if dc == 15 else None)
                p.wait("scalar", s_pe)
                p.step("scalar", lambda e: e.activation(
                    out=bufs["ncb"][0:1, 16:144], in_=bufs["stat_ps"][0:1, 0:128],
                    func=AF.Copy, bias=0.0, scale=-1.0), s_act)
                for dc in range(16):
                    p.step("tensor", lambda e, dc=dc, wdq=wdq: e.matmul(
                        y_ps[:, 0:512], wdq[:, dc * 128:(dc + 1) * 128],
                        bufs["zt"][:, dc * 512:(dc + 1) * 512],
                        start=(dc == 0), stop=False))
                p.step("tensor", lambda e, hc=hc: e.matmul(
                    y_ps[:, 0:512], bufs["lb_u"][:, hc * 128:(hc + 1) * 128],
                    bufs["tlo"][:], start=False, stop=False))
                p.wait("tensor", s_act)
                p.step("tensor", lambda e: e.matmul(
                    y_ps[:, 0:512], bufs["ncb"][0:1, 16:144], bufs["mur"][0:1, 0:512],
                    start=False, stop=True), s_pe)
                p.wait("scalar", s_pe)
                p.step("scalar", lambda e, hc=hc: e.activation(
                    out=bufs["x2"][:, hc * 512:(hc + 1) * 512], in_=y_ps[:, 0:512],
                    func=AF.Relu, bias=bufs["bup"][:, hc:hc + 1], scale=1.0), s_act)
        if up_only:
            continue
        p.wait("tensor", s_act)
        p.wait("tensor", s_dve)
        for hc in range(64):
            p.step("tensor", lambda e, hc=hc: e.matmul(
                tlo_ps[0:16, 0:512], bufs["la_d"][:, hc * 16:(hc + 1) * 16],
                bufs["x2"][:, hc * 512:(hc + 1) * 512],
                start=(hc == 0), stop=(hc == 63)), s_pe if hc == 63 else None)
        p.wait("scalar", s_pe)
        p.step("scalar", lambda e: e.activation(
            out=bufs["tlo"][:], in_=tlo_ps[0:16, 0:512], func=AF.Copy,
            bias=0.0, scale=1.0), s_act)
        with contextlib.ExitStack() as dn_es:
            db8 = [dn_es.enter_context(nc.sbuf_tensor(f"db8_{sl}{i}", [128, 8192], U8, side="right"))
                   for i in range(2)]
            ddq = dn_es.enter_context(nc.sbuf_tensor(f"ddq{sl}", [128, 8192], BF16, side="right"))
            dq_mark = {}
            smm_mark = {}

            def em_load_d(dc):
                p.wait("gpsimd", s_dve)
                if dc <= 1:
                    p.wait("gpsimd", s_pe)
                p.dma("gpsimd", db8[dc % 2][:], io["wdn_b"][dc * 128:(dc + 1) * 128, :],
                      sbp[dc % 2], max_dma_last_dim=8192)

            def em_smm_d(dc, ss):
                scp = scps[ss % 2]
                for h4 in range(4):
                    hc = ss * 4 + h4
                    g, cc = hc // 32, hc % 32
                    p.step("tensor", lambda e, dc=dc, h4=h4, g=g, cc=cc, scp=scp: e.matmul(
                        scp[:, h4 * 128:(h4 + 1) * 128],
                        bufs["selm"][g * 32:g * 32 + 32, dc * 128:(dc + 1) * 128],
                        bufs["scdh"][g * 32:g * 32 + 32, cc * 128:(cc + 1) * 128],
                        start=(h4 == 0), stop=False))
                    p.step("tensor", lambda e, dc=dc, h4=h4, g=g, cc=cc, scp=scp: e.matmul(
                        scp[:, h4 * 128:(h4 + 1) * 128],
                        bufs["selm"][g * 32:g * 32 + 32, dc * 128:(dc + 1) * 128],
                        bufs["scdl"][g * 32:g * 32 + 32, cc * 128:(cc + 1) * 128],
                        start=False, stop=(h4 == 3)), s_pe if h4 == 3 else None)
                smm_mark[(dc, ss)] = s_pe.n

            em_load_d(0)
            p.wait("tensor", s_act)
            for dc in range(16):
                if dc + 1 < 16:
                    em_load_d(dc + 1)
                for ss in range(16):
                    scp = scps[ss % 2]
                    prev = ("d", dc, ss - 2) if ss >= 2 else ("d", dc - 1, ss + 14)
                    if prev in dq_mark:
                        p.wait("tensor", s_dve, val=dq_mark[prev])
                    elif dc == 0 and ss <= 1:
                        p.wait("tensor", s_dve)
                    em_smm_d(dc, ss)
                    p.wait("vector", s_pe, val=smm_mark[(dc, ss)])
                    p.wait("vector", sbp[dc % 2])
                    p.step("vector", lambda e, dc=dc, ss=ss, scp=scp: e.scalar_tensor_tensor(
                        out=ddq[:, ss * 512:(ss + 1) * 512],
                        in0=db8[dc % 2][:, ss * 512:(ss + 1) * 512],
                        scalar=7.5, in1=scp[:, 0:512], op0=ALU.subtract, op1=ALU.mult), s_dve)
                    dq_mark[("d", dc, ss)] = s_dve.n
                p.wait("tensor", s_dve)
                for j in range(64):
                    p.step("tensor", lambda e, j=j: e.matmul(
                        y_ps[:, 0:512], ddq[:, j * 128:(j + 1) * 128],
                        bufs["x2"][:, j * 512:(j + 1) * 512],
                        start=(j == 0), stop=False))
                p.step("tensor", lambda e, dc=dc: e.matmul(
                    y_ps[:, 0:512], bufs["lb_d"][:, dc * 128:(dc + 1) * 128],
                    bufs["tlo"][:], start=False, stop=True), s_pe)
                p.wait("sync", s_dve)
                p.dma("sync", bufs["xr"][:],
                      io["x1_slots"][dc * 128:(dc + 1) * 128, sl * 512:(sl + 1) * 512], s_ld)
                p.wait("vector", s_pe)
                p.wait("vector", s_ld)
                p.wait("vector", s_st)
                p.step("vector", lambda e, dc=dc: e.scalar_tensor_tensor(
                    out=bufs["x2o"][:], in0=y_ps[:, 0:512], scalar=bufs["bdn"][:, dc:dc + 1],
                    in1=bufs["xr"][:], op0=ALU.add, op1=ALU.add), s_dve)
                p.wait("sync", s_dve)
                p.dma("sync", io["x2_slots"][dc * 128:(dc + 1) * 128, sl * 512:(sl + 1) * 512],
                      bufs["x2o"][:], s_st)
    p.wait("gpsimd", s_st)
    _pes.close()
    if env is None:
        p.build()
    return p, io


def emit_proj_cols(p, bufs, w_dram, w_ncols, oc, zt, zst, zoff, nt, out_ps,
                   s_ld2, s_pe, s_act, wcol, do_rank1, mur_bf=None, mur_off=0):
    p.wait("gpsimd", s_pe)
    for dc in range(16):
        p.dma("gpsimd", wcol[:, dc * 128:(dc + 1) * 128],
              w_dram[dc * 128:(dc + 1) * 128, oc * 128:(oc + 1) * 128], s_ld2)
    p.wait("tensor", s_ld2)
    if do_rank1:
        p.wait("tensor", bufs["s_dve_ln"])
        p.wait("tensor", s_act)
        for dc in range(16):
            p.step("tensor", lambda e, dc=dc: e.matmul(
                bufs["stat_ps"][0:1, 0:128], bufs["ones_c_bf"][:],
                wcol[:, dc * 128:(dc + 1) * 128],
                start=(dc == 0), stop=(dc == 15)), s_pe if dc == 15 else None)
        p.wait("scalar", s_pe)
        p.step("scalar", lambda e: e.activation(
            out=bufs["ncb1"][0:1, 0:128], in_=bufs["stat_ps"][0:1, 0:128],
            func=AF.Copy, bias=0.0, scale=-1.0), s_act)
        p.wait("tensor", s_act)
    for dc in range(16):
        last = (dc == 15) and not do_rank1
        p.step("tensor", lambda e, dc=dc, last=last: e.matmul(
            out_ps[:, 0:nt], wcol[:, dc * 128:(dc + 1) * 128],
            zt[:, dc * zst + zoff:dc * zst + zoff + nt],
            start=(dc == 0), stop=last), s_pe if last else None)
    if do_rank1:
        p.step("tensor", lambda e: e.matmul(
            out_ps[:, 0:nt], bufs["ncb1"][0:1, 0:128],
            mur_bf[0:1, mur_off:mur_off + nt], start=False, stop=True), s_pe)


def build_p3(env=None):
    if env is None:
        p = Prog()
        io = {}
    else:
        p = env["p"]
        io = env["io"]
    _pes = contextlib.ExitStack()
    def _psb(name, shape, dtype):
        return _pes.enter_context(p.nc.sbuf_tensor("p3" + name, list(shape), dtype, side="right"))
    def _pps(name, shape, dtype=F32):
        return _pes.enter_context(p.nc.psum_tensor("p3" + name, list(shape), dtype))
    def _dram(name, shape, dtype, kind=None):
        if name not in io:
            kw = {} if kind is None else {"kind": kind}
            io[name] = p.nc.dram_tensor(name, list(shape), dtype, **kw).ap()
        return io[name]
    nc = p.nc
    _dram("x2_slots", [D, NSLOT * BLK], F32, "ExternalInput")
    _dram("wqT1", [D, D], F32, "ExternalInput")
    _dram("wkvT1", [D, 1024], F32, "ExternalInput")
    _dram("woT1", [D, D], F32, "ExternalInput")
    _dram("rope_cos", [128, NSLOT * BLK], F32, "ExternalInput")
    _dram("rope_sin", [128, NSLOT * BLK], F32, "ExternalInput")
    _dram("l1mask", [128, 4 * 4 * 1024], BF16, "ExternalInput")
    _dram("swapm", [128, 128], BF16, "ExternalInput")
    for k, shp, dt in [("ident", [128, 128], BF16), ("ones_c_bf", [128, 1], BF16),
                       ("ones_c_f", [128, 1], F32), ("ones_r_f", [1, 128], F32),
                       ("ones_r_bf", [1, 128], BF16), ("cfix", [1, 8], F32),
                       ("ccol", [128, 4], F32)]:
        _dram(k, shp, dt, "ExternalInput")
    if env is None:
        io["av3_slots"] = nc.dram_tensor("av_slots", [D, 4 * BLK], BF16, kind="ExternalOutput").ap()
    else:
        io["av3_slots"] = nc.dram_tensor("av3_slots", [D, 4 * BLK], BF16).ap()
    _dram("x3_slots", [D, 4 * BLK], F32, None if env is not None else "ExternalOutput")

    bufs = {}
    bufs["xt"] = _psb("xt", [128, 16 * 256], F32)
    bufs["zt"] = _psb("zt", [128, 16 * 512], BF16)
    bufs["kbuf"] = _psb("kbuf", [128, 4 * NSLOT * 512], BF16)
    bufs["vbuf"] = _psb("vbuf", [128, 4 * NSLOT * 512], BF16)
    bufs["probs"] = _psb("probs", [128, 3 * 1024], BF16)
    bufs["tnh"] = _psb("tnh", [128, 1024], F32)
    bufs["maskslot"] = _psb("maskslot", [128, 4 * 1024], BF16)
    bufs["sq"] = _psb("sq", [128, 512], F32)
    bufs["rows"] = _psb("rows", [1, 4 * 512], F32)
    bufs["rbc"] = _psb("rbc", [128, 256], F32)
    bufs["rec"] = _psb("rec", [1, 256], F32)
    bufs["recbc"] = _psb("recbc", [128, 256], F32)
    bufs["mur"] = _psb("mur", [1, 512], BF16)
    bufs["ncol_q"] = _psb("ncol_q", [1, 2048], BF16)
    bufs["ncol_kv"] = _psb("ncol_kv", [1, 1024], BF16)
    bufs["rc"] = _psb("rc", [128, 512], F32)
    bufs["rs"] = _psb("rs", [128, 512], F32)
    bufs["psc"] = _psb("psc", [128, 512], BF16)
    bufs["rt1"] = _psb("rt1", [128, 512], F32)
    bufs["rt2"] = _psb("rt2", [128, 512], F32)
    bufs["swap"] = _psb("swap_sb", [128, 128], BF16)
    bufs["ncb1"] = _psb("ncb1", [1, 128], BF16)
    bufs["wcol"] = _psb("wcol", [128, 16 * 128], BF16)
    for k in ["ident", "ones_c_bf", "ones_c_f", "ones_r_f", "ones_r_bf", "cfix", "ccol"]:
        bufs[k] = _psb(k + "_sb", io[k].tensor.shape, io[k].dtype)
    bufs["stat_ps"] = _pps("stat_ps", [128, 512])
    bufs["rbc_ps"] = _pps("rbc_ps", [128, 512])
    sc_ps = _pps("sc_ps", [128, 1024])

    if env is None:
        s_ld = p.sem("ld", queue="sync"); s_pe = p.sem("pe"); s_act = p.sem("act")
        s_dve = p.sem("dve"); s_gp = p.sem("gp"); s_st = p.sem("st", queue="sync")
        s_gld = p.sem("gld", queue="gpsimd")
    else:
        s_ld = env["s_ld"]; s_pe = env["s_pe"]; s_act = env["s_act"]
        s_dve = env["s_dve"]; s_gp = env["s_gp"]; s_st = env["s_st"]; s_gld = env["s_gld"]
    bufs["s_ld"] = s_ld

    emit_load_consts(p, io, bufs)
    p.dma("sync", bufs["ccol"][:], io["ccol"], s_ld)
    p.dma("sync", bufs["swap"][:], io["swapm"], s_ld)

    def rope_emit(src_bf, dst_bf, qsw_ps):
        p.wait("vector", s_pe)
        p.step("vector", lambda e: e.tensor_tensor(
            out=bufs["rt1"][:], in0=qsw_ps, in1=bufs["rs"][:], op=ALU.mult), s_dve)
        p.wait("vector", s_dve)
        p.wait("gpsimd", s_act)
        p.wait("gpsimd", s_dve)
        p.step("gpsimd", lambda e: e.tensor_tensor(
            out=bufs["rt2"][:], in0=src_bf, in1=bufs["rc"][:], op=ALU.mult), s_gp)
        p.wait("vector", s_gp)
        p.step("vector", lambda e: e.tensor_tensor(
            out=dst_bf, in0=bufs["rt1"][:], in1=bufs["rt2"][:], op=ALU.add), s_dve)

    bufs["s_dve_ln"] = s_dve
    with contextlib.ExitStack() as l1_es:
        q_sb = l1_es.enter_context(nc.sbuf_tensor("q1", [128, 16 * 512], BF16, side="right"))
        av_sb = l1_es.enter_context(nc.sbuf_tensor("av1", [128, 16 * 512], BF16, side="right"))
        p.wait("tensor", s_ld)
        p.wait("scalar", s_ld)
        p.wait("vector", s_ld)
        p.wait("gpsimd", s_ld)

        for sl in range(NSLOT):
            p.wait("sync", s_gp)
            p.wait("sync", s_dve)
            p.dma("sync", bufs["rc"][:], io["rope_cos"][:, sl * 512:(sl + 1) * 512], s_ld)
            p.dma("sync", bufs["rs"][:], io["rope_sin"][:, sl * 512:(sl + 1) * 512], s_ld)
            p.wait("sync", s_pe)
            for hh in range(2):
                for dc in range(16):
                    p.dma("sync", bufs["xt"][:, dc * 256:(dc + 1) * 256],
                          io["x2_slots"][dc * 128:(dc + 1) * 128,
                                         sl * 512 + hh * 256:sl * 512 + hh * 256 + 256], s_ld)
                p.wait("scalar", s_ld)
                p.wait("tensor", s_ld)
                p.wait("gpsimd", s_ld)
                emit_ln(p, bufs, bufs["xt"], 256, 0, bufs["zt"], 512, hh * 256, 256,
                        bufs["mur"], hh * 256, s_pe, s_act, s_dve, s_gp)
                if hh == 0:
                    p.wait("sync", s_gp)
                    p.wait("sync", s_pe)
            p.wait("tensor", s_gp)
            with contextlib.ExitStack() as kv_es:
                vt_ps = kv_es.enter_context(nc.psum_tensor(f"vt_ps{sl}", [128, 1024], BF16))
                for oc in range(8):
                    g = oc % 4
                    p.wait("tensor", s_act)
                    p.wait("tensor", s_dve)
                    emit_proj_cols(p, bufs, io["wkvT1"], 1024, oc, bufs["zt"], 512, 0, 512,
                                   sc_ps[:, 0:512], s_gld, s_pe, s_act, bufs["wcol"], True,
                                   mur_bf=bufs["mur"], mur_off=0)
                    p.wait("scalar", s_pe)
                    p.wait("scalar", s_dve)
                    p.wait("scalar", s_gp)
                    p.step("scalar", lambda e: e.activation(
                        out=bufs["psc"][:], in_=sc_ps[:, 0:512], func=AF.Copy,
                        bias=0.0, scale=1.0), s_act)
                    if oc < 4:
                        p.wait("tensor", s_act)
                        p.step("tensor", lambda e: e.matmul(
                            sc_ps[:, 512:1024], bufs["swap"][:], bufs["psc"][:],
                            start=True, stop=True), s_pe)
                        rope_emit(bufs["psc"][:],
                                  bufs["kbuf"][:, (g * NSLOT + sl) * 512:(g * NSLOT + sl) * 512 + 512],
                                  sc_ps[:, 512:1024])
                    else:
                        p.wait("tensor", s_act)
                        for half in range(4):
                            p.step("tensor", lambda e, half=half: e.transpose(
                                vt_ps[:, half * 128:(half + 1) * 128],
                                bufs["psc"][:, half * 128:(half + 1) * 128],
                                bufs["ident"][:]), s_pe if half == 3 else None)
                        p.wait("vector", s_pe)
                        p.step("vector", lambda e, g=g, sl=sl: e.tensor_copy(
                            out=bufs["vbuf"][:, (g * NSLOT + sl) * 512:(g * NSLOT + sl) * 512 + 512],
                            in_=vt_ps[:, 0:512]), s_dve)
                        p.wait("tensor", s_dve)
            if sl == 0:
                continue
            for oc in range(16):
                p.wait("tensor", s_act)
                p.wait("tensor", s_dve)
                emit_proj_cols(p, bufs, io["wqT1"], 2048, oc, bufs["zt"], 512, 0, 512,
                               sc_ps[:, 0:512], s_gld, s_pe, s_act, bufs["wcol"], True,
                               mur_bf=bufs["mur"], mur_off=0)
                p.wait("scalar", s_pe)
                p.wait("scalar", s_dve)
                p.wait("scalar", s_gp)
                p.step("scalar", lambda e: e.activation(
                    out=bufs["psc"][:], in_=sc_ps[:, 0:512], func=AF.Copy,
                    bias=0.0, scale=1.0), s_act)
                p.wait("tensor", s_act)
                p.step("tensor", lambda e: e.matmul(
                    sc_ps[:, 512:1024], bufs["swap"][:], bufs["psc"][:],
                    start=True, stop=True), s_pe)
                rope_emit(bufs["psc"][:], q_sb[:, oc * 512:(oc + 1) * 512], sc_ps[:, 512:1024])
            qi = sl - 1
            p.wait("sync", s_dve)
            p.wait("sync", s_gp)
            p.dma("sync", bufs["maskslot"][:], io["l1mask"][:, qi * 4096:(qi + 1) * 4096], s_ld)
            p.wait("vector", s_ld)
            p.wait("gpsimd", s_ld)
            p.wait("vector", s_st)
            with contextlib.ExitStack() as at_es:
                av_ps = [at_es.enter_context(nc.psum_tensor(f"av3_{sl}_{h}", [128, 512], F32))
                         for h in range(4)]
                p.wait("tensor", s_ld)
                for g in range(4):
                    for qh in range(2):
                        for h4 in range(4):
                            h = g * 4 + h4
                            for b in range(2):
                                kvs = sl - 1 + b
                                pb = b % 2
                                p.wait("tensor", s_act)
                                p.wait("tensor", s_dve)
                                p.wait("tensor", s_gp)
                                for j in range(4):
                                    p.step("tensor", lambda e, g=g, kvs=kvs, j=j, h=h, qh=qh: e.matmul(
                                        sc_ps[:, j * 256:(j + 1) * 256],
                                        bufs["kbuf"][:, (g * NSLOT + kvs) * 512 + j * 128:
                                                     (g * NSLOT + kvs) * 512 + (j + 1) * 128],
                                        q_sb[:, h * 512 + qh * 256: h * 512 + qh * 256 + 256],
                                        start=(j % 2 == 0), stop=(j % 2 == 1)),
                                        s_pe if j == 3 else None)
                                p.wait("scalar", s_pe)
                                p.wait("scalar", s_dve)
                                p.wait("scalar", s_gp)
                                p.step("scalar", lambda e: e.activation(
                                    out=bufs["tnh"][:], in_=sc_ps[:, 0:1024], func=AF.Tanh,
                                    scale=SCALE0 / CAP1), s_act)
                                p.wait("scalar", s_act)
                                p.step("scalar", lambda e, pb=pb: e.activation(
                                    out=bufs["probs"][:, pb * 1024:(pb + 1) * 1024],
                                    in_=bufs["tnh"][:], func=AF.Exp,
                                    bias=bufs["ccol"][:, 0:1], scale=CAP1), s_act)
                                eng = "vector" if b == 0 else "gpsimd"
                                sem = s_dve if b == 0 else s_gp
                                p.wait(eng, s_act)
                                p.wait(eng, s_pe)
                                p.step(eng, lambda e, pb=pb, b=b, qh=qh: e.tensor_tensor(
                                    out=bufs["probs"][:, 2048:3072],
                                    in0=bufs["probs"][:, pb * 1024:(pb + 1) * 1024],
                                    in1=bufs["maskslot"][:, (b * 2 + qh) * 1024:(b * 2 + qh + 1) * 1024],
                                    op=ALU.mult), sem)
                                p.wait("tensor", sem)
                                first = (b == 0)
                                lastb = (b == 1)
                                def em_av(stop_on_j3, first=first, g=g, kvs=kvs, h4=h4):
                                    for j in range(4):
                                        stp = stop_on_j3 and (j == 3)
                                        p.step("tensor", lambda e, j=j, g=g, kvs=kvs, h4=h4, first=first, stp=stp: e.matmul(
                                            av_ps[h4][:, 0:256],
                                            bufs["vbuf"][:, (g * NSLOT + kvs) * 512 + j * 128:
                                                         (g * NSLOT + kvs) * 512 + (j + 1) * 128],
                                            bufs["probs"][:, 2048 + j * 256: 2048 + (j + 1) * 256],
                                            start=(first and j == 0), stop=stp),
                                            s_pe if stp else None)
                                def em_sums(h4=h4):
                                    for j in range(4):
                                        p.step("tensor", lambda e, h4=h4, j=j: e.matmul(
                                            av_ps[h4][0:1, 256:512],
                                            bufs["ones_c_bf"][:],
                                            bufs["probs"][:, 2048 + j * 256: 2048 + (j + 1) * 256],
                                            start=False, stop=False))
                                if lastb:
                                    em_sums()
                                    em_av(True)
                                else:
                                    em_av(False)
                                    em_sums()
                        for h4 in range(4):
                            h = g * 4 + h4
                            p.wait("vector", s_pe)
                            p.wait("vector", s_act)
                            p.step("vector", lambda e, h4=h4: e.reciprocal(
                                out=bufs["rec"][0:1, 0:256], in_=av_ps[h4][0:1, 256:512]), s_dve)
                            p.wait("tensor", s_dve)
                            p.wait("tensor", s_act)
                            p.step("tensor", lambda e: e.matmul(
                                bufs["rbc_ps"][:, 0:256], bufs["ones_r_f"][:],
                                bufs["rec"][0:1, 0:256], start=True, stop=True), s_pe)
                            p.wait("scalar", s_pe)
                            p.wait("scalar", s_dve)
                            p.step("scalar", lambda e: e.activation(
                                out=bufs["recbc"][:, 0:256], in_=bufs["rbc_ps"][:, 0:256],
                                func=AF.Copy, bias=0.0, scale=1.0), s_act)
                            p.wait("vector", s_act)
                            p.step("vector", lambda e, h=h, h4=h4, qh=qh: e.tensor_tensor(
                                out=av_sb[:, h * 512 + qh * 256: h * 512 + qh * 256 + 256],
                                in0=av_ps[h4][:, 0:256], in1=bufs["recbc"][:, 0:256],
                                op=ALU.mult), s_dve)
            p.wait("sync", s_dve)
            for oc in range(16):
                p.dma("sync", io["av3_slots"][oc * 128:(oc + 1) * 128, qi * 512:(qi + 1) * 512],
                      av_sb[:, oc * 512:(oc + 1) * 512], s_st)
    with contextlib.ExitStack() as wo_es:
        avs = wo_es.enter_context(nc.sbuf_tensor("avs3", [128, 16 * 512], BF16, side="right"))
        xr = wo_es.enter_context(nc.sbuf_tensor("xr3", [128, 2 * 512], F32, side="right"))
        x3o = wo_es.enter_context(nc.sbuf_tensor("x3o", [128, 2 * 512], F32, side="right"))
        wo_ps = wo_es.enter_context(nc.psum_tensor("wo3_ps", [128, 512], F32))
        p.wait("gpsimd", s_st)
        p.wait("gpsimd", s_pe)
        for qi in range(4):
            p.wait("sync", s_pe)
            for oc in range(16):
                p.dma("sync", avs[:, oc * 512:(oc + 1) * 512],
                      io["av3_slots"][oc * 128:(oc + 1) * 128, qi * 512:(qi + 1) * 512], s_ld)
            p.wait("tensor", s_ld)
            for oc in range(16):
                pb = oc % 2
                p.wait("sync", s_dve)
                p.dma("sync", xr[:, pb * 512:(pb + 1) * 512],
                      io["x2_slots"][oc * 128:(oc + 1) * 128, (qi + 1) * 512:(qi + 2) * 512], s_ld)
                p.wait("tensor", s_dve)
                emit_proj_cols(p, bufs, io["woT1"], 2048, oc, avs, 512, 0, 512,
                               wo_ps, s_gld, s_pe, s_act, bufs["wcol"], False)
                p.wait("vector", s_pe)
                p.wait("vector", s_ld)
                p.wait("vector", s_st)
                p.step("vector", lambda e, pb=pb: e.tensor_tensor(
                    out=x3o[:, pb * 512:(pb + 1) * 512], in0=wo_ps[:, 0:512],
                    in1=xr[:, pb * 512:(pb + 1) * 512], op=ALU.add), s_dve)
                p.wait("sync", s_dve)
                p.dma("sync", io["x3_slots"][oc * 128:(oc + 1) * 128, qi * 512:(qi + 1) * 512],
                      x3o[:, pb * 512:(pb + 1) * 512], s_st)
    p.wait("gpsimd", s_st)
    _pes.close()
    if env is None:
        p.build()
    return p, io


def build_p4(env=None):
    if env is None:
        p = Prog()
        io = {}
    else:
        p = env["p"]
        io = env["io"]
    _pes = contextlib.ExitStack()
    def _psb(name, shape, dtype):
        return _pes.enter_context(p.nc.sbuf_tensor("p4" + name, list(shape), dtype, side="right"))
    def _pps(name, shape, dtype=F32):
        return _pes.enter_context(p.nc.psum_tensor("p4" + name, list(shape), dtype))
    def _dram(name, shape, dtype, kind=None):
        if name not in io:
            kw = {} if kind is None else {"kind": kind}
            io[name] = p.nc.dram_tensor(name, list(shape), dtype, **kw).ap()
        return io[name]
    nc = p.nc
    _dram("x3_slots", [D, 4 * BLK], F32, "ExternalInput")
    _dram("w1_b", [64 * 128, 16 * 128], F32, "ExternalInput")
    _dram("w2_b", [16 * 128, 64 * 128], F32, "ExternalInput")
    _dram("wr", [D, 8], F32, "ExternalInput")
    _dram("br_r", [8, 1], F32, "ExternalInput")
    _dram("b1_r", [128, 64], F32, "ExternalInput")
    _dram("b2_r", [1, D], F32, "ExternalInput")
    _dram("a1t_c", [8 * 128, 256], F32, "ExternalInput")
    _dram("a2t_c", [8 * 128, 1024], F32, "ExternalInput")
    _dram("b1t", [128, 8192], F32, "ExternalInput")
    _dram("b2t", [128, 2048], F32, "ExternalInput")
    _dram("sel8", [8, 8 * 128], BF16, "ExternalInput")
    for k, shp, dt in [("ident", [128, 128], BF16), ("ones_c_bf", [128, 1], BF16),
                       ("ones_c_f", [128, 1], F32), ("ones_r_f", [1, 128], F32),
                       ("ones_r_bf", [1, 128], BF16), ("cfix", [1, 8], F32)]:
        _dram(k, shp, dt, "ExternalInput")
    _dram("out_slots", [D, 4 * BLK], F32, "ExternalOutput")
    t2scr = nc.dram_tensor("t2scr", [16, 8 * 512], BF16)

    bufs = {}
    bufs["xt"] = _psb("xt", [128, 16 * 128], F32)
    bufs["base"] = _psb("base", [128, 64 * 512], BF16)
    bufs["mix"] = _psb("mix", [128, 64 * 512], BF16)
    bufs["sq"] = _psb("sq", [128, 512], F32)
    bufs["rows"] = _psb("rows", [1, 4 * 256], F32)
    bufs["rows_stride"] = 256
    bufs["rbc"] = _psb("rbc", [128, 256], F32)
    bufs["mur"] = _psb("mur", [1, 512], BF16)
    bufs["a1t"] = _psb("a1t", [128, 256], BF16)
    bufs["a2t"] = _psb("a2t", [128, 1024], BF16)
    bufs["b1t"] = _psb("b1t_sb", [16, 2048], BF16)
    bufs["b2t"] = _psb("b2t_sb", [128, 2048], BF16)
    bufs["wr"] = _psb("wr_sb", [128, 16 * 8], BF16)
    bufs["br"] = _psb("br_sb", [8, 1], F32)
    bufs["b1"] = _psb("b1_sb", [128, 64], F32)
    bufs["b2"] = _psb("b2_sb", [1, 2048], BF16)
    bufs["sel8"] = _psb("sel8_sb", [8, 8 * 128], BF16)
    bufs["t1"] = _psb("t1_sb", [16, 8 * 512], BF16)
    bufs["t2st"] = _psb("t2st", [16, 512], BF16)
    bufs["t2a"] = _psb("t2a", [128, 512], BF16)
    bufs["cbc"] = _psb("cbc_sb", [128, 512], BF16)
    bufs["ssb"] = _psb("ssb", [128, 512], BF16)
    bufs["asb"] = _psb("asb", [128, 512], BF16)
    bufs["ncb"] = _psb("ncb", [1, 512], BF16)
    bufs["lg"] = _psb("lg", [8, 512], BF16)
    bufs["lt"] = _psb("lt", [128, 32], F32)
    bufs["rtw"] = _psb("rtw", [128, 40], F32)
    bufs["cmb"] = _psb("cmb", [128, 8], BF16)
    bufs["comb8"] = _psb("comb8", [8, 512], BF16)
    bufs["csumr"] = _psb("csumr", [1, 512], BF16)
    bufs["xr"] = _psb("xr", [128, 512], F32)
    bufs["oout"] = _psb("oout", [128, 512], F32)
    for k in ["ident", "ones_c_bf", "ones_c_f", "ones_r_f", "ones_r_bf", "cfix"]:
        bufs[k] = _psb(k + "_sb", io[k].tensor.shape, io[k].dtype)
    bufs["stat_ps"] = _pps("stat_ps", [128, 512])
    bufs["rbc_ps"] = _pps("rbc_ps", [128, 512])
    y_ps = _pps("y_ps", [128, 512])
    t2_ps = _pps("t2_ps", [128, 512])
    tr_ps = _pps("tr_ps", [128, 512], BF16)
    cb_ps = _pps("cb_ps", [128, 512])

    if env is None:
        s_ld = p.sem("ld", queue="sync"); s_pe = p.sem("pe"); s_act = p.sem("act")
        s_dve = p.sem("dve"); s_gp = p.sem("gp"); s_st = p.sem("st", queue="sync")
        s_gld = p.sem("gld", queue="gpsimd")
    else:
        s_ld = env["s_ld"]; s_pe = env["s_pe"]; s_act = env["s_act"]
        s_dve = env["s_dve"]; s_gp = env["s_gp"]; s_st = env["s_st"]; s_gld = env["s_gld"]
    bufs["s_ld"] = s_ld

    emit_load_consts(p, io, bufs)
    p.dma("sync", bufs["br"][:], io["br_r"], s_ld)
    p.dma("sync", bufs["b1"][:], io["b1_r"], s_ld)
    p.dma("sync", bufs["sel8"][:], io["sel8"], s_ld)
    p.dma("gpsimd", bufs["b2"][:], io["b2_r"], s_gld)
    for dc in range(16):
        p.dma("gpsimd", bufs["wr"][:, dc * 8:(dc + 1) * 8],
              io["wr"][dc * 128:(dc + 1) * 128, :], s_gld)
    p.dma("gpsimd", bufs["b2t"][:], io["b2t"], s_gld, max_dma_last_dim=8192)
    p.wait("tensor", s_ld)
    p.wait("tensor", s_gld)
    p.wait("scalar", s_ld)
    p.wait("scalar", s_gld)
    p.wait("vector", s_ld)
    p.wait("gpsimd", s_ld)

    p.wait("tensor", s_dve)
    for dc in range(16):
        p.step("tensor", lambda e, dc=dc: e.matmul(
            bufs["stat_ps"][0:1, 0:8], bufs["ones_c_bf"][:],
            bufs["wr"][:, dc * 8:(dc + 1) * 8],
            start=(dc == 0), stop=(dc == 15)), s_pe if dc == 15 else None)
    p.wait("scalar", s_pe)
    p.step("scalar", lambda e: e.activation(
        out=bufs["ncb"][0:1, 0:8], in_=bufs["stat_ps"][0:1, 0:8],
        func=AF.Copy, bias=0.0, scale=-1.0), s_act)
    for ex in range(8):
        p.wait("gpsimd", s_pe)
        p.dma("gpsimd", bufs["a1t"][:], io["a1t_c"][ex * 128:(ex + 1) * 128, :], s_gld)
        p.wait("tensor", s_gld)
        p.wait("tensor", s_act)
        for dc in range(16):
            p.step("tensor", lambda e, ex=ex, dc=dc: e.matmul(
                bufs["stat_ps"][0:1, 0:16], bufs["ones_c_bf"][:],
                bufs["a1t"][:, dc * 16:(dc + 1) * 16],
                start=(dc == 0), stop=(dc == 15)), s_pe if dc == 15 else None)
        p.wait("scalar", s_pe)
        p.step("scalar", lambda e, ex=ex: e.activation(
            out=bufs["ncb"][0:1, 16 + ex * 16:16 + (ex + 1) * 16],
            in_=bufs["stat_ps"][0:1, 0:16],
            func=AF.Copy, bias=0.0, scale=-1.0), s_act)

    for sl in range(4):
        zsc = contextlib.ExitStack()
        ztl = zsc.enter_context(nc.sbuf_tensor(f"zt{sl}", [128, 16 * 512], BF16, side="right"))
        w1b = zsc.enter_context(nc.sbuf_tensor(f"w1b{sl}", [128, 2048], BF16, side="right"))
        bufs["zt"] = ztl
        p.wait("gpsimd", s_pe)
        for hh in range(4):
            p.wait("sync", s_gp)
            p.wait("sync", s_pe)
            for dc in range(16):
                p.dma("sync", bufs["xt"][:, dc * 128:(dc + 1) * 128],
                      io["x3_slots"][dc * 128:(dc + 1) * 128,
                                     sl * 512 + hh * 128:sl * 512 + hh * 128 + 128], s_ld)
            p.wait("scalar", s_ld)
            p.wait("tensor", s_ld)
            p.wait("gpsimd", s_ld)
            emit_ln(p, bufs, bufs["xt"], 128, 0, ztl, 512, hh * 128, 128,
                    bufs["mur"], hh * 128, s_pe, s_act, s_dve, s_gp)
        p.wait("tensor", s_gp)
        p.wait("tensor", s_act)
        p.wait("tensor", s_dve)
        for dc in range(16):
            p.step("tensor", lambda e, dc=dc, ztl=ztl: e.matmul(
                y_ps[0:8, 0:512], bufs["wr"][:, dc * 8:(dc + 1) * 8],
                ztl[:, dc * 512:(dc + 1) * 512],
                start=(dc == 0), stop=False))
        p.step("tensor", lambda e: e.matmul(
            y_ps[0:8, 0:512], bufs["ncb"][0:1, 0:8], bufs["mur"][0:1, 0:512],
            start=False, stop=True), s_pe)
        p.wait("scalar", s_pe)
        p.step("scalar", lambda e: e.activation(
            out=bufs["lg"][:], in_=y_ps[0:8, 0:512], func=AF.Identity,
            bias=bufs["br"][:], scale=1.0), s_act)
        p.wait("tensor", s_act)
        p.wait("tensor", s_dve)
        for ts in range(4):
            p.step("tensor", lambda e, ts=ts: e.transpose(
                tr_ps[:, ts * 128:ts * 128 + 8], bufs["lg"][:, ts * 128:(ts + 1) * 128],
                bufs["ident"][0:8, 0:8]), s_pe if ts == 3 else None)
        p.wait("vector", s_pe)
        for ts in range(4):
            p.step("vector", lambda e, ts=ts: e.tensor_copy(
                out=bufs["lt"][:, ts * 8:(ts + 1) * 8],
                in_=tr_ps[:, ts * 128:ts * 128 + 8]), s_dve)
        p.wait("tensor", s_dve)
        for ts in range(4):
            sl8 = bufs["lt"][:, ts * 8:(ts + 1) * 8]
            W = bufs["rtw"]
            def dv(fn):
                p.wait("vector", s_dve)
                p.step("vector", fn, s_dve)
            dv(lambda e, sl8=sl8: e.reduce_max(out=W[:, 0:1], in_=sl8, axis=AX.X))
            dv(lambda e, sl8=sl8: e.tensor_scalar(
                out=W[:, 8:16], in0=sl8, scalar1=W[:, 0:1], scalar2=None, op0=ALU.is_equal))
            dv(lambda e, sl8=sl8: e.scalar_tensor_tensor(
                out=W[:, 16:24], in0=W[:, 8:16], scalar=-1e30, in1=sl8,
                op0=ALU.mult, op1=ALU.add))
            dv(lambda e: e.reduce_max(out=W[:, 1:2], in_=W[:, 16:24], axis=AX.X))
            dv(lambda e: e.tensor_scalar(
                out=W[:, 2:3], in0=W[:, 0:1], scalar1=-1.0, scalar2=None, op0=ALU.mult))
            p.wait("scalar", s_dve)
            p.step("scalar", lambda e, sl8=sl8: e.activation(
                out=W[:, 24:32], in_=sl8, func=AF.Exp, bias=W[:, 2:3], scale=1.0), s_act)
            p.wait("vector", s_act)
            dv(lambda e: e.reduce_sum(out=W[:, 3:4], in_=W[:, 24:32], axis=AX.X))
            dv(lambda e: e.reciprocal(out=W[:, 4:5], in_=W[:, 3:4]))
            dv(lambda e, sl8=sl8: e.tensor_scalar(
                out=W[:, 16:24], in0=sl8, scalar1=W[:, 1:2], scalar2=None, op0=ALU.is_equal))
            dv(lambda e: e.tensor_tensor(
                out=W[:, 32:40], in0=W[:, 8:16], in1=W[:, 16:24], op=ALU.add))
            dv(lambda e: e.tensor_tensor(
                out=W[:, 8:16], in0=W[:, 24:32], in1=W[:, 32:40], op=ALU.mult))
            p.wait("vector", s_pe)
            dv(lambda e: e.tensor_scalar(
                out=bufs["cmb"][:], in0=W[:, 8:16], scalar1=W[:, 4:5], scalar2=None,
                op0=ALU.mult))
            p.wait("tensor", s_dve)
            p.step("tensor", lambda e, ts=ts: e.transpose(
                tr_ps[0:8, ts * 128:(ts + 1) * 128], bufs["cmb"][:], bufs["ident"][:]),
                s_pe)
        p.wait("scalar", s_pe)
        p.step("scalar", lambda e: e.activation(
            out=bufs["comb8"][:], in_=tr_ps[0:8, 0:512], func=AF.Copy,
            bias=0.0, scale=1.0), s_act)
        p.wait("tensor", s_act)
        p.wait("tensor", s_dve)
        p.step("tensor", lambda e: e.matmul(
            cb_ps[0:1, 0:512], bufs["ones_c_bf"][0:8, 0:1], bufs["comb8"][:],
            start=True, stop=True), s_pe)
        p.wait("scalar", s_pe)
        p.step("scalar", lambda e: e.activation(
            out=bufs["csumr"][:], in_=cb_ps[0:1, 0:512], func=AF.Copy,
            bias=0.0, scale=1.0), s_act)
        for ex in range(8):
            p.wait("gpsimd", s_pe)
            p.dma("gpsimd", bufs["a1t"][:], io["a1t_c"][ex * 128:(ex + 1) * 128, :], s_gld)
            p.wait("tensor", s_gld)
            p.wait("tensor", s_act)
            p.wait("tensor", s_dve)
            for dc in range(16):
                p.step("tensor", lambda e, ex=ex, dc=dc, ztl=ztl: e.matmul(
                    t2_ps[0:16, 0:512], bufs["a1t"][:, dc * 16:(dc + 1) * 16],
                    ztl[:, dc * 512:(dc + 1) * 512],
                    start=(dc == 0), stop=False))
            p.step("tensor", lambda e, ex=ex: e.matmul(
                t2_ps[0:16, 0:512], bufs["ncb"][0:1, 16 + ex * 16:16 + (ex + 1) * 16],
                bufs["mur"][0:1, 0:512], start=False, stop=True), s_pe)
            p.wait("scalar", s_pe)
            p.step("scalar", lambda e, ex=ex: e.activation(
                out=bufs["t1"][:, ex * 512:(ex + 1) * 512], in_=t2_ps[0:16, 0:512],
                func=AF.Copy, bias=0.0, scale=2.0), s_act)
        for hc in range(64):
            p.wait("gpsimd", s_pe)
            p.dma("gpsimd", w1b[:],
                  io["w1_b"][hc * 128:(hc + 1) * 128, :], s_gld, max_dma_last_dim=8192)
            p.wait("tensor", s_gld)
            p.wait("tensor", s_act)
            p.wait("tensor", s_dve)
            for dc in range(16):
                p.step("tensor", lambda e, dc=dc, w1b=w1b: e.matmul(
                    bufs["stat_ps"][0:1, 0:128], bufs["ones_c_bf"][:],
                    w1b[:, dc * 128:(dc + 1) * 128],
                    start=(dc == 0), stop=(dc == 15)), s_pe if dc == 15 else None)
            p.wait("scalar", s_pe)
            p.step("scalar", lambda e: e.activation(
                out=bufs["ncb"][0:1, 256:384], in_=bufs["stat_ps"][0:1, 0:128],
                func=AF.Copy, bias=0.0, scale=-1.0), s_act)
            p.wait("tensor", s_act)
            for dc in range(16):
                p.step("tensor", lambda e, dc=dc, w1b=w1b, ztl=ztl: e.matmul(
                    y_ps[:, 0:512], w1b[:, dc * 128:(dc + 1) * 128],
                    ztl[:, dc * 512:(dc + 1) * 512],
                    start=(dc == 0), stop=False))
            p.step("tensor", lambda e: e.matmul(
                y_ps[:, 0:512], bufs["ncb"][0:1, 256:384], bufs["mur"][0:1, 0:512],
                start=False, stop=True), s_pe)
            p.wait("scalar", s_pe)
            p.step("scalar", lambda e, hc=hc: e.activation(
                out=bufs["base"][:, hc * 512:(hc + 1) * 512], in_=y_ps[:, 0:512],
                func=AF.Identity, bias=bufs["b1"][:, hc:hc + 1], scale=1.0), s_act)
        for ex in range(8):
            p.wait("tensor", s_act)
            p.wait("tensor", s_dve)
            p.step("tensor", lambda e, ex=ex: e.matmul(
                cb_ps[:, 0:512], bufs["sel8"][:, ex * 128:(ex + 1) * 128],
                bufs["comb8"][:], start=True, stop=True), s_pe)
            p.wait("scalar", s_pe)
            p.step("scalar", lambda e: e.activation(
                out=bufs["cbc"][:], in_=cb_ps[:, 0:512], func=AF.Copy,
                bias=0.0, scale=1.0), s_act)
            p.wait("gpsimd", s_pe)
            p.dma("gpsimd", bufs["a2t"][:],
                  io["a2t_c"][ex * 128:(ex + 1) * 128, :], s_gld, max_dma_last_dim=4096)
            p.wait("tensor", s_gld)
            for hc in range(64):
                if hc % 16 == 0:
                    p.wait("gpsimd", s_pe)
                    p.dma("gpsimd", bufs["b1t"][:],
                          io["b1t"][ex * 16:(ex + 1) * 16,
                                    hc * 128:(hc + 16) * 128], s_gld, max_dma_last_dim=4096)
                    p.wait("tensor", s_gld)
                p.wait("tensor", s_act)
                p.wait("tensor", s_dve)
                p.step("tensor", lambda e, ex=ex, hc=hc: e.matmul(
                    y_ps[:, 0:512], bufs["b1t"][:, (hc % 16) * 128:(hc % 16 + 1) * 128],
                    bufs["t1"][:, ex * 512:(ex + 1) * 512],
                    start=True, stop=True), s_pe)
                p.wait("vector", s_pe)
                p.wait("vector", s_gp)
                p.step("vector", lambda e, hc=hc: e.tensor_tensor(
                    out=bufs["ssb"][:], in0=y_ps[:, 0:512],
                    in1=bufs["base"][:, hc * 512:(hc + 1) * 512], op=ALU.add), s_dve)
                p.wait("scalar", s_dve)
                p.step("scalar", lambda e: e.activation(
                    out=bufs["asb"][:], in_=bufs["ssb"][:], func=AF.Gelu_apprx_tanh), s_act)
                p.wait("vector", s_act)
                p.wait("vector", s_pe)
                p.step("vector", lambda e: e.tensor_tensor(
                    out=bufs["ssb"][:], in0=bufs["asb"][:], in1=bufs["cbc"][:],
                    op=ALU.mult), s_dve)
                p.wait("gpsimd", s_dve)
                p.step("gpsimd", lambda e, hc=hc, ex=ex: e.tensor_tensor(
                    out=bufs["mix"][:, hc * 512:(hc + 1) * 512],
                    in0=bufs["ssb"][:],
                    in1=bufs["mix"][:, hc * 512:(hc + 1) * 512], op=ALU.add)
                    if ex > 0 else e.tensor_copy(
                    out=bufs["mix"][:, hc * 512:(hc + 1) * 512], in_=bufs["ssb"][:]), s_gp)
                p.wait("tensor", s_dve)
                p.step("tensor", lambda e, hc=hc: e.matmul(
                    t2_ps[0:16, 0:512], bufs["a2t"][:, hc * 16:(hc + 1) * 16],
                    bufs["ssb"][:], start=(hc == 0), stop=(hc == 63)),
                    s_pe if hc == 63 else None)
            p.wait("scalar", s_pe)
            p.wait("scalar", s_st)
            p.step("scalar", lambda e, ex=ex: e.activation(
                out=bufs["t2st"][:], in_=t2_ps[0:16, 0:512],
                func=AF.Copy, bias=0.0, scale=2.0), s_act)
            p.wait("sync", s_act)
            p.dma("sync", t2scr[:, ex * 512:(ex + 1) * 512], bufs["t2st"][:], s_st)
        zsc.close()
        wsc = contextlib.ExitStack()
        w2b = wsc.enter_context(nc.sbuf_tensor(f"w2b{sl}", [128, 8192], BF16, side="right"))
        p.wait("sync", s_st)
        p.dma("sync", bufs["t2a"][:],
              t2scr.ap().rearrange("r (e t) -> e r t", t=512), s_ld)
        for dc in range(16):
            p.wait("gpsimd", s_pe)
            if dc == 0:
                p.wait("gpsimd", s_gp)
            p.dma("gpsimd", w2b[:],
                  io["w2_b"][dc * 128:(dc + 1) * 128, :], s_gld, max_dma_last_dim=8192)
            p.wait("tensor", s_gld)
            p.wait("tensor", s_act)
            p.wait("tensor", s_gp)
            p.wait("tensor", s_ld)
            p.wait("tensor", s_dve)
            for hc in range(64):
                p.step("tensor", lambda e, hc=hc, w2b=w2b: e.matmul(
                    y_ps[:, 0:512], w2b[:, hc * 128:(hc + 1) * 128],
                    bufs["mix"][:, hc * 512:(hc + 1) * 512],
                    start=(hc == 0), stop=False))
            p.step("tensor", lambda e, dc=dc: e.matmul(
                y_ps[:, 0:512], bufs["b2t"][:, dc * 128:(dc + 1) * 128],
                bufs["t2a"][:], start=False, stop=False))
            p.step("tensor", lambda e, dc=dc: e.matmul(
                y_ps[:, 0:512], bufs["b2"][0:1, dc * 128:(dc + 1) * 128],
                bufs["csumr"][0:1, 0:512], start=False, stop=True), s_pe)
            p.wait("sync", s_dve)
            p.dma("sync", bufs["xr"][:],
                  io["x3_slots"][dc * 128:(dc + 1) * 128, sl * 512:(sl + 1) * 512], s_ld)
            p.wait("vector", s_pe)
            p.wait("vector", s_ld)
            p.wait("vector", s_st)
            p.step("vector", lambda e: e.tensor_tensor(
                out=bufs["oout"][:], in0=y_ps[:, 0:512], in1=bufs["xr"][:],
                op=ALU.add), s_dve)
            p.wait("sync", s_dve)
            p.dma("sync", io["out_slots"][dc * 128:(dc + 1) * 128, sl * 512:(sl + 1) * 512],
                  bufs["oout"][:], s_st)
        wsc.close()
    p.wait("gpsimd", s_st)
    _pes.close()
    if env is None:
        p.build()
    return p, io


def emit_phase_barrier(p, sems):
    for eng in ENGS:
        for s in sems:
            p.wait(eng, s)


def build_fused():
    p = Prog()
    io = {}
    env = {"p": p, "io": io,
           "s_ld": p.sem("ld", queue="sync"), "s_pe": p.sem("pe"),
           "s_act": p.sem("act"), "s_dve": p.sem("dve"), "s_gp": p.sem("gp"),
           "s_st": p.sem("st", queue="sync"), "s_gld": p.sem("gld", queue="gpsimd")}
    sems = [env[k] for k in ("s_ld", "s_pe", "s_act", "s_dve", "s_gp", "s_st", "s_gld")]
    build_p1a(env)
    emit_phase_barrier(p, sems)
    build_p2(env)
    emit_phase_barrier(p, sems)
    build_p3(env)
    emit_phase_barrier(p, sems)
    build_p4(env)
    p.build()
    return p, io



def _l1_attn_inputs(inputs):
    perm = np.zeros(128, np.int64)
    perm[:64] = np.arange(0, 128, 2)
    perm[64:] = np.arange(1, 128, 2)
    wq1, wk1, wv1, wo1 = (inputs["L1_attn_wq"], inputs["L1_attn_wk"],
                          inputs["L1_attn_wv"], inputs["L1_attn_wo"])
    wqp = wq1.reshape(16, 128, 2048)[:, perm, :].reshape(2048, 2048)
    wkp = wk1.reshape(4, 128, 2048)[:, perm, :].reshape(512, 2048)
    swapm = np.zeros((128, 128), np.float32)
    for m in range(128):
        swapm[(m + 64) % 128, m] = 1.0
    return {
        "wqT1": np.ascontiguousarray(wqp.T),
        "wkvT1": np.ascontiguousarray(np.concatenate([wkp, wv1], axis=0).T),
        "woT1": np.ascontiguousarray(wo1.T),
        "swapm": swapm.astype(BF),
    }


def _l0_ffn_inputs(inputs):
    sclT_u = np.ascontiguousarray(inputs["L0_ffn_w_up_scale"].T)
    sclT_d = np.ascontiguousarray(inputs["L0_ffn_w_down_scale"].T)
    selm = np.zeros((32, 2048), np.float32)
    for dc in range(16):
        for m in range(128):
            selm[(m // 64) * 16 + dc, dc * 128 + m] = 1.0
    return {
        "wup_b": np.ascontiguousarray(
            inputs["L0_ffn_w_up_q"].T.reshape(16, 128, 64, 128).transpose(2, 1, 0, 3)
        ).reshape(64 * 128, 16 * 128).astype(np.uint8),
        "wdn_b": np.ascontiguousarray(
            inputs["L0_ffn_w_down_q"].T.reshape(64, 128, 16, 128).transpose(2, 1, 0, 3)
        ).reshape(16 * 128, 64 * 128).astype(np.uint8),
        "slu": np.ascontiguousarray(
            sclT_u.reshape(16, 2, 8192).transpose(1, 0, 2).reshape(32, 8192)
            .reshape(32, 32, 2, 128).transpose(2, 0, 1, 3).reshape(64, 4096)),
        "scld": np.ascontiguousarray(
            sclT_d.reshape(64, 2, 16, 128).transpose(1, 2, 0, 3).reshape(32, 8192)
            .reshape(32, 2, 32, 128).transpose(1, 0, 2, 3).reshape(64, 4096)),
        "selm": np.tile(selm, (2, 1)).astype(BF),
        "la_up": inputs["L0_ffn_w_up_lora_a"],
        "lb_up": inputs["L0_ffn_w_up_lora_b"],
        "la_dn": inputs["L0_ffn_w_down_lora_a"],
        "lb_dn": inputs["L0_ffn_w_down_lora_b"],
        "bup_r": np.ascontiguousarray(inputs["L0_ffn_b_up"].reshape(64, 128).T),
        "bdn_r": np.ascontiguousarray(inputs["L0_ffn_b_down"].reshape(16, 128).T),
    }


def _l1_ffn_inputs(inputs):
    W1, W2 = inputs["L1_ffn_W1"], inputs["L1_ffn_W2"]
    A1T = inputs["L1_ffn_A1"].transpose(0, 2, 1)
    A2T = inputs["L1_ffn_A2"].transpose(0, 2, 1)
    sel8 = np.zeros((8, 8 * 128), np.float32)
    for e in range(8):
        sel8[e, e * 128:(e + 1) * 128] = 1.0
    return {
        "w1_b": np.ascontiguousarray(
            W1.reshape(16, 128, 64, 128).transpose(2, 1, 0, 3)).reshape(8192, 2048),
        "w2_b": np.ascontiguousarray(
            W2.reshape(64, 128, 16, 128).transpose(2, 1, 0, 3)).reshape(2048, 8192),
        "wr": inputs["L1_ffn_Wr"],
        "br_r": inputs["L1_ffn_br"].reshape(8, 1),
        "b1_r": np.ascontiguousarray(inputs["L1_ffn_b1"].reshape(64, 128).T),
        "b2_r": inputs["L1_ffn_b2"].reshape(1, 2048),
        "a1t_c": np.ascontiguousarray(
            A1T.reshape(8, 16, 128, 16).transpose(0, 2, 1, 3)).reshape(8 * 128, 256),
        "a2t_c": np.ascontiguousarray(
            A2T.reshape(8, 64, 128, 16).transpose(0, 2, 1, 3)).reshape(8 * 128, 1024),
        "b1t": np.ascontiguousarray(inputs["L1_ffn_B1"].transpose(0, 2, 1)).reshape(128, 8192),
        "b2t": np.ascontiguousarray(inputs["L1_ffn_B2"].transpose(0, 2, 1)).reshape(128, 2048),
        "sel8": sel8.astype(BF),
    }


def _rope_inputs(slots):
    posl = np.concatenate([np.arange(s * BLK, (s + 1) * BLK) for s in slots])
    cos2, sin2 = rope_tables(posl)
    return {"rope_cos": cos2, "rope_sin": sin2}


def kernel(**inputs):
    inputs = {k: np.asarray(v) for k, v in inputs.items()}
    shared, _ = host_prepare(inputs)
    x = np.ascontiguousarray(inputs["L0_attn_x"].astype(np.float32, copy=False))
    l0w = {"wq0": inputs["L0_attn_Wq"],
           "wkv0": np.ascontiguousarray(
               np.concatenate([inputs["L0_attn_Wk"], inputs["L0_attn_Wv"]], axis=1)),
           "wo0": inputs["L0_attn_Wo"]}
    ffw = _l0_ffn_inputs(inputs)
    a1w = _l1_attn_inputs(inputs)
    f1w = _l1_ffn_inputs(inputs)

    p, io = build_fused()
    names = {t.tensor.name for t in io.values()}
    maps = []
    for c in range(8):
        b, role = c // 2, c % 2
        slots = SLOTS_A if role == 0 else SLOTS_B
        xb = x[b]
        m = dict(shared); m.update(l0w); m.update(ffw); m.update(a1w); m.update(f1w)
        m.update(_rope_inputs(slots))
        m["x_fm"] = np.ascontiguousarray(xb.T)
        m["x_slots"] = np.ascontiguousarray(
            np.concatenate([xb[s * BLK:(s + 1) * BLK] for s in slots], axis=0).T)
        m["l0bias"] = l0_bias_for(slots)
        m["l0mask"] = l0_masks_for(slots)
        m["l1mask"] = l1_masks_for(slots, role)
        maps.append({k: v for k, v in m.items() if k in names})

    res = run_bass_kernel_spmd(p.nc, maps, core_ids=list(range(8)))

    out = np.empty((4, S, D), np.float32)
    for c in range(8):
        b, role = c // 2, c % 2
        slots = SLOTS_A if role == 0 else SLOTS_B
        o = np.asarray(res.results[c]["out_slots"])
        for qi in range(4):
            blk = slots[qi + 1]
            out[b, blk * BLK:(blk + 1) * BLK, :] = o[:, qi * BLK:(qi + 1) * BLK].T
    return out
